# version 08
# baseline (speedup 1.0000x reference)
import numpy as np
from contextlib import ExitStack

import concourse.bass as bass
import concourse.mybir as mybir
from concourse.bass_utils import run_bass_kernel_spmd

F32 = mybir.dt.float32
BF16 = mybir.dt.bfloat16
I32 = mybir.dt.int32
AF = mybir.ActivationFunctionType
ALU = mybir.AluOpType
AX = mybir.AxisListType

NCORES = 8
T = 1024
D = 2048
NH = 16
DH = 128
BLK = 256
NBLK = T // BLK
NMT = T // 128
H0 = 5120
H1 = 8192
WIN = 1024
NKT = 12

ENGS = ("sync", "gpsimd", "tensor", "vector", "scalar")


class _B:

    def __init__(self, nc):
        self.nc = nc
        self.ctx = ExitStack()
        self.sem = {}
        self.cnt = {}
        for name in ("wd", "gd", "mm", "vx", "sc", "gx", "cc", "bar", "od",
                     "w0", "w1", "w2", "w3", "ev", "vt", "cs", "lq", "lk",
                     "lv", "lu", "lc", "lr"):
            self.sem[name] = self.ctx.enter_context(nc.semaphore(name))
            self.cnt[name] = 0
        self.en = None
        self.eng = None

    def dma(self, qeng, out, in_, sem="gd"):
        self.cnt[sem] += 16
        if self.en == qeng:
            if callable(out):
                out = out()
            if callable(in_):
                in_ = in_()
            self.eng.dma_start(out=out, in_=in_).then_inc(self.sem[sem], 16)
        return self.cnt[sem]

    AUTO_CHAIN = {"vector": "vx", "scalar": "sc"}

    def op(self, oeng, sem, fn, inc=1):
        own = self.AUTO_CHAIN.get(oeng)
        if own is not None and self.en == oeng and self.cnt[own] > 0:
            self.eng.wait_ge(self.sem[own], self.cnt[own])
        if inc:
            self.cnt[sem] += inc
        if self.en == oeng:
            inst = fn(self.eng)
            if inc:
                inst.then_inc(self.sem[sem], inc)
        return self.cnt[sem]

    def vop(self, fn):
        return self.op("vector", "vx", fn)

    def sop(self, fn):
        return self.op("scalar", "sc", fn)

    def sil(self, oeng, fn):
        if self.en == oeng:
            fn(self.eng)

    def wait(self, oeng, sem, val):
        if self.en == oeng and val > 0:
            self.eng.wait_ge(self.sem[sem], val)

    WSEMS = ("w0", "w1", "w2", "w3")

    ENG_SEMS = {"sync": ("wd", "w0", "w1", "w2", "w3"),
                "gpsimd": ("gd", "gx", "od", "cc", "ev", "vt", "cs",
                           "lq", "lk", "lv", "lu", "lc", "lr"),
                "tensor": ("mm",), "vector": ("vx",), "scalar": ("sc",)}

    def barrier(self):
        self.cnt["bar"] += len(ENGS)
        tgt = self.cnt["bar"]
        snap = dict(self.cnt)
        with self.nc.Block() as blk:
            for name in ENGS:
                def body(eng, name=name, tgt=tgt, snap=snap):
                    for s in self.ENG_SEMS[name]:
                        if snap[s] > 0:
                            eng.wait_ge(self.sem[s], snap[s])
                    eng.sem_inc(self.sem["bar"], 1)
                    eng.wait_ge(self.sem["bar"], tgt)
                getattr(blk, name)(body)

    def chain(self, oeng, sem, fns, first_wait=None):
        v = None
        for i, fn in enumerate(fns):
            if i == 0 and first_wait is not None:
                self.wait(oeng, first_wait[0], first_wait[1])
            if v is not None:
                self.wait(oeng, sem, v)
            v = self.op(oeng, sem, fn)
        return v

    def phase(self, emit_fn, name=None):
        snap = dict(self.cnt)
        final = None
        scope = None
        if name is not None:
            try:
                scope = self.nc.enter_named_scope(name, False)
            except Exception:
                scope = None
        with self.nc.Block() as blk:
            for name in ENGS:
                self.cnt = dict(snap)
                self.en = name

                def body(eng):
                    self.eng = eng
                    emit_fn()
                getattr(blk, name)(body)
                if final is None:
                    final = dict(self.cnt)
                else:
                    assert final == self.cnt, "phase emitter is engine-dependent!"
        self.en = self.eng = None
        if scope is not None:
            try:
                self.nc.leave_named_scope(name, scope[0], False)
            except Exception:
                pass
        self.barrier()


def build_nc(upto=4, dbg=(), world=NCORES):
    nc = bass.Bass("TRN2", target_bir_lowering=False)
    B = _B(nc)

    xin = nc.dram_tensor("xin", [T, D], F32, kind="ExternalInput").ap()
    w4qkvu = nc.dram_tensor("w4qkvu", [4, NH, 128, 2048], F32, kind="ExternalInput").ap()
    woutT = nc.dram_tensor("woutT", [D, D], F32, kind="ExternalInput").ap()
    wfcT = nc.dram_tensor("wfcT", [D, H0], F32, kind="ExternalInput").ap()
    wprojT = nc.dram_tensor("wprojT", [H0, D], F32, kind="ExternalInput").ap()
    w3q = nc.dram_tensor("w3q", [NH, 128, 2048], F32, kind="ExternalInput").ap()
    w3k = nc.dram_tensor("w3k", [NH, 128, 2048], F32, kind="ExternalInput").ap()
    w3v = nc.dram_tensor("w3v", [NH, 128, 2048], F32, kind="ExternalInput").ap()
    woT = nc.dram_tensor("woT", [D, D], F32, kind="ExternalInput").ap()
    w1n = nc.dram_tensor("w1n", [H1, D], F32, kind="ExternalInput").ap()
    w1T = nc.dram_tensor("w1T", [D, H1], F32, kind="ExternalInput").ap()
    w2n = nc.dram_tensor("w2n", [D, H1], F32, kind="ExternalInput").ap()
    w2T = nc.dram_tensor("w2T", [H1, D], F32, kind="ExternalInput").ap()
    slope_in = nc.dram_tensor("slope_in", [NH], F32, kind="ExternalInput").ap()
    gates_in = nc.dram_tensor("gates_in", [NCORES], F32, kind="ExternalInput").ap()
    lmask = nc.dram_tensor("lmask", [2, NKT, 128, 512], BF16, kind="ExternalInput").ap()
    out = nc.dram_tensor("out", [T, D], F32, kind="ExternalOutput").ap()

    d_q = nc.dram_tensor("d_q", [D, T], F32).ap()
    d_kf = nc.dram_tensor("d_kf", [D, T], F32).ap()
    d_u = nc.dram_tensor("d_u", [D, T], F32).ap()
    d_vt = nc.dram_tensor("d_vt", [T, D], F32).ap()
    d_C = nc.dram_tensor("d_C", [NH, NBLK, DH, DH], F32).ap()
    ag_in0 = nc.dram_tensor("ag_in0", [NH * DH, DH], F32)
    ag_out0 = nc.dram_tensor("ag_out0", [world * NH * DH, DH], F32, addr_space=("Shared" if world > 1 else "Local"))
    ag_in_k = nc.dram_tensor("ag_in_k", [D, T], F32)
    ag_out_k = nc.dram_tensor("ag_out_k", [world * D, T], F32, addr_space=("Shared" if world > 1 else "Local"))
    ag_in_v = nc.dram_tensor("ag_in_v", [T, D], F32)
    ag_out_v = nc.dram_tensor("ag_out_v", [world * T, D], F32, addr_space=("Shared" if world > 1 else "Local"))
    kwin = nc.dram_tensor("kwin", [D, 2 * T], F32).ap()
    vwin = nc.dram_tensor("vwin", [2 * T, D], F32).ap()
    d_tq1l0 = nc.dram_tensor("d_tq1l0", [D, H0], BF16).ap()
    d_tq2l0 = nc.dram_tensor("d_tq2l0", [H0, D], BF16).ap()
    d_tq1l1 = nc.dram_tensor("d_tq1l1", [D, H1], BF16).ap()
    d_tq2l1 = nc.dram_tensor("d_tq2l1", [H1, D], BF16).ap()
    d_scr = nc.dram_tensor("d_scr", [8192], F32).ap()
    d_s1 = nc.dram_tensor("d_s1", [H1], F32).ap()
    d_s2 = nc.dram_tensor("d_s2", [D], F32).ap()

    dbg_specs = {"d_q": ([D, T], F32), "d_kf": ([D, T], F32), "d_u": ([D, T], F32),
                 "d_vt": ([T, D], F32), "d_C": ([NH, NBLK, DH, DH], F32),
                 "ag_out0": ([world * NH * DH, DH], F32),
                 "kwin": ([D, 2 * T], F32), "vwin": ([2 * T, D], F32),
                 "d_tq1l0": ([D, H0], BF16), "d_tq2l0": ([H0, D], BF16),
                 "d_tq1l1": ([D, H1], BF16), "d_tq2l1": ([H1, D], BF16)}
    dbg_outs = {n: nc.dram_tensor("dbg_" + n, dbg_specs[n][0], dbg_specs[n][1],
                                  kind="ExternalOutput").ap() for n in dbg}

    ctx = B.ctx

    def sb(name, shape, dt=F32):
        return ctx.enter_context(nc.sbuf_tensor(name, shape, dt))

    x_sb = sb("x_sb", [128, NMT * D])
    ident = sb("ident", [128, 128])
    ones_c = sb("ones_c", [128, 1])
    ones_r = sb("ones_r", [1, 128])
    slope_row = sb("slope_row", [1, NH])
    negs_bc = sb("negs_bc", [128, NH])
    bd_bc = sb("bd_bc", [128, NH])
    bd4m1_bc = sb("bd4m1_bc", [128, NH])
    G = sb("G", [128, NCORES])
    iotaq = sb("iotaq", [128, BLK])
    idxd = sb("idxd", [128, 2 * BLK])
    io255 = sb("io255", [128, 1])
    neg30 = sb("neg30", [128, 1])
    io127 = sb("io127", [128, 1])
    am_fc = sb("am_fc", [128, 1])
    am_pr = sb("am_pr", [128, 1])
    thr_fc = sb("thr_fc", [128, 1])
    thr_pr = sb("thr_pr", [128, 1])
    nthr_fc = sb("nthr_fc", [128, 1])
    nthr_pr = sb("nthr_pr", [128, 1])
    pb = [ctx.enter_context(nc.psum_tensor(f"pb{i}", [128, 512], F32)) for i in range(8)]
    M_sb = kv_sb = None

    def ph_const():
        B.dma("sync", slope_row[0:1, :], slope_in[None, :], "wd")
        B.dma("sync", G[0:1, 0:NCORES], gates_in[None, :], "wd")
        m_slope = m_gates = B.cnt["wd"]

        B.op("gpsimd", "gx", lambda e: e.memset(ones_c[:, :], 1.0))
        B.op("gpsimd", "gx", lambda e: e.memset(ones_r[:, :], 1.0))
        v = B.op("gpsimd", "gx", lambda e: e.memset(ident[:, :], 0.0))
        B.wait("gpsimd", "gx", v)
        B.op("gpsimd", "gx", lambda e: e.affine_select(
            out=ident[:, :], in_=ident[:, :], compare_op=ALU.not_equal,
            fill=1.0, base=0, pattern=[[-1, 128]], channel_multiplier=1))
        B.op("gpsimd", "gx", lambda e: e.iota(iotaq[:, :], [[1, BLK]], base=1,
                                         channel_multiplier=0, allow_small_or_imprecise_dtypes=True))
        B.op("gpsimd", "gx", lambda e: e.memset(neg30[:, :], -30.0))
        v2 = B.op("gpsimd", "gx", lambda e: e.iota(
            idxd[:, :], [[-128, 2], [1, BLK]], base=0, channel_multiplier=-1, allow_small_or_imprecise_dtypes=True))
        B.wait("gpsimd", "gx", v2)
        B.op("gpsimd", "gx", lambda e: e.affine_select(
            out=idxd[:, :], in_=idxd[:, :], compare_op=ALU.is_ge,
            fill=1.0e30, base=0, pattern=[[-128, 2], [1, BLK]],
            channel_multiplier=-1))
        B.op("gpsimd", "gx", lambda e: e.iota(io255[:, :], [[0, 1]], base=255,
                                         channel_multiplier=-1, allow_small_or_imprecise_dtypes=True))
        v4 = B.op("gpsimd", "gx", lambda e: e.iota(io127[:, :], [[0, 1]], base=127,
                                                   channel_multiplier=-1, allow_small_or_imprecise_dtypes=True))

        B.wait("scalar", "wd", m_slope)
        B.op("scalar", "sc", lambda e: e.activation(
            out=bd_bc[0:1, :], in_=slope_row[0:1, :], func=AF.Exp, scale=-256.0))
        B.op("scalar", "sc", lambda e: e.activation(
            out=bd4m1_bc[0:1, :], in_=slope_row[0:1, :], func=AF.Exp, scale=-1024.0))
        s_ng = B.op("scalar", "sc", lambda e: e.mul(
            negs_bc[0:1, :], slope_row[0:1, :], -1.0))

        B.wait("tensor", "sc", s_ng)
        B.wait("tensor", "gx", v4)
        B.wait("tensor", "wd", m_gates)
        B.op("tensor", "mm", lambda e: e.matmul(
            pb[0][:, 0:NH], ones_r[:, :], negs_bc[0:1, :], start=True, stop=True))
        B.op("tensor", "mm", lambda e: e.matmul(
            pb[0][:, 64:64 + NH], ones_r[:, :], bd_bc[0:1, :], start=True, stop=True))
        B.op("tensor", "mm", lambda e: e.matmul(
            pb[0][:, 128:128 + NH], ones_r[:, :], bd4m1_bc[0:1, :], start=True, stop=True))
        mmv = B.op("tensor", "mm", lambda e: e.matmul(
            pb[0][:, 192:192 + NCORES], ones_r[:, :], G[0:1, 0:NCORES],
            start=True, stop=True))

        B.wait("vector", "mm", mmv)
        B.op("vector", "vx", lambda e: e.tensor_copy(negs_bc[:, :], pb[0][:, 0:NH]))
        B.op("vector", "vx", lambda e: e.tensor_copy(bd_bc[:, :], pb[0][:, 64:64 + NH]))
        B.op("vector", "vx", lambda e: e.tensor_copy(G[:, :], pb[0][:, 192:192 + NCORES]))
        B.op("vector", "vx", lambda e: e.tensor_single_scalar(
            bd4m1_bc[:, :], pb[0][:, 128:128 + NH], 1.0, ALU.subtract))

        B.dma("gpsimd", x_sb.ap().rearrange("p (t d) -> p t d", d=D),
              xin.rearrange("(t p) d -> p t d", p=128), "gd")
        B.wait("gpsimd", "gd", B.cnt["gd"])

    def emit_ln(xs, ybuf, st, dump):
        s1 = B.op("scalar", "sc", lambda e: e.activation(
            out=dump[:, 0:D], in_=xs, func=AF.Square, accum_out=st["ssq"][:, 0:1]))
        B.op("vector", "vx", lambda e: e.reduce_sum(
            st["ssum"][:, 0:1], xs, axis=AX.X))
        B.wait("vector", "sc", s1)
        B.op("vector", "vx", lambda e: e.tensor_single_scalar(
            st["mean"][:, 0:1], st["ssum"][:, 0:1], 1.0 / D, ALU.mult))
        B.op("vector", "vx", lambda e: e.tensor_single_scalar(
            st["var"][:, 0:1], st["ssq"][:, 0:1], 1.0 / D, ALU.mult))
        B.op("vector", "vx", lambda e: e.tensor_tensor(
            st["m2"][:, 0:1], st["mean"][:, 0:1], st["mean"][:, 0:1], ALU.mult))
        v2 = B.op("vector", "vx", lambda e: e.scalar_tensor_tensor(
            out=st["var"][:, 0:1], in0=st["var"][:, 0:1], scalar=1e-5,
            in1=st["m2"][:, 0:1], op0=ALU.add, op1=ALU.subtract))
        B.wait("scalar", "vx", v2)
        s2 = B.op("scalar", "sc", lambda e: e.activation(
            out=st["rstd"][:, 0:1], in_=st["var"][:, 0:1], func=AF.Sqrt))
        B.wait("vector", "sc", s2)
        B.op("vector", "vx", lambda e: e.reciprocal(st["rstd"][:, 0:1], st["rstd"][:, 0:1]))
        B.op("vector", "vx", lambda e: e.tensor_tensor(
            st["m2"][:, 0:1], st["rstd"][:, 0:1], st["rstd"][:, 0:1], ALU.mult))
        B.op("vector", "vx", lambda e: e.tensor_tensor(
            st["m2"][:, 0:1], st["m2"][:, 0:1], st["var"][:, 0:1], ALU.mult))
        B.op("vector", "vx", lambda e: e.tensor_scalar(
            st["m2"][:, 0:1], st["m2"][:, 0:1], -0.5, 1.5, op0=ALU.mult, op1=ALU.add))
        B.op("vector", "vx", lambda e: e.tensor_tensor(
            st["rstd"][:, 0:1], st["rstd"][:, 0:1], st["m2"][:, 0:1], ALU.mult))
        v4 = B.op("vector", "vx", lambda e: e.tensor_scalar(
            ybuf[:, 0:D], xs, st["mean"][:, 0:1], st["rstd"][:, 0:1],
            op0=ALU.subtract, op1=ALU.mult))
        return v4

    def stats_tiles(alloc, pfx):
        return {k: alloc(pfx + k, [128, 1]) for k in
                ("ssq", "ssum", "mean", "var", "m2", "rstd", "nmr")}

    def ph_l0_sweep1(bufs):
        yT = bufs["yT"]; wb = bufs["wb"]; ev = bufs["ev"]
        ktok = bufs["ktok"]; vtok = bufs["vtok"]; Cb = bufs["Cb"]
        KD = bufs["KD"]; ybuf = bufs["ybuf"]; st = bufs["st"]
        dump = ybuf

        bank_evac = {0: 0, 1: 0}
        for mt in range(NMT):
            s3 = emit_ln(x_sb[:, mt * D:(mt + 1) * D], ybuf, st, dump)
            B.wait("tensor", "vx", s3)
            for k in range(16):
                bi = k % 2
                B.wait("tensor", "vx", bank_evac[bi])
                mmv = B.op("tensor", "mm", lambda e, k=k, bi=bi: e.transpose(
                    pb[bi][:, 0:128], ybuf[:, k * 128:(k + 1) * 128], ident[:, :]))
                B.wait("vector", "mm", mmv)
                bank_evac[bi] = B.op("vector", "vx",
                                     lambda e, k=k, mt=mt, bi=bi: e.tensor_copy(
                                         yT[:, k * T + mt * 128: k * T + (mt + 1) * 128],
                                         pb[bi][:, 0:128]))
            B.wait("scalar", "mm", B.cnt["mm"])

        yT_done = B.cnt["vx"]
        vM = B.op("vector", "vx", lambda e: e.memset(M_sb[:, :], 0.0))
        B.wait("tensor", "vx", yT_done)

        dests = [d_q, d_kf, None, d_u]
        wlast = {0: 0, 1: 0}
        prev_evac = ("vx", 0)
        ev_spill = 0
        tr_bank_evac = {0: 0, 1: 0}
        vtok_spill = 0
        c_spill = 0
        kd_ready = 0
        for h in range(NH):
            B.wait("scalar", "vx", max(tr_bank_evac.values()))
            s_kd1 = B.op("scalar", "sc", lambda e, h=h: e.activation(
                out=KD[:, 0:1], in_=io255[:, :], func=AF.Exp, scale=negs_bc[:, h:h + 1]))
            kd_ready = B.op("scalar", "sc", lambda e, h=h: e.activation(
                out=KD[:, 1:2], in_=io127[:, :], func=AF.Exp, scale=negs_bc[:, h:h + 1]))
            for r in range(4):
                bi = (h * 4 + r) % 2
                wsem = "w0" if bi == 0 else "w1"
                B.wait("sync", "mm", wlast[bi])
                mw = B.dma("sync", wb[bi].ap(), w4qkvu[r, h], wsem)
                B.wait("tensor", wsem, mw)
                B.wait("tensor", prev_evac[0], prev_evac[1])
                for half in range(2):
                    for kc in range(16):
                        B.op("tensor", "mm",
                             lambda e, kc=kc, half=half, bi=bi:
                             e.matmul(pb[2 + half][:, :],
                                      wb[bi][:, kc * 128:(kc + 1) * 128],
                                      yT[:, kc * T + half * 512: kc * T + (half + 1) * 512],
                                      start=(kc == 0), stop=(kc == 15)),
                             inc=(1 if kc == 15 else 0))
                wlast[bi] = B.cnt["mm"]
                g1 = B.cnt["mm"]
                eeng, esem = (("scalar", "sc") if r in (0, 1) else ("vector", "vx"))
                B.wait(eeng, "mm", g1)
                B.wait(eeng, "ev", ev_spill)
                if r in (0, 1):
                    B.op("scalar", "sc", lambda e: e.activation(
                        out=ev[:, 0:512], in_=pb[2][:, :], func=AF.Silu))
                    e2 = B.op("scalar", "sc", lambda e: e.activation(
                        out=ev[:, 512:1024], in_=pb[3][:, :], func=AF.Silu))
                else:
                    B.op("vector", "vx", lambda e: e.tensor_copy(ev[:, 0:512], pb[2][:, :]))
                    e2 = B.op("vector", "vx", lambda e: e.tensor_copy(
                        ev[:, 512:1024], pb[3][:, :]))
                prev_evac = (esem, e2)
                if r != 2:
                    B.wait("gpsimd", esem, e2)
                    ev_spill = B.dma("gpsimd", dests[r][h * 128:(h + 1) * 128, :],
                                     ev[:, 0:1024], "ev")
                if r in (1, 2):
                    tgt = ktok if r == 1 else vtok
                    B.wait("tensor", esem, e2)
                    if r == 1:
                        B.wait("tensor", "mm", 0)
                    for t in range(8):
                        tb = t % 2
                        B.wait("tensor", "vx", tr_bank_evac[tb])
                        mmv = B.op("tensor", "mm", lambda e, t=t, tb=tb: e.transpose(
                            pb[4 + tb][:, 0:128], ev[:, t * 128:(t + 1) * 128],
                            ident[:, :]))
                        B.wait("vector", "mm", mmv)
                        if r == 1:
                            B.wait("vector", "sc", kd_ready)
                            B.wait("vector", "gd", 0)
                            tr_bank_evac[tb] = B.op(
                                "vector", "vx", lambda e, t=t, tb=tb: e.tensor_scalar(
                                    tgt[:, t * 128:(t + 1) * 128], pb[4 + tb][:, 0:128],
                                    KD[:, (t % 2):(t % 2) + 1], None, op0=ALU.mult))
                        else:
                            B.wait("vector", "vt", vtok_spill)
                            tr_bank_evac[tb] = B.op(
                                "vector", "vx", lambda e, t=t, tb=tb: e.tensor_copy(
                                    tgt[:, t * 128:(t + 1) * 128], pb[4 + tb][:, 0:128]))
                    if r == 2:
                        B.wait("gpsimd", "vx", tr_bank_evac[(7) % 2])
                        vtok_spill = B.dma(
                            "gpsimd",
                            d_vt.rearrange("(t p) d -> p t d", p=128)[:, :, h * 128:(h + 1) * 128],
                            vtok.ap().rearrange("p (t d) -> p t d", d=128), "vt")
            B.wait("tensor", "vx", max(tr_bank_evac[0], tr_bank_evac[1]))
            vM_h = vM
            for b in range(NBLK):
                mmv = 0
                for sc_ in range(2):
                    t = 2 * b + sc_
                    mmv = B.op("tensor", "mm", lambda e, t=t: e.matmul(
                        pb[6][:, 0:128], ktok[:, t * 128:(t + 1) * 128],
                        vtok[:, t * 128:(t + 1) * 128],
                        start=(t % 2 == 0), stop=(t % 2 == 1)),
                        inc=(1 if t % 2 == 1 else 0))
                B.wait("vector", "mm", mmv)
                B.wait("vector", "cs", c_spill)
                B.op("vector", "vx", lambda e, b=b: e.tensor_copy(
                    Cb[:, b * 128:(b + 1) * 128], pb[6][:, 0:128]))
                vM_h = B.op("vector", "vx", lambda e, h=h: e.scalar_tensor_tensor(
                    out=M_sb[:, h * 128:(h + 1) * 128],
                    in0=M_sb[:, h * 128:(h + 1) * 128],
                    scalar=bd_bc[:, h:h + 1], in1=pb[6][:, 0:128],
                    op0=ALU.mult, op1=ALU.add))
                B.wait("tensor", "vx", vM_h)
            vM = vM_h
            B.wait("gpsimd", "vx", vM)
            c_spill = B.dma("gpsimd", d_C[h].rearrange("b p e -> p b e"),
                            Cb.ap().rearrange("p (b e) -> p b e", e=128), "cs")

        B.dma("gpsimd", ag_in0.ap().rearrange("(h p) e -> p h e", p=128),
              M_sb.ap().rearrange("p (h e) -> p h e", e=128), "gd")
        for s_ in ("gd", "ev", "vt", "cs"):
            B.wait("gpsimd", s_, B.cnt[s_])
        B.op("gpsimd", "cc", lambda e: e.collective_compute(
            "AllGather", ALU.bypass, replica_groups=[list(range(world))],
            ins=[ag_in0.ap().opt()], outs=[ag_out0.ap().opt()]))

    def ph_l0_sweep2(bufs):
        qb = bufs["qb"]; kb = bufs["kb"]; vtb = bufs["vtb"]; ub = bufs["ub"]
        Cl = bufs["Cl"]; ob = bufs["ob"]; agm = bufs["agm"]
        qp = bufs["qp"]; qd = bufs["qd"]; dct = bufs["dct"]; pbuf = bufs["pbuf"]
        djc = bufs["djc"]; tmph = bufs["tmph"]; wv = bufs["wv"]
        dump = bufs["dump"]; rmsc = bufs["rmsc"]
        r_a = bufs["r_a"]; r_b = bufs["r_b"]; r_c = bufs["r_c"]

        B.wait("gpsimd", "cc", B.cnt["cc"])
        vz = B.op("vector", "vx", lambda e: e.memset(kv_sb[:, :], 0.0))
        for j in range(world):
            B.wait("gpsimd", "vx", vz)
            agl = B.dma("gpsimd", agm.ap().rearrange("p (h e) -> p h e", e=128),
                        ag_out0.ap()[j * NH * DH:(j + 1) * NH * DH, :]
                        .rearrange("(h p) e -> p h e", p=128), "gd")
            B.wait("vector", "gd", agl)
            for h in range(NH):
                B.op("vector", "vx", lambda e, h=h: e.scalar_tensor_tensor(
                    out=djc[:, 0:1], in0=bd4m1_bc[:, h:h + 1], scalar=G[:, j:j + 1],
                    in1=ones_c[:, :], op0=ALU.mult, op1=ALU.add))
                B.op("vector", "vx", lambda e, h=h: e.tensor_scalar(
                    tmph[:, :], agm[:, h * 128:(h + 1) * 128], G[:, j:j + 1], None,
                    op0=ALU.mult))
                vz = B.op("vector", "vx", lambda e, h=h: e.scalar_tensor_tensor(
                    out=kv_sb[:, h * 128:(h + 1) * 128],
                    in0=kv_sb[:, h * 128:(h + 1) * 128],
                    scalar=djc[:, 0:1], in1=tmph[:, :], op0=ALU.mult, op1=ALU.add))

        pm_done = 0
        o_done = 0
        kvup = vz
        wuse = {0: 0, 1: 0}
        for b in range(NBLK):
            c0 = b * BLK
            B.wait("gpsimd", "mm", B.cnt["mm"])
            B.wait("gpsimd", "vx", B.cnt["vx"])
            for s_ in ("lq", "lk", "lv", "lu", "lc"):
                B.wait("gpsimd", s_, B.cnt[s_])
            ld_q = B.dma("gpsimd", qb.ap().rearrange("p (c m) -> p c m", m=BLK),
                         d_q.rearrange("(c p) m -> p c m", p=128)[:, :, c0:c0 + BLK], "lq")
            ld_k = B.dma("gpsimd", kb.ap().rearrange("p (c m) -> p c m", m=BLK),
                         d_kf.rearrange("(c p) m -> p c m", p=128)[:, :, c0:c0 + BLK], "lk")
            ld_v = B.dma("gpsimd", vtb.ap().rearrange("p (s d) -> p s d", d=D),
                         d_vt.rearrange("(s p) d -> p s d", p=128)[:, 2 * b:2 * b + 2, :], "lv")
            ld_u = B.dma("gpsimd", ub.ap().rearrange("p (c m) -> p c m", m=BLK),
                         d_u.rearrange("(c p) m -> p c m", p=128)[:, :, c0:c0 + BLK], "lu")
            ld_c = B.dma("gpsimd", Cl.ap().rearrange("p (h e) -> p h e", e=128),
                         d_C[:, b].rearrange("h p e -> p h e"), "lc")

            v_qp = 0
            o_ev = {0: 0, 1: 0}
            for h in range(NH):
                B.wait("scalar", "vx", max(v_qp, pm_done))
                s_qd = B.op("scalar", "sc", lambda e, h=h: e.activation(
                    out=qd[:, :], in_=iotaq[:, :], func=AF.Exp,
                    scale=negs_bc[:, h:h + 1]))
                s_dc = B.op("scalar", "sc", lambda e, h=h: e.activation(
                    out=dct[:, :], in_=idxd[:, :], func=AF.Exp,
                    scale=negs_bc[:, h:h + 1]))
                B.wait("vector", "lq", ld_q)
                B.wait("vector", "sc", s_qd)
                B.wait("vector", "mm", o_done)
                v_qp = B.op("vector", "vx", lambda e, h=h: e.tensor_tensor(
                    qp[:, :], qb[:, h * BLK:(h + 1) * BLK], qd[:, :], ALU.mult))
                B.wait("tensor", "lq", ld_q); B.wait("tensor", "lk", ld_k)
                B.wait("tensor", "vx", pm_done)
                mmv = 0
                for sc_ in range(2):
                    mmv = B.op("tensor", "mm", lambda e, sc_=sc_, h=h: e.matmul(
                        pb[sc_][:, 0:BLK],
                        kb[:, h * BLK + sc_ * 128: h * BLK + (sc_ + 1) * 128],
                        qb[:, h * BLK:(h + 1) * BLK], start=True, stop=True))
                B.wait("vector", "mm", mmv)
                B.wait("vector", "sc", s_dc)
                for sc_ in range(2):
                    pm_done = B.op("vector", "vx", lambda e, sc_=sc_: e.tensor_tensor(
                        pbuf[:, sc_ * BLK:(sc_ + 1) * BLK], pb[sc_][:, 0:BLK],
                        dct[:, sc_ * BLK:(sc_ + 1) * BLK], ALU.mult))
                opb = pb[2 + h % 2]
                B.wait("tensor", "vx", max(pm_done, v_qp, kvup, o_ev[h % 2]))
                B.wait("tensor", "lv", ld_v)
                B.op("tensor", "mm", lambda e, h=h, opb=opb: e.matmul(
                    opb[:, 0:BLK], kv_sb[:, h * 128:(h + 1) * 128], qp[:, :],
                    start=True, stop=False), 0)
                for sc_ in range(2):
                    o_done = B.op("tensor", "mm",
                                  lambda e, sc_=sc_, h=h, opb=opb: e.matmul(
                                      opb[:, 0:BLK],
                                      vtb[:, sc_ * D + h * 128: sc_ * D + (h + 1) * 128],
                                      pbuf[:, sc_ * BLK:(sc_ + 1) * BLK],
                                      start=False, stop=(sc_ == 1)),
                                  inc=(1 if sc_ == 1 else 0))
                B.wait("vector", "mm", o_done)
                o_ev[h % 2] = B.op("vector", "vx", lambda e, h=h, opb=opb: e.tensor_copy(
                    ob[:, h * BLK:(h + 1) * BLK], opb[:, 0:BLK]))
                B.wait("vector", "lc", ld_c)
                kvup = B.op("vector", "vx", lambda e, h=h: e.scalar_tensor_tensor(
                    out=kv_sb[:, h * 128:(h + 1) * 128],
                    in0=kv_sb[:, h * 128:(h + 1) * 128],
                    scalar=bd_bc[:, h:h + 1], in1=Cl[:, h * 128:(h + 1) * 128],
                    op0=ALU.mult, op1=ALU.add))

            B.wait("scalar", "vx", kvup)
            prev_mm = 0
            for c in range(16):
                B.wait("scalar", "mm", prev_mm)
                sqd = B.op("scalar", "sc", lambda e, c=c: e.activation(
                    out=dump[:, 0:BLK], in_=ob[:, c * BLK:(c + 1) * BLK],
                    func=AF.Square))
                B.wait("tensor", "sc", sqd)
                prev_mm = B.op("tensor", "mm", lambda e, c=c: e.matmul(
                    pb[6][0:1, 0:BLK], ones_c[:, :], dump[:, 0:BLK],
                    start=(c == 0), stop=(c == 15)))
            B.wait("vector", "mm", prev_mm)
            B.op("vector", "vx", lambda e: e.tensor_single_scalar(
                r_a[0:1, :], pb[6][0:1, 0:BLK], 1.0 / D, ALU.mult))
            v1 = B.op("vector", "vx", lambda e: e.tensor_single_scalar(
                r_a[0:1, :], r_a[0:1, :], 1e-6, ALU.add))
            B.wait("scalar", "vx", v1)
            s1 = B.op("scalar", "sc", lambda e: e.activation(
                out=r_b[0:1, :], in_=r_a[0:1, :], func=AF.Sqrt))
            B.wait("vector", "sc", s1)
            B.op("vector", "vx", lambda e: e.reciprocal(r_b[0:1, :], r_b[0:1, :]))
            B.op("vector", "vx", lambda e: e.tensor_tensor(
                r_c[0:1, :], r_b[0:1, :], r_b[0:1, :], ALU.mult))
            B.op("vector", "vx", lambda e: e.tensor_tensor(
                r_c[0:1, :], r_c[0:1, :], r_a[0:1, :], ALU.mult))
            B.op("vector", "vx", lambda e: e.tensor_scalar(
                r_c[0:1, :], r_c[0:1, :], -0.5, 1.5, op0=ALU.mult, op1=ALU.add))
            v2 = B.op("vector", "vx", lambda e: e.tensor_tensor(
                r_b[0:1, :], r_b[0:1, :], r_c[0:1, :], ALU.mult))
            B.wait("gpsimd", "vx", v2)
            B.dma("gpsimd", d_scr[0:BLK][None, :], r_b[0:1, :], "lr")
            B.wait("gpsimd", "lr", B.cnt["lr"])
            for t_ in range(2):
                g1 = B.dma("gpsimd", rmsc[:, t_:t_ + 1],
                           d_scr[t_ * 128:(t_ + 1) * 128][:, None], "lr")

            B.wait("vector", "lu", ld_u)
            g_done = 0
            for c in range(16):
                g_done = B.op("vector", "vx", lambda e, c=c: e.tensor_tensor(
                    ob[:, c * BLK:(c + 1) * BLK], ob[:, c * BLK:(c + 1) * BLK],
                    ub[:, c * BLK:(c + 1) * BLK], ALU.mult))

            for n in range(4):
                evv = 0
                for c in range(16):
                    bi = c % 2
                    wsem = "w0" if bi == 0 else "w1"
                    B.wait("sync", "mm", wuse[bi])
                    mw = B.dma("sync", wv[bi].ap(),
                               woutT[c * 128:(c + 1) * 128, n * 512:(n + 1) * 512], wsem)
                    B.wait("tensor", wsem, mw)
                    if c == 0:
                        B.wait("tensor", "vx", g_done)
                    for s_ in range(2):
                        B.op("tensor", "mm", lambda e, c=c, s_=s_, bi=bi: e.matmul(
                            pb[4 + s_][:, :],
                            ob[:, c * BLK + s_ * 128: c * BLK + (s_ + 1) * 128],
                            wv[bi].ap(), start=(c == 0), stop=(c == 15)),
                            inc=(1 if s_ == 1 else 0))
                    wuse[bi] = B.cnt["mm"]
                B.wait("vector", "mm", B.cnt["mm"])
                B.wait("vector", "lr", g1)
                for s_ in range(2):
                    mt = 2 * b + s_
                    evv = B.op("vector", "vx",
                               lambda e, s_=s_, mt=mt, n=n: e.scalar_tensor_tensor(
                                   out=x_sb[:, mt * D + n * 512: mt * D + (n + 1) * 512],
                                   in0=pb[4 + s_][:, :], scalar=rmsc[:, s_:s_ + 1],
                                   in1=x_sb[:, mt * D + n * 512: mt * D + (n + 1) * 512],
                                   op0=ALU.mult, op1=ALU.add))
                B.wait("tensor", "vx", evv)

    def ph_wstats(bufs):
        wsb = bufs["wsb"]
        dmp = bufs["dmp"]
        acol = bufs["acol"]
        acol2 = bufs["acol2"]
        acc = bufs["acc"]
        s1t = bufs["s1t"]
        s2t = bufs["s2t"]
        srow = bufs["srow"]

        W = 1024
        vz = B.vop(lambda e: e.memset(acc[:, :], 0.0))
        suse = {0: ("sc", 0), 1: ("vx", 0)}

        def stream(dram, nr, ncol, per_tile):
            idx = 0
            for r in range(nr):
                for cb in range(ncol // W):
                    bi = idx % 2
                    wsem = "w0" if bi == 0 else "w1"
                    B.wait("sync", suse[bi][0], suse[bi][1])
                    mw = B.dma("sync", wsb[bi].ap(),
                               dram[r * 128:(r + 1) * 128, cb * W:(cb + 1) * W], wsem)
                    B.wait("scalar" if bi == 0 else "vector", wsem, mw)
                    per_tile(r, cb, bi)
                    suse[bi] = (("sc", B.cnt["sc"]) if bi == 0
                                else ("vx", B.cnt["vx"]))
                    idx += 1

        def glob_tile(col):
            def f(r, cb, bi):
                if bi == 0:
                    B.wait("scalar", "vx", B.cnt["vx"])
                    s = B.sop(lambda e: e.activation(
                        out=dmp[:, :], in_=wsb[bi][:, :], func=AF.Abs,
                        accum_out=acol[:, 0:1]))
                    B.wait("vector", "sc", s)
                    B.vop(lambda e: e.tensor_tensor(
                        acc[:, col:col + 1], acc[:, col:col + 1], acol[:, 0:1],
                        ALU.add))
                else:
                    B.vop(lambda e: e.reduce_sum(
                        acol2[:, 0:1], wsb[bi][:, :], axis=AX.X,
                        apply_absolute_value=True))
                    B.vop(lambda e: e.tensor_tensor(
                        acc[:, col:col + 1], acc[:, col:col + 1], acol2[:, 0:1],
                        ALU.add))
            return f
        stream(wfcT, 16, H0, glob_tile(0))
        stream(wprojT, 40, D, glob_tile(1))

        def row_tile(tgt, nchunks, inv):
            def f(r, cb, bi):
                if bi == 0:
                    B.wait("scalar", "vx", B.cnt["vx"])
                    s = B.sop(lambda e: e.activation(
                        out=dmp[:, :], in_=wsb[bi][:, :], func=AF.Abs,
                        accum_out=acol[:, 0:1]))
                    B.wait("vector", "sc", s)
                else:
                    s = None
                    B.vop(lambda e: e.reduce_sum(
                        acol[:, 0:1], wsb[bi][:, :], axis=AX.X,
                        apply_absolute_value=True))
                if cb == 0:
                    B.vop(lambda e, r=r: e.tensor_copy(
                        tgt[:, r:r + 1], acol[:, 0:1]))
                else:
                    B.vop(lambda e, r=r: e.tensor_tensor(
                        tgt[:, r:r + 1], tgt[:, r:r + 1], acol[:, 0:1], ALU.add))
                if cb == nchunks - 1:
                    B.vop(lambda e, r=r: e.tensor_scalar(
                        tgt[:, r:r + 1], tgt[:, r:r + 1], inv, 1e-5,
                        op0=ALU.mult, op1=ALU.max))
            return f
        stream(w1n, 64, D, row_tile(s1t, D // W, 1.0 / D))
        stream(w2n, 16, H1, row_tile(s2t, H1 // W, 1.0 / H1))

        vlast = B.cnt["vx"]
        B.wait("tensor", "vx", vlast)
        B.op("tensor", "mm", lambda e: e.matmul(
            pb[0][0:1, 0:1], acc[:, 0:1], ones_c[:, :], start=True, stop=True))
        m1 = B.op("tensor", "mm", lambda e: e.matmul(
            pb[0][0:1, 1:2], acc[:, 1:2], ones_c[:, :], start=True, stop=True))
        B.wait("vector", "mm", m1)
        B.vop(lambda e: e.tensor_scalar(
            srow[0:1, 0:1], pb[0][0:1, 0:1], 1.0 / (H0 * D), 1e-5,
            op0=ALU.mult, op1=ALU.max))
        B.vop(lambda e: e.tensor_scalar(
            srow[0:1, 1:2], pb[0][0:1, 1:2], 1.0 / (H0 * D), 1e-5,
            op0=ALU.mult, op1=ALU.max))
        B.vop(lambda e: e.tensor_single_scalar(
            srow[0:1, 2:3], srow[0:1, 0:1], 0.7, ALU.mult))
        B.vop(lambda e: e.tensor_single_scalar(
            srow[0:1, 3:4], srow[0:1, 1:2], 0.7, ALU.mult))
        B.vop(lambda e: e.tensor_single_scalar(
            srow[0:1, 4:5], srow[0:1, 2:3], -1.0, ALU.mult))
        v2 = B.vop(lambda e: e.tensor_single_scalar(
            srow[0:1, 5:6], srow[0:1, 3:4], -1.0, ALU.mult))
        B.wait("tensor", "vx", v2)
        m2 = B.op("tensor", "mm", lambda e: e.matmul(
            pb[1][:, 0:6], ones_r[:, :], srow[0:1, 0:6], start=True, stop=True))
        B.wait("vector", "mm", m2)
        B.vop(lambda e: e.tensor_copy(am_fc[:, 0:1], pb[1][:, 0:1]))
        B.vop(lambda e: e.tensor_copy(am_pr[:, 0:1], pb[1][:, 1:2]))
        B.vop(lambda e: e.tensor_copy(thr_fc[:, 0:1], pb[1][:, 2:3]))
        B.vop(lambda e: e.tensor_copy(thr_pr[:, 0:1], pb[1][:, 3:4]))
        B.vop(lambda e: e.tensor_copy(nthr_fc[:, 0:1], pb[1][:, 4:5]))
        v3 = B.vop(lambda e: e.tensor_copy(nthr_pr[:, 0:1], pb[1][:, 5:6]))

        B.wait("tensor", "vx", v3)
        m3 = B.op("tensor", "mm", lambda e: e.transpose(
            pb[2][0:64, 0:128], s1t[:, 0:64], ident[:, :]))
        m4 = B.op("tensor", "mm", lambda e: e.transpose(
            pb[3][0:16, 0:128], s2t[:, 0:16], ident[:, :]))
        B.wait("vector", "mm", m4)
        B.vop(lambda e: e.tensor_copy(dmp[0:64, 0:128], pb[2][0:64, 0:128]))
        v4 = B.vop(lambda e: e.tensor_copy(dmp[0:16, 128:256], pb[3][0:16, 0:128]))
        B.wait("gpsimd", "vx", v4)
        B.dma("gpsimd", d_s1.rearrange("(r c) -> r c", c=128), dmp[0:64, 0:128], "gd")
        B.dma("gpsimd", d_s2.rearrange("(r c) -> r c", c=128), dmp[0:16, 128:256], "gd")
        B.wait("gpsimd", "gd", B.cnt["gd"])

    def ph_tern(bufs):
        wsb = bufs["wsb"]; tn = bufs["tn"]; qo = bufs["qo"]; qo2 = bufs["qo2"]
        thp = bufs["thp"]; thn = bufs["thn"]; s1r = bufs["s1r"]; tn2 = bufs["tn2"]
        W = 1024
        suse = {0: ("vx", 0), 1: ("gx", 0)}
        qfree = {0: 0, 1: 0}
        sgn = [0]

        def tern_global(dramT, dst, nr, ncol, thr, nthr):
            idx = 0
            for r in range(nr):
                for cb in range(ncol // W):
                    bi = idx % 2
                    wsem = "w0" if bi == 0 else "w1"
                    B.wait("sync", suse[bi][0], suse[bi][1])
                    mw = B.dma("sync", wsb[bi].ap(),
                               dramT[r * 128:(r + 1) * 128, cb * W:(cb + 1) * W], wsem)
                    qi = idx % 2
                    qsem = "ev" if qi == 0 else "vt"
                    if qi == 0:
                        B.wait("vector", qsem, qfree[qi])
                        B.wait("vector", wsem, mw)
                        B.vop(lambda e, bi=bi: e.tensor_scalar(
                            tn[:, :], wsb[bi][:, :], nthr[:, 0:1], None,
                            op0=ALU.is_lt))
                        vq = B.vop(lambda e, bi=bi, qi=qi:
                                   e.scalar_tensor_tensor(
                                       out=qo[qi][:, :], in0=wsb[bi][:, :],
                                       scalar=thr[:, 0:1], in1=tn[:, :],
                                       op0=ALU.is_gt, op1=ALU.subtract))
                        esem = "vx"
                    else:
                        B.wait("scalar", wsem, mw)
                        B.wait("scalar", "vx", sgn[0])
                        B.sop(lambda e, bi=bi: e.activation(
                            out=tn2[:, :], in_=wsb[bi][:, :], func=AF.Sign,
                            bias=nthr[:, 0:1]))
                        s2 = B.sop(lambda e, bi=bi: e.activation(
                            out=qo2[:, :], in_=wsb[bi][:, :], func=AF.Sign,
                            bias=thr[:, 0:1]))
                        B.wait("vector", "sc", s2)
                        B.wait("vector", qsem, qfree[qi])
                        vq = B.vop(lambda e, qi=qi: e.scalar_tensor_tensor(
                            out=qo[qi][:, :], in0=tn2[:, :], scalar=1.0,
                            in1=qo2[:, :], op0=ALU.mult, op1=ALU.add))
                        vq2 = B.vop(lambda e, qi=qi: e.tensor_single_scalar(
                            qo[qi][:, :], qo[qi][:, :], 0.5, ALU.mult))
                        vq = vq2
                        sgn[0] = vq
                        esem = "vx"
                    suse[bi] = (esem if qi == 0 else "sc",
                                vq if qi == 0 else B.cnt["sc"])
                    B.wait("gpsimd", esem, vq)
                    qfree[qi] = B.dma(
                        "gpsimd", dst[r * 128:(r + 1) * 128, cb * W:(cb + 1) * W],
                        qo[qi].ap(), qsem)
                    idx += 1
        tern_global(wfcT, d_tq1l0, 16, H0, thr_fc, nthr_fc)
        tern_global(wprojT, d_tq2l0, 40, D, thr_pr, nthr_pr)

        def tern_rows(dramT, dst, scl_dram, nr, ncol):
            idx = 0
            for cb in range(ncol // W):
                ml = B.dma("gpsimd", s1r[0:1, 0:W],
                           scl_dram[cb * W:(cb + 1) * W][None, :], "lr")
                B.wait("tensor", "lr", ml)
                B.wait("tensor", suse[0][0], suse[0][1])
                B.wait("tensor", suse[1][0], suse[1][1])
                mm1 = B.op("tensor", "mm", lambda e: e.matmul(
                    pb[4][:, 0:512], ones_r[:, :], s1r[0:1, 0:512],
                    start=True, stop=True))
                mm2 = B.op("tensor", "mm", lambda e: e.matmul(
                    pb[5][:, 0:512], ones_r[:, :], s1r[0:1, 512:1024],
                    start=True, stop=True))
                B.wait("vector", "mm", mm2)
                B.vop(lambda e: e.tensor_single_scalar(
                    thp[:, 0:512], pb[4][:, 0:512], 0.5, ALU.mult))
                B.vop(lambda e: e.tensor_single_scalar(
                    thp[:, 512:1024], pb[5][:, 0:512], 0.5, ALU.mult))
                B.vop(lambda e: e.tensor_single_scalar(
                    thn[:, 0:512], pb[4][:, 0:512], -0.5, ALU.mult))
                vb = B.vop(lambda e: e.tensor_single_scalar(
                    thn[:, 512:1024], pb[5][:, 0:512], -0.5, ALU.mult))
                for r in range(nr):
                    bi = idx % 2
                    wsem = "w0" if bi == 0 else "w1"
                    B.wait("sync", suse[bi][0], suse[bi][1])
                    mw = B.dma("sync", wsb[bi].ap(),
                               dramT[r * 128:(r + 1) * 128, cb * W:(cb + 1) * W], wsem)
                    qi = idx % 2
                    qsem = "ev" if qi == 0 else "vt"
                    B.wait("vector", wsem, mw)
                    B.wait("vector", qsem, qfree[qi])
                    B.vop(lambda e, bi=bi: e.tensor_tensor(
                        tn[:, :], wsb[bi][:, :], thn[:, :], ALU.is_lt))
                    B.vop(lambda e, bi=bi: e.tensor_tensor(
                        qo2[:, :], wsb[bi][:, :], thp[:, :], ALU.is_gt))
                    vq = B.vop(lambda e, qi=qi: e.tensor_tensor(
                        qo[qi][:, :], qo2[:, :], tn[:, :], ALU.subtract))
                    suse[bi] = ("vx", vq)
                    B.wait("gpsimd", "vx", vq)
                    qfree[qi] = B.dma(
                        "gpsimd", dst[r * 128:(r + 1) * 128, cb * W:(cb + 1) * W],
                        qo[qi].ap(), qsem)
                    idx += 1
        tern_rows(w1T, d_tq1l1, d_s1, 16, H1)
        tern_rows(w2T, d_tq2l1, d_s2, 64, D)

    def ph_l0_ffn(bufs):
        ybuf = bufs["ybuf"]; st = bufs["st"]
        y2h = bufs["y2h"]; y2l = bufs["y2l"]
        wsb = bufs["wsb"]
        rb = bufs["rb"]; sqf = bufs["sqf"]
        sqh = bufs["sqh"]; sql = bufs["sql"]
        dump = ybuf

        wuse = {0: 0, 1: 0, 2: 0, 3: 0}
        prev_ln = 0
        for mc in range(4):
            bank_evac = {0: B.cnt["vx"], 1: B.cnt["vx"]}
            for mt_rel in range(2):
                mt = 2 * mc + mt_rel
                B.wait("scalar", "mm", B.cnt["mm"])
                B.wait("scalar", "vx", prev_ln)
                s3 = emit_ln(x_sb[:, mt * D:(mt + 1) * D], ybuf, st, dump)
                prev_ln = s3
                B.wait("tensor", "vx", s3)
                for k in range(16):
                    bi = k % 2
                    B.wait("tensor", "vx", bank_evac[bi])
                    mmv = B.op("tensor", "mm", lambda e, k=k, bi=bi: e.transpose(
                        pb[bi][:, 0:128], ybuf[:, k * 128:(k + 1) * 128], ident[:, :]))
                    B.wait("vector", "mm", mmv)
                    col = k * BLK + mt_rel * 128
                    B.vop(lambda e, bi=bi, col=col: e.tensor_copy(
                        y2h[:, col:col + 128], pb[bi][:, 0:128]))
                    bank_evac[bi] = B.vop(lambda e, bi=bi, col=col: e.tensor_tensor(
                        y2l[:, col:col + 128], pb[bi][:, 0:128], y2h[:, col:col + 128],
                        ALU.subtract))
            yT_done = B.cnt["vx"]
            B.wait("tensor", "vx", yT_done)

            set_done = {0: B.cnt["vx"], 1: B.cnt["vx"]}
            prev_sq = 0
            for cb in range(5):
                bset = [pb[4 * (cb % 2) + i] for i in range(4)]
                B.wait("tensor", "vx", set_done[cb % 2])
                for kr in range(16):
                    bi = (cb * 16 + kr) % 4
                    wsem = B.WSEMS[bi]
                    B.wait("sync", "mm", wuse[bi])
                    mw = B.dma("sync", wsb[bi][:, 0:1024],
                               d_tq1l0[kr * 128:(kr + 1) * 128,
                                       cb * 1024:(cb + 1) * 1024], wsem)
                    B.wait("tensor", wsem, mw)
                    for j in range(8):
                        bank = bset[j // 2]
                        half = (j % 2) * BLK
                        for hl, ysrc in ((0, y2h), (1, y2l)):
                            B.op("tensor", "mm",
                                 lambda e, kr=kr, j=j, hl=hl, ysrc=ysrc,
                                 bank=bank, half=half, bi=bi: e.matmul(
                                     bank[:, half:half + BLK],
                                     wsb[bi][:, j * 128:(j + 1) * 128],
                                     ysrc[:, kr * BLK:(kr + 1) * BLK],
                                     start=(kr == 0 and j % 2 == 0 and hl == 0),
                                     stop=(kr == 15 and j % 2 == 1 and hl == 1)),
                                 inc=(1 if (j == 7 and hl == 1) else 0))
                    wuse[bi] = B.cnt["mm"]
                gmm = B.cnt["mm"]
                for j in range(8):
                    bank = bset[j // 2]
                    half = (j % 2) * BLK
                    hc = cb * 8 + j
                    B.wait("vector", "mm", gmm)
                    B.wait("vector", "sc", prev_sq)
                    B.vop(lambda e, bank=bank, half=half: e.tensor_relu(
                        rb[:, 0:BLK], bank[:, half:half + BLK]))
                    vr = B.cnt["vx"]
                    B.wait("scalar", "vx", vr)
                    prev_sq = B.sop(lambda e: e.activation(
                        out=sqf[:, 0:BLK], in_=rb[:, 0:BLK], func=AF.Square,
                        scale=am_fc[:, 0:1]))
                    B.wait("vector", "sc", prev_sq)
                    col = hc * BLK
                    B.vop(lambda e, col=col: e.tensor_copy(
                        sqh[:, col:col + BLK], sqf[:, 0:BLK]))
                    B.vop(lambda e, col=col: e.tensor_tensor(
                        sql[:, col:col + BLK], sqf[:, 0:BLK], sqh[:, col:col + BLK],
                        ALU.subtract))
                set_done[cb % 2] = B.cnt["vx"]
            sq_done = B.cnt["vx"]

            for hc in range(40):
                bi = hc % 4
                wsem = B.WSEMS[bi]
                B.wait("sync", "mm", wuse[bi])
                mw = B.dma("sync", wsb[bi].ap(),
                           d_tq2l0[hc * 128:(hc + 1) * 128, :], wsem)
                B.wait("tensor", wsem, mw)
                if hc == 0:
                    B.wait("tensor", "vx", sq_done)
                for s_ in range(2):
                    for n in range(4):
                        for hl, sqsrc in ((0, sqh), (1, sql)):
                            B.op("tensor", "mm",
                                 lambda e, hc=hc, s_=s_, n=n, sqsrc=sqsrc, bi=bi:
                                 e.matmul(
                                     pb[4 * s_ + n][:, :],
                                     sqsrc[:, hc * BLK + s_ * 128: hc * BLK + (s_ + 1) * 128],
                                     wsb[bi][:, n * 512:(n + 1) * 512],
                                     start=(hc == 0 and hl == 0),
                                     stop=(hc == 39 and hl == 1)),
                                 inc=(1 if (s_ == 1 and n == 3 and hl == 1) else 0))
                wuse[bi] = B.cnt["mm"]
            B.wait("vector", "mm", B.cnt["mm"])
            evv = 0
            for s_ in range(2):
                for n in range(4):
                    mt = 2 * mc + s_
                    evv = B.vop(lambda e, s_=s_, mt=mt, n=n: e.scalar_tensor_tensor(
                        out=x_sb[:, mt * D + n * 512: mt * D + (n + 1) * 512],
                        in0=pb[4 * s_ + n][:, :], scalar=am_pr[:, 0:1],
                        in1=x_sb[:, mt * D + n * 512: mt * D + (n + 1) * 512],
                        op0=ALU.mult, op1=ALU.add))
            B.wait("tensor", "vx", evv)

    def ph_l1_attn_a(bufs):
        y3T = bufs["y3T"]; wb = bufs["wb"]; ev = bufs["ev"]; vtok = bufs["vtok"]
        ybuf = bufs["ybuf"]; st = bufs["st"]
        dump = ybuf

        bank_evac = {0: B.cnt["vx"], 1: B.cnt["vx"]}
        prev_ln = 0
        for mt in range(NMT):
            B.wait("scalar", "mm", B.cnt["mm"])
            B.wait("scalar", "vx", prev_ln)
            s3 = emit_ln(x_sb[:, mt * D:(mt + 1) * D], ybuf, st, dump)
            prev_ln = s3
            B.wait("tensor", "vx", s3)
            for k in range(16):
                bi = k % 2
                B.wait("tensor", "vx", bank_evac[bi])
                mmv = B.op("tensor", "mm", lambda e, k=k, bi=bi: e.transpose(
                    pb[bi][:, 0:128], ybuf[:, k * 128:(k + 1) * 128], ident[:, :]))
                B.wait("vector", "mm", mmv)
                bank_evac[bi] = B.vop(lambda e, k=k, mt=mt, bi=bi: e.tensor_copy(
                    y3T[:, k * T + mt * 128: k * T + (mt + 1) * 128], pb[bi][:, 0:128]))
        yT_done = B.cnt["vx"]
        B.wait("tensor", "vx", yT_done)

        wlast = {0: 0, 1: 0}
        prev_evac = ("vx", 0)
        ev_spill = {"q": 0, "k": 0}
        vt_spill = 0
        tr_bank_evac = {0: B.cnt["vx"], 1: B.cnt["vx"]}
        regs = [(w3q, "q"), (w3k, "k"), (w3v, "v")]
        for h in range(NH):
            for ri, (wT, r) in enumerate(regs):
                bi = (h * 3 + ri) % 2
                wsem = "w0" if bi == 0 else "w1"
                B.wait("sync", "mm", wlast[bi])
                mw = B.dma("sync", wb[bi].ap(), wT[h], wsem)
                B.wait("tensor", wsem, mw)
                B.wait("tensor", prev_evac[0], prev_evac[1])
                for half in range(2):
                    for kc in range(16):
                        B.op("tensor", "mm",
                             lambda e, kc=kc, half=half, bi=bi:
                             e.matmul(pb[2 + half][:, :],
                                      wb[bi][:, kc * 128:(kc + 1) * 128],
                                      y3T[:, kc * T + half * 512: kc * T + (half + 1) * 512],
                                      start=(kc == 0), stop=(kc == 15)),
                             inc=(1 if kc == 15 else 0))
                wlast[bi] = B.cnt["mm"]
                g1 = B.cnt["mm"]
                B.wait("vector", "mm", g1)
                B.wait("vector", "lq", ev_spill["q"])
                B.wait("vector", "ev", ev_spill["k"])
                if r != "v":
                    B.vop(lambda e: e.tensor_copy(ev[:, 0:512], pb[2][:, :]))
                    e2 = B.vop(lambda e: e.tensor_copy(ev[:, 512:1024], pb[3][:, :]))
                    prev_evac = ("vx", e2)
                    B.wait("gpsimd", "vx", e2)
                    if r == "q":
                        ev_spill["q"] = B.dma("gpsimd", d_q[h * 128:(h + 1) * 128, :],
                                              ev[:, 0:1024], "lq")
                    else:
                        ev_spill["k"] = B.dma("gpsimd",
                                              ag_in_k.ap()[h * 128:(h + 1) * 128, :],
                                              ev[:, 0:1024], "ev")
                else:
                    B.vop(lambda e: e.tensor_copy(ev[:, 0:512], pb[2][:, :]))
                    e2 = B.vop(lambda e: e.tensor_copy(ev[:, 512:1024], pb[3][:, :]))
                    prev_evac = ("vx", e2)
                    B.wait("tensor", "vx", e2)
                    for t in range(8):
                        tb_ = t % 2
                        B.wait("tensor", "vx", tr_bank_evac[tb_])
                        mmv = B.op("tensor", "mm", lambda e, t=t, tb_=tb_: e.transpose(
                            pb[4 + tb_][:, 0:128], ev[:, t * 128:(t + 1) * 128],
                            ident[:, :]))
                        B.wait("vector", "mm", mmv)
                        B.wait("vector", "vt", vt_spill)
                        tr_bank_evac[tb_] = B.vop(
                            lambda e, t=t, tb_=tb_: e.tensor_copy(
                                vtok[:, t * 128:(t + 1) * 128], pb[4 + tb_][:, 0:128]))
                    B.wait("gpsimd", "vx", max(tr_bank_evac.values()))
                    vt_spill = B.dma(
                        "gpsimd",
                        ag_in_v.ap().rearrange("(t p) d -> p t d", p=128)[:, :, h * 128:(h + 1) * 128],
                        vtok.ap().rearrange("p (t d) -> p t d", d=128), "vt")

        for s_ in ("lq", "ev", "vt"):
            B.wait("gpsimd", s_, B.cnt[s_])
        B.op("gpsimd", "cc", lambda e: e.collective_compute(
            "AllGather", ALU.bypass, replica_groups=[list(range(world))],
            ins=[ag_in_k.ap().opt()], outs=[ag_out_k.ap().opt()]))
        B.op("gpsimd", "cc", lambda e: e.collective_compute(
            "AllGather", ALU.bypass, replica_groups=[list(range(world))],
            ins=[ag_in_v.ap().opt()], outs=[ag_out_v.ap().opt()]))
        B.wait("gpsimd", "cc", B.cnt["cc"])

        kh = ag_out_k.ap().tensor
        vh = ag_out_v.ap().tensor

        def mk(off_fn, dims):
            return lambda: bass.AP(off_fn[0], nc.snap(off_fn[1]()), dims)

        def prev_off():
            pid = B.eng.partition_id()
            return (pid + world - 1) % world

        def self_off():
            return B.eng.partition_id()

        B.dma("gpsimd", kwin[:, 0:T],
              mk((kh, lambda: prev_off() * (D * T)), [[T, D], [1, T]]), "gd")
        B.dma("gpsimd", kwin[:, T:2 * T],
              mk((kh, lambda: self_off() * (D * T)), [[T, D], [1, T]]), "gd")
        B.dma("gpsimd", vwin[0:T, :],
              mk((vh, lambda: prev_off() * (T * D)), [[D, T], [1, D]]), "gd")
        B.dma("gpsimd", vwin[T:2 * T, :],
              mk((vh, lambda: self_off() * (T * D)), [[D, T], [1, D]]), "gd")
        B.wait("gpsimd", "gd", B.cnt["gd"])

    def ph_l1_attn_b(bufs):
        qmb = bufs["qmb"]; osb = bufs["osb"]; masks = bufs["masks"]
        ktile = bufs["ktile"]; vtile = bufs["vtile"]
        tbv = bufs["tbv"]; pbm = bufs["pbm"]; lsb = bufs["lsb"]; lrow = bufs["lrow"]
        lsum = bufs["lsum"]; lrows = bufs["lrows"]
        wvb = bufs["wvb"]
        SC = 1.0 / float(np.sqrt(128.0))

        wuse = {0: 0, 1: 0}
        for mb in range(2):
            B.wait("gpsimd", "vx", B.cnt["vx"])
            B.wait("gpsimd", "mm", B.cnt["mm"])
            for s_ in ("lr", "lq", "lk", "lv", "lu", "lc"):
                B.wait("gpsimd", s_, B.cnt[s_])
            ml = B.dma("gpsimd", masks.ap().rearrange("p (k m) -> p k m", m=512),
                       lmask[mb].rearrange("k p m -> p k m"), "lr")
            lq = B.dma("gpsimd", qmb.ap().rearrange("p (c m) -> p c m", m=512),
                       d_q.rearrange("(c p) m -> p c m", p=128)[:, :, mb * 512:(mb + 1) * 512],
                       "lq")
            vz = B.vop(lambda e: e.memset(osb[:, :], 0.0))

            tanh_done = {0: 0, 1: 0, 2: 0}
            av_evac = {0: ("vx", vz), 1: ("vx", vz)}
            lrow_acc = {0: 0, 1: 0}
            pbm_freed = {0: 0, 1: 0, 2: 0}
            vmask = {0: 0, 1: 0, 2: 0}
            kv_ld = {}
            st_mm_done = {}

            def issue_kv(st_):
                par = st_ % 2
                ks, vs = ("lk", "lv") if par == 0 else ("lu", "lc")
                B.wait("gpsimd", "mm", st_mm_done.get(st_ - 2, 0))
                lk_ = B.dma("gpsimd",
                            ktile[par].ap().rearrange("p (c s) -> p c s", s=128),
                            kwin.rearrange("(c p) t -> p c t", p=128)
                            [:, :, mb * 512 + st_ * 128: mb * 512 + (st_ + 1) * 128], ks)
                lv_ = B.dma("gpsimd", vtile[par].ap(),
                            vwin[mb * 512 + st_ * 128: mb * 512 + (st_ + 1) * 128, :], vs)
                kv_ld[st_] = (ks, lk_, vs, lv_)

            issue_kv(0)
            for st_ in range(12):
                if st_ + 1 < 12:
                    issue_kv(st_ + 1)
                par = st_ % 2
                ks, lk, vs, lv = kv_ld[st_]
                ktl = ktile[par]
                vtl = vtile[par]
                for h in range(NH):
                    sb_ = h % 3
                    B.wait("tensor", ks, lk)
                    B.wait("tensor", "lq", lq)
                    B.wait("tensor", "sc", tanh_done[sb_])
                    mmv = B.op("tensor", "mm", lambda e, h=h, sb_=sb_, ktl=ktl: e.matmul(
                        pb[sb_][:, :], ktl[:, h * 128:(h + 1) * 128],
                        qmb[:, h * 512:(h + 1) * 512], start=True, stop=True))
                    B.wait("scalar", "mm", max(mmv, pbm_freed[sb_]))
                    B.wait("scalar", "vx", vmask[sb_])
                    B.sop(lambda e, sb_=sb_: e.activation(
                        out=tbv[sb_][:, :], in_=pb[sb_][:, :], func=AF.Tanh,
                        scale=SC / 30.0))
                    tanh_done[sb_] = B.cnt["sc"]
                    s_e = B.sop(lambda e, sb_=sb_: e.activation(
                        out=pbm[sb_][:, :], in_=tbv[sb_][:, :], func=AF.Exp,
                        scale=30.0, bias=neg30[:, 0:1]))
                    B.wait("vector", "sc", s_e)
                    B.wait("vector", "lr", ml)
                    v_m = B.vop(lambda e, st_=st_, sb_=sb_: e.tensor_tensor(
                        pbm[sb_][:, :], pbm[sb_][:, :],
                        masks[:, st_ * 512:(st_ + 1) * 512], ALU.mult))
                    vmask[sb_] = v_m
                    ab = 3 + (h % 2)
                    B.wait("tensor", "vx", v_m)
                    B.wait("tensor", av_evac[h % 2][0], av_evac[h % 2][1])
                    B.wait("tensor", vs, lv)
                    av = B.op("tensor", "mm",
                              lambda e, h=h, ab=ab, sb_=sb_, vtl=vtl: e.matmul(
                                  pb[ab][:, :], vtl[:, h * 128:(h + 1) * 128],
                                  pbm[sb_][:, :], start=True, stop=True))
                    lb_ = 5 + (h % 2)
                    B.wait("tensor", "vx", lrow_acc[h % 2])
                    B.op("tensor", "mm", lambda e, sb_=sb_, lb_=lb_: e.matmul(
                        pb[lb_][0:1, 0:512], ones_c[:, :], pbm[sb_][:, :],
                        start=True, stop=True))
                    pbm_freed[sb_] = B.cnt["mm"]
                    B.wait("vector", "mm", pbm_freed[sb_])
                    lpr = 32 * (h % 4)
                    lcc = (h // 4) * 512
                    if st_ == 0:
                        lrow_acc[h % 2] = B.vop(lambda e, lpr=lpr, lcc=lcc, lb_=lb_:
                                                e.tensor_copy(
                                                    lsum[lpr:lpr + 1, lcc:lcc + 512],
                                                    pb[lb_][0:1, 0:512]))
                    else:
                        lrow_acc[h % 2] = B.vop(lambda e, lpr=lpr, lcc=lcc, lb_=lb_:
                                                e.tensor_tensor(
                                                    lsum[lpr:lpr + 1, lcc:lcc + 512],
                                                    lsum[lpr:lpr + 1, lcc:lcc + 512],
                                                    pb[lb_][0:1, 0:512], ALU.add))
                    B.wait("vector", "mm", av)
                    av_evac[h % 2] = ("vx", B.vop(
                        lambda e, h=h, ab=ab: e.tensor_tensor(
                            osb[:, h * 512:(h + 1) * 512],
                            osb[:, h * 512:(h + 1) * 512], pb[ab][:, :], ALU.add)))
                st_mm_done[st_] = B.cnt["mm"]

            for k_ in range(4):
                B.vop(lambda e, k_=k_: e.reciprocal(
                    lsum[32 * k_:32 * k_ + 1, :], lsum[32 * k_:32 * k_ + 1, :]))
            vrec0 = B.cnt["vx"]
            B.wait("gpsimd", "vx", vrec0)
            for k_ in range(4):
                B.dma("gpsimd", d_scr[k_ * 2048:(k_ + 1) * 2048][None, :],
                      lsum[32 * k_:32 * k_ + 1, :], "lr")
            B.wait("gpsimd", "lr", B.cnt["lr"])
            g2 = B.dma("gpsimd", lrows[0:64, 0:128],
                       d_scr[0:NH * 512].rearrange("(r p) -> r p", p=128), "lr")
            B.wait("tensor", "lr", g2)
            m1 = B.op("tensor", "mm", lambda e: e.transpose(
                pb[6][:, 0:64], lrows[0:64, 0:128], ident[0:64, 0:64]))
            B.wait("vector", "mm", m1)
            vrec = B.vop(lambda e: e.tensor_copy(lsb[:, 0:64], pb[6][:, 0:64]))
            B.wait("tensor", "vx", vrec)

            for h in range(NH):
                for sub in range(4):
                    col = (h % 4) * 16 + (h // 4) * 4 + sub
                    B.wait("tensor", "vx", B.cnt["vx"])
                    m1 = B.op("tensor", "mm", lambda e, col=col: e.matmul(
                        pb[6][0:1, 0:128], lsb[:, col:col + 1], ident[:, :],
                        start=True, stop=True))
                    B.wait("vector", "mm", m1)
                    vv = B.vop(lambda e: e.tensor_copy(lrow[0:1, :], pb[6][0:1, 0:128]))
                    B.wait("tensor", "vx", vv)
                    m2 = B.op("tensor", "mm", lambda e: e.matmul(
                        pb[7][:, 0:128], ones_r[:, :], lrow[0:1, :],
                        start=True, stop=True))
                    B.wait("vector", "mm", m2)
                    B.vop(lambda e, h=h, sub=sub: e.tensor_tensor(
                        osb[:, h * 512 + sub * 128: h * 512 + (sub + 1) * 128],
                        osb[:, h * 512 + sub * 128: h * 512 + (sub + 1) * 128],
                        pb[7][:, 0:128], ALU.mult))
            on_done = B.cnt["vx"]

            for n in range(4):
                for h in range(NH):
                    bi = h % 2
                    wsem = "w0" if bi == 0 else "w1"
                    B.wait("sync", "mm", wuse[bi])
                    mw = B.dma("sync", wvb[bi].ap(),
                               woT[h * 128:(h + 1) * 128, n * 512:(n + 1) * 512], wsem)
                    B.wait("tensor", wsem, mw)
                    if h == 0:
                        B.wait("tensor", "vx", on_done)
                    for ms in range(4):
                        B.op("tensor", "mm", lambda e, h=h, ms=ms, bi=bi: e.matmul(
                            pb[ms][:, :],
                            osb[:, h * 512 + ms * 128: h * 512 + (ms + 1) * 128],
                            wvb[bi].ap(), start=(h == 0), stop=(h == NH - 1)),
                            inc=(1 if ms == 3 else 0))
                    wuse[bi] = B.cnt["mm"]
                B.wait("vector", "mm", B.cnt["mm"])
                evv = 0
                for ms in range(4):
                    mt = mb * 4 + ms
                    evv = B.vop(lambda e, ms=ms, mt=mt, n=n: e.tensor_tensor(
                        x_sb[:, mt * D + n * 512: mt * D + (n + 1) * 512],
                        x_sb[:, mt * D + n * 512: mt * D + (n + 1) * 512],
                        pb[ms][:, :], ALU.add))
                B.wait("tensor", "vx", evv)

    def ph_l1_ffn(bufs):
        ybuf = bufs["ybuf"]; st = bufs["st"]
        sc1 = bufs["sc1"]; sc2 = bufs["sc2"]
        s1r = bufs["s1r"]
        zb = bufs["zb"]; fb = bufs["fb"]; db = bufs["db"]
        ib = bufs["ib"]
        xif = bufs["xif"]
        xif2 = bufs["gbuf"]
        RW = 512
        xiT = bufs["xiT"]
        gbuf = bufs["gbuf"]
        k2T = bufs["k2T"]
        wsb = bufs["wsb"]
        mxc = bufs["mxc"]
        W = 1024

        def emit_round(src_ap, n, s_col, out_ap, out_cols):
            for cc in range(n // RW):
                sl = slice(cc * RW, (cc + 1) * RW)
                B.vop(lambda e, sl=sl: e.tensor_scalar(
                    zb[:, :], src_ap[:, sl], mxc[:, s_col:s_col + 1], None,
                    op0=ALU.mult))
                B.vop(lambda e: e.tensor_copy(ib[:, :], zb[:, :]))
                B.vop(lambda e: e.tensor_copy(fb[:, :], ib[:, :]))
                B.vop(lambda e: e.tensor_sub(db[:, :], zb[:, :], fb[:, :]))
                B.vop(lambda e: e.scalar_tensor_tensor(
                    out=zb[:, :], in0=db[:, :], scalar=0.5, in1=fb[:, :],
                    op0=ALU.is_gt, op1=ALU.add))
                B.vop(lambda e: e.tensor_single_scalar(
                    db[:, :], db[:, :], -0.5, ALU.is_lt))
                B.vop(lambda e, sl=sl, cc=cc: e.tensor_tensor(
                    out_ap[:, out_cols.start + cc * RW: out_cols.start + (cc + 1) * RW],
                    zb[:, :], db[:, :], ALU.subtract))

        for tgt, dsrc, ncol in ((sc1, d_s1, H1), (sc2, d_s2, D)):
            for cb in range(ncol // W):
                ml = B.dma("gpsimd", s1r[0:1, 0:W], dsrc[cb * W:(cb + 1) * W][None, :], "lr")
                B.wait("tensor", "lr", ml)
                B.wait("tensor", "vx", B.cnt["vx"])
                B.op("tensor", "mm", lambda e: e.matmul(
                    pb[4][:, 0:512], ones_r[:, :], s1r[0:1, 0:512],
                    start=True, stop=True))
                m2 = B.op("tensor", "mm", lambda e: e.matmul(
                    pb[5][:, 0:512], ones_r[:, :], s1r[0:1, 512:1024],
                    start=True, stop=True))
                B.wait("vector", "mm", m2)
                B.vop(lambda e, tgt=tgt, cb=cb: e.tensor_copy(
                    tgt[:, cb * W: cb * W + 512], pb[4][:, 0:512]))
                B.vop(lambda e, tgt=tgt, cb=cb: e.tensor_copy(
                    tgt[:, cb * W + 512: (cb + 1) * W], pb[5][:, 0:512]))
                B.wait("gpsimd", "mm", m2)

        wuse = {0: 0, 1: 0, 2: 0, 3: 0}
        prev_ln = 0
        for mt in range(NMT):
            xs = x_sb[:, mt * D:(mt + 1) * D]
            B.wait("scalar", "mm", B.cnt["mm"])
            B.wait("scalar", "vx", prev_ln)
            s3 = emit_ln(xs, ybuf, st, ybuf)
            prev_ln = s3
            B.vop(lambda e: e.reduce_max(mxc[:, 0:1], ybuf[:, 0:D], axis=AX.X,
                                         apply_absolute_value=True))
            B.vop(lambda e: e.tensor_single_scalar(mxc[:, 0:1], mxc[:, 0:1], 1e-5,
                                                   ALU.max))
            B.vop(lambda e: e.reciprocal(mxc[:, 1:2], mxc[:, 0:1]))
            B.vop(lambda e: e.tensor_single_scalar(mxc[:, 1:2], mxc[:, 1:2], 127.0,
                                                   ALU.mult))
            B.vop(lambda e: e.reciprocal(mxc[:, 2:3], mxc[:, 1:2]))
            emit_round(ybuf, D, 1, xif2, slice(0, D))
            xi_done = B.cnt["vx"]
            B.wait("tensor", "vx", xi_done)
            bank_evac = {0: 0, 1: 0}
            for k in range(16):
                bi = k % 2
                B.wait("tensor", "sc", bank_evac[bi])
                mmv = B.op("tensor", "mm", lambda e, k=k, bi=bi: e.transpose(
                    pb[bi][:, 0:128], xif2[:, k * 128:(k + 1) * 128], ident[:, :]))
                B.wait("scalar", "mm", mmv)
                bank_evac[bi] = B.sop(lambda e, k=k, bi=bi: e.copy(
                    xiT[:, k * 128:(k + 1) * 128], pb[bi][:, 0:128]))
            xiT_done_sc = B.cnt["sc"]

            prev_gelu = 0
            set_done = {0: B.cnt["vx"], 1: B.cnt["vx"]}
            for cb4 in range(4):
                bset = [pb[4 * (cb4 % 2) + i] for i in range(4)]
                B.wait("tensor", "vx", set_done[cb4 % 2])
                if cb4 == 0:
                    B.wait("tensor", "sc", xiT_done_sc)
                for kr in range(16):
                    bi = (cb4 * 16 + kr) % 4
                    wsem = B.WSEMS[bi]
                    B.wait("sync", "mm", wuse[bi])
                    mw = B.dma("sync", wsb[bi].ap(),
                               d_tq1l1[kr * 128:(kr + 1) * 128,
                                       cb4 * 2048:(cb4 + 1) * 2048], wsem)
                    B.wait("tensor", wsem, mw)
                    for j in range(4):
                        B.op("tensor", "mm",
                             lambda e, kr=kr, j=j, bi=bi, bset=bset: e.matmul(
                                 bset[j][:, :], xiT[:, kr * 128:(kr + 1) * 128],
                                 wsb[bi][:, j * 512:(j + 1) * 512],
                                 start=(kr == 0), stop=(kr == 15)),
                             inc=(1 if j == 3 else 0))
                    wuse[bi] = B.cnt["mm"]
                gmm = B.cnt["mm"]
                for j in range(4):
                    hb = cb4 * 4 + j
                    B.wait("vector", "mm", gmm)
                    B.wait("vector", "sc", prev_gelu)
                    B.vop(lambda e, hb=hb, j=j, bset=bset: e.tensor_tensor(
                        zb[:, 0:512], bset[j][:, :], sc1[:, hb * 512:(hb + 1) * 512],
                        ALU.mult))
                    vt_ = B.cnt["vx"]
                    B.wait("scalar", "vx", vt_)
                    prev_gelu = B.sop(lambda e, hb=hb: e.activation(
                        out=gbuf[:, hb * 512:(hb + 1) * 512], in_=zb[:, 0:512],
                        func=AF.Gelu_apprx_tanh, scale=mxc[:, 2:3]))
                set_done[cb4 % 2] = B.cnt["vx"]
            g_done = B.cnt["sc"]

            B.wait("vector", "sc", g_done)
            B.vop(lambda e: e.reduce_max(mxc[:, 0:1], gbuf[:, :], axis=AX.X,
                                         apply_absolute_value=True))
            B.vop(lambda e: e.tensor_single_scalar(mxc[:, 0:1], mxc[:, 0:1], 1e-5,
                                                   ALU.max))
            B.vop(lambda e: e.reciprocal(mxc[:, 1:2], mxc[:, 0:1]))
            B.vop(lambda e: e.tensor_single_scalar(mxc[:, 1:2], mxc[:, 1:2], 127.0,
                                                   ALU.mult))
            B.vop(lambda e: e.reciprocal(mxc[:, 2:3], mxc[:, 1:2]))
            tr_banks = {0: ("sc", 0), 1: ("sc", 0)}
            tr_done = {0: 0, 1: 0}
            for cc in range(H1 // RW):
                xpar = cc % 2
                xx = xif[:, xpar * RW:(xpar + 1) * RW]
                B.wait("vector", "mm", tr_done[xpar])
                B.vop(lambda e, cc=cc: e.tensor_scalar(
                    zb[:, :], gbuf[:, cc * RW:(cc + 1) * RW], mxc[:, 1:2], None,
                    op0=ALU.mult))
                B.vop(lambda e: e.tensor_copy(ib[:, :], zb[:, :]))
                B.vop(lambda e: e.tensor_copy(fb[:, :], ib[:, :]))
                B.vop(lambda e: e.tensor_sub(db[:, :], zb[:, :], fb[:, :]))
                B.vop(lambda e: e.scalar_tensor_tensor(
                    out=zb[:, :], in0=db[:, :], scalar=0.5, in1=fb[:, :],
                    op0=ALU.is_gt, op1=ALU.add))
                B.vop(lambda e: e.tensor_single_scalar(
                    db[:, :], db[:, :], -0.5, ALU.is_lt))
                k2_done = B.vop(lambda e, xx=xx: e.tensor_sub(xx, zb[:, :], db[:, :]))
                B.wait("tensor", "vx", k2_done)
                for k in range(RW // 128):
                    bi = k % 2
                    B.wait("tensor", tr_banks[bi][0], tr_banks[bi][1])
                    mmv = B.op("tensor", "mm", lambda e, k=k, bi=bi, xx=xx: e.transpose(
                        pb[bi][:, 0:128], xx[:, k * 128:(k + 1) * 128], ident[:, :]))
                    B.wait("scalar", "mm", mmv)
                    tr_banks[bi] = ("sc", B.sop(
                        lambda e, cc=cc, k=k, bi=bi: e.copy(
                            k2T[:, (cc * 4 + k) * 128:(cc * 4 + k + 1) * 128],
                            pb[bi][:, 0:128])))
                tr_done[xpar] = B.cnt["mm"]
            B.wait("vector", "sc", B.cnt["sc"])
            k2_all = B.vop(lambda e: e.engine_nop()) if False else B.cnt["vx"]
            k2_all_sc = B.cnt["sc"]

            for hc in range(64):
                bi = hc % 4
                wsem = B.WSEMS[bi]
                B.wait("sync", "mm", wuse[bi])
                mw = B.dma("sync", wsb[bi].ap(),
                           d_tq2l1[hc * 128:(hc + 1) * 128, :], wsem)
                B.wait("tensor", wsem, mw)
                if hc == 0:
                    B.wait("tensor", "vx", k2_all)
                    B.wait("tensor", "sc", k2_all_sc)
                for n in range(4):
                    B.op("tensor", "mm", lambda e, hc=hc, n=n, bi=bi: e.matmul(
                        pb[4 + n][:, :], k2T[:, hc * 128:(hc + 1) * 128],
                        wsb[bi][:, n * 512:(n + 1) * 512],
                        start=(hc == 0), stop=(hc == 63)),
                        inc=(1 if n == 3 else 0))
                wuse[bi] = B.cnt["mm"]
            g2 = B.cnt["mm"]
            B.wait("vector", "mm", g2)
            evv = 0
            for n in range(4):
                B.vop(lambda e, n=n: e.tensor_tensor(
                    zb[:, 0:512], pb[4 + n][:, :], sc2[:, n * 512:(n + 1) * 512],
                    ALU.mult))
                evv = B.vop(lambda e, n=n: e.scalar_tensor_tensor(
                    out=x_sb[:, mt * D + n * 512: mt * D + (n + 1) * 512],
                    in0=zb[:, 0:512], scalar=mxc[:, 2:3],
                    in1=x_sb[:, mt * D + n * 512: mt * D + (n + 1) * 512],
                    op0=ALU.mult, op1=ALU.add))
            B.wait("tensor", "vx", evv)

    def ph_out():
        B.dma("gpsimd", out.rearrange("(t p) d -> p t d", p=128),
              x_sb.ap().rearrange("p (t d) -> p t d", d=D), "od")
        for name, dst in dbg_outs.items():
            srcs = {"d_q": d_q, "d_kf": d_kf, "d_u": d_u, "d_vt": d_vt,
                    "d_C": d_C, "ag_out0": ag_out0.ap(),
                    "d_tq1l0": d_tq1l0, "d_tq2l0": d_tq2l0,
                    "d_tq1l1": d_tq1l1, "d_tq2l1": d_tq2l1,
                    "kwin": kwin, "vwin": vwin}
            B.dma("gpsimd", dst, srcs[name], "od")
        B.wait("gpsimd", "od", B.cnt["od"])

    B.phase(ph_const, name='const')

    if upto >= 2:
        with ExitStack() as wctx:
            def sbw(name, shape, dt=F32):
                return wctx.enter_context(nc.sbuf_tensor(name, shape, dt))
            bufs_w = dict(
                wsb=[sbw("ws0", [128, 1024]), sbw("ws1", [128, 1024])],
                dmp=sbw("dmp", [128, 1024]),
                acol=sbw("acol", [128, 1]),
                acol2=sbw("acol2", [128, 1]),
                acc=sbw("acc", [128, 2]),
                s1t=sbw("s1t", [128, 64]),
                s2t=sbw("s2t", [128, 16]),
                srow=sbw("srow", [1, 8]),
                tn=sbw("tn", [128, 1024]),
                tn2=sbw("tn2", [128, 1024]),
                qo=[sbw("qo0", [128, 1024], BF16), sbw("qo1", [128, 1024], BF16)],
                qo2=sbw("qo2", [128, 1024]),
                qo3=sbw("qo3", [128, 1024]),
                thp=sbw("thp", [128, 1024]),
                thn=sbw("thn", [128, 1024]),
                s1r=sbw("s1r", [1, 1024]),
            )
            B.phase(lambda: ph_wstats(bufs_w), name='wstats')
            B.phase(lambda: ph_tern(bufs_w), name='tern')

    l0outer = ExitStack()
    M_sb = l0outer.enter_context(nc.sbuf_tensor("M_sb", [128, NH * DH], F32))
    kv_sb = l0outer.enter_context(nc.sbuf_tensor("kv_sb", [128, NH * DH], F32))
    with ExitStack() as l1ctx:
        def sbl(name, shape, dt=F32):
            return l1ctx.enter_context(nc.sbuf_tensor(name, shape, dt))
        bufs1 = dict(
            yT=sbl("yT", [128, 16 * T]),
            wb=[sbl("wb0", [128, 16 * 128]), sbl("wb1", [128, 16 * 128])],
            ev=sbl("ev", [128, 1024]),
            ktok=sbl("ktok", [128, 8 * 128]),
            vtok=sbl("vtok", [128, 8 * 128]),
            Cb=sbl("Cb", [128, 4 * 128]),
            KD=sbl("KD", [128, 2]),
            ybuf=sbl("ybuf", [128, D]),
            st=stats_tiles(sbl, "s1_"),
        )
        B.phase(lambda: ph_l0_sweep1(bufs1), name='l0s1')

    with ExitStack() as l2ctx:
        def sbl2(name, shape, dt=F32):
            return l2ctx.enter_context(nc.sbuf_tensor(name, shape, dt))
        bufs2 = dict(
            qb=sbl2("qb", [128, NH * BLK]),
            kb=sbl2("kb", [128, NH * BLK]),
            vtb=sbl2("vtb", [128, 2 * D]),
            ub=sbl2("ub", [128, NH * BLK]),
            Cl=sbl2("Cl", [128, NH * 128]),
            ob=sbl2("ob", [128, NH * BLK]),
            agm=sbl2("agm", [128, NH * 128]),
            qp=sbl2("qp", [128, BLK]),
            qd=sbl2("qd", [128, BLK]),
            dct=sbl2("dct", [128, 2 * BLK]),
            pbuf=sbl2("pbuf", [128, 2 * BLK]),
            djc=sbl2("djc", [128, 1]),
            tmph=sbl2("tmph", [128, 128]),
            wv=[sbl2("wv0", [128, 512]), sbl2("wv1", [128, 512])],
            dump=sbl2("dump2", [128, BLK]),
            rmsc=sbl2("rmsc", [128, 2]),
            r_a=sbl2("r_a", [1, BLK]),
            r_b=sbl2("r_b", [1, BLK]),
            r_c=sbl2("r_c", [1, BLK]),
        )
        B.phase(lambda: ph_l0_sweep2(bufs2), name='l0s2')
    l0outer.close()

    if upto >= 2:
        with ExitStack() as fctx:
            def sbf(name, shape, dt=F32):
                return fctx.enter_context(nc.sbuf_tensor(name, shape, dt))
            bufs3 = dict(
                ybuf=sbf("f_ybuf", [128, D]),
                st=stats_tiles(sbf, "f_st_"),
                y2h=sbf("y2h", [128, 16 * BLK], BF16),
                y2l=sbf("y2l", [128, 16 * BLK], BF16),
                wsb=[sbf(f"fw{i}", [128, 2048], BF16) for i in range(4)],
                rb=sbf("rb", [128, BLK]),
                sqf=sbf("sqf", [128, BLK]),
                sqh=sbf("sqh", [128, 40 * BLK], BF16),
                sql=sbf("sql", [128, 40 * BLK], BF16),
            )
            B.phase(lambda: ph_l0_ffn(bufs3), name='l0f')

    if upto >= 3:
        with ExitStack() as actx:
            def sba(name, shape, dt=F32):
                return actx.enter_context(nc.sbuf_tensor(name, shape, dt))
            bufs4 = dict(
                y3T=sba("y3T", [128, 16 * T]),
                wb=[sba("aw0", [128, 16 * 128]), sba("aw1", [128, 16 * 128])],
                ev=sba("a_ev", [128, 1024]),
                vtok=sba("a_vtok", [128, 8 * 128]),
                ybuf=sba("a_ybuf", [128, D]),
                st=stats_tiles(sba, "a_st_"),
            )
            B.phase(lambda: ph_l1_attn_a(bufs4), name='l1a_a')
        with ExitStack() as bctx:
            def sbb(name, shape, dt=F32):
                return bctx.enter_context(nc.sbuf_tensor(name, shape, dt))
            bufs5 = dict(
                qmb=sbb("qmb", [128, NH * 512]),
                osb=sbb("osb", [128, NH * 512]),
                masks=sbb("masks", [128, NKT * 512], BF16),
                ktile=[sbb("ktile0", [128, 2048]), sbb("ktile1", [128, 2048])],
                vtile=[sbb("vtile0", [128, 2048]), sbb("vtile1", [128, 2048])],
                tbv=[sbb(f"tbv{i}", [128, 512]) for i in range(3)],
                pbm=[sbb(f"pbm{i}", [128, 512]) for i in range(3)],
                lsb=sbb("lsb", [128, 64]),
                lsum=sbb("lsum", [128, 2048]),
                lrows=sbb("lrows", [64, 128]),
                lrow=sbb("lrow", [1, 128]),
                wvb=[sbb("bw0", [128, 512]), sbb("bw1", [128, 512])],
            )
            B.phase(lambda: ph_l1_attn_b(bufs5), name='l1a_b')

    if upto >= 4:
        with ExitStack() as gctx:
            def sbg(name, shape, dt=F32):
                return gctx.enter_context(nc.sbuf_tensor(name, shape, dt))
            bufs6 = dict(
                ybuf=sbg("g_ybuf", [128, D]),
                st=stats_tiles(sbg, "g_st_"),
                sc1=sbg("sc1", [128, H1]),
                sc2=sbg("sc2", [128, D]),
                s1r=sbg("g_s1r", [1, 1024]),
                zb=sbg("g_zb", [128, 512]),
                fb=sbg("g_fb", [128, 512]),
                db=sbg("g_db", [128, 512]),
                ib=sbg("g_ib", [128, 512], I32),
                xif=sbg("g_xif", [128, 2 * 512]),
                xiT=sbg("g_xiT", [128, 16 * 128], BF16),
                gbuf=sbg("g_gbuf", [128, H1]),
                k2T=sbg("g_k2T", [128, 64 * 128], BF16),
                wsb=[sbg(f"g_w{i}", [128, 2048], BF16) for i in range(4)],
                mxc=sbg("g_mxc", [128, 4]),
            )
            B.phase(lambda: ph_l1_ffn(bufs6), name='l1f')

    B.phase(ph_out, name='out')
    B.ctx.close()
    return nc


def _tp(a):
    return np.ascontiguousarray(np.asarray(a, dtype=np.float32).T)


def _pack4(w):
    r = w.shape[0] // D
    t = np.asarray(w, np.float32).reshape(r, NH, 128, NH, 128)
    return np.ascontiguousarray(t.transpose(0, 1, 4, 3, 2)).reshape(r, NH, 128, D)


def _masks_for_core(c):
    import ml_dtypes
    m = np.zeros((2, NKT, 128, 512), dtype=ml_dtypes.bfloat16)
    sp = np.arange(128)[:, None]
    mp = np.arange(512)[None, :]
    for mb in range(2):
        gm = T * c + 512 * mb
        for k in range(NKT):
            bj = gm - WIN + 128 * k
            i = gm + mp
            j = bj + sp
            ok = (j <= i) & (i - j < WIN) & (j >= 0)
            m[mb, k] = ok.astype(np.float32)
    return m


def kernel(L0_attn_x, L0_attn_slope_rate, L0_attn_W_qkvu, L0_attn_W_out,
           L0_ffn_w_fc, L0_ffn_w_proj, L1_attn_Wq, L1_attn_Wk, L1_attn_Wv,
           L1_attn_Wo, L1_ffn_w1, L1_ffn_w2, _upto=4, _dbg=()):
    x = np.ascontiguousarray(np.asarray(L0_attn_x, np.float32).reshape(NCORES * T, D))
    shared = dict(
        w4qkvu=_pack4(L0_attn_W_qkvu), woutT=_tp(L0_attn_W_out),
        wfcT=_tp(L0_ffn_w_fc), wprojT=_tp(L0_ffn_w_proj),
        w3q=_pack4(L1_attn_Wq)[0], w3k=_pack4(L1_attn_Wk)[0],
        w3v=_pack4(L1_attn_Wv)[0],
        woT=_tp(L1_attn_Wo),
        w1n=np.ascontiguousarray(np.asarray(L1_ffn_w1, np.float32)),
        w1T=_tp(L1_ffn_w1),
        w2n=np.ascontiguousarray(np.asarray(L1_ffn_w2, np.float32)),
        w2T=_tp(L1_ffn_w2),
        slope_in=np.ascontiguousarray(
            np.asarray(L0_attn_slope_rate, np.float32).reshape(NH)),
    )
    in_maps = []
    for c in range(NCORES):
        m = dict(shared)
        m["xin"] = np.ascontiguousarray(x[c * T:(c + 1) * T])
        m["gates_in"] = (np.arange(NCORES) < c).astype(np.float32)
        m["lmask"] = _masks_for_core(c)
        in_maps.append(m)
    nc = build_nc(upto=_upto, dbg=_dbg)
    res = run_bass_kernel_spmd(nc, in_maps, core_ids=list(range(NCORES)))
    outs = np.concatenate([res.results[c]["out"] for c in range(NCORES)], axis=0)
    if _dbg:
        dbgd = [{k: v for k, v in res.results[c].items() if k.startswith("dbg_")}
                for c in range(NCORES)]
        return outs.reshape(1, NCORES * T, D), dbgd
    return outs.reshape(1, NCORES * T, D)


# version 09
# speedup vs baseline: 1.0058x; 1.0058x over previous
import numpy as np
from contextlib import ExitStack

import concourse.bass as bass
import concourse.mybir as mybir
from concourse.bass_utils import run_bass_kernel_spmd

F32 = mybir.dt.float32
BF16 = mybir.dt.bfloat16
I32 = mybir.dt.int32
AF = mybir.ActivationFunctionType
ALU = mybir.AluOpType
AX = mybir.AxisListType

NCORES = 8
T = 1024
D = 2048
NH = 16
DH = 128
BLK = 256
NBLK = T // BLK
NMT = T // 128
H0 = 5120
H1 = 8192
WIN = 1024
NKT = 12

ENGS = ("sync", "gpsimd", "tensor", "vector", "scalar")


class _B:

    def __init__(self, nc):
        self.nc = nc
        self.ctx = ExitStack()
        self.sem = {}
        self.cnt = {}
        for name in ("wd", "gd", "mm", "vx", "sc", "gx", "cc", "bar", "od",
                     "w0", "w1", "w2", "w3", "ev", "vt", "cs", "lq", "lk",
                     "lv", "lu", "lc", "lr"):
            self.sem[name] = self.ctx.enter_context(nc.semaphore(name))
            self.cnt[name] = 0
        self.en = None
        self.eng = None

    def dma(self, qeng, out, in_, sem="gd"):
        self.cnt[sem] += 16
        if self.en == qeng:
            if callable(out):
                out = out()
            if callable(in_):
                in_ = in_()
            self.eng.dma_start(out=out, in_=in_).then_inc(self.sem[sem], 16)
        return self.cnt[sem]

    AUTO_CHAIN = {"vector": "vx", "scalar": "sc"}

    def op(self, oeng, sem, fn, inc=1):
        own = self.AUTO_CHAIN.get(oeng)
        if own is not None and self.en == oeng and self.cnt[own] > 0:
            self.eng.wait_ge(self.sem[own], self.cnt[own])
        if inc:
            self.cnt[sem] += inc
        if self.en == oeng:
            inst = fn(self.eng)
            if inc:
                inst.then_inc(self.sem[sem], inc)
        return self.cnt[sem]

    def vop(self, fn):
        return self.op("vector", "vx", fn)

    def sop(self, fn):
        return self.op("scalar", "sc", fn)

    def sil(self, oeng, fn):
        if self.en == oeng:
            fn(self.eng)

    def wait(self, oeng, sem, val):
        if self.en == oeng and val > 0:
            self.eng.wait_ge(self.sem[sem], val)

    WSEMS = ("w0", "w1", "w2", "w3")

    ENG_SEMS = {"sync": ("wd", "w0", "w1", "w2", "w3"),
                "gpsimd": ("gd", "gx", "od", "cc", "ev", "vt", "cs",
                           "lq", "lk", "lv", "lu", "lc", "lr"),
                "tensor": ("mm",), "vector": ("vx",), "scalar": ("sc",)}

    def barrier(self):
        self.cnt["bar"] += len(ENGS)
        tgt = self.cnt["bar"]
        snap = dict(self.cnt)
        with self.nc.Block() as blk:
            for name in ENGS:
                def body(eng, name=name, tgt=tgt, snap=snap):
                    for s in self.ENG_SEMS[name]:
                        if snap[s] > 0:
                            eng.wait_ge(self.sem[s], snap[s])
                    eng.sem_inc(self.sem["bar"], 1)
                    eng.wait_ge(self.sem["bar"], tgt)
                getattr(blk, name)(body)

    def chain(self, oeng, sem, fns, first_wait=None):
        v = None
        for i, fn in enumerate(fns):
            if i == 0 and first_wait is not None:
                self.wait(oeng, first_wait[0], first_wait[1])
            if v is not None:
                self.wait(oeng, sem, v)
            v = self.op(oeng, sem, fn)
        return v

    def phase(self, emit_fn, name=None):
        snap = dict(self.cnt)
        final = None
        scope = None
        if name is not None:
            try:
                scope = self.nc.enter_named_scope(name, False)
            except Exception:
                scope = None
        with self.nc.Block() as blk:
            for name in ENGS:
                self.cnt = dict(snap)
                self.en = name

                def body(eng):
                    self.eng = eng
                    emit_fn()
                getattr(blk, name)(body)
                if final is None:
                    final = dict(self.cnt)
                else:
                    assert final == self.cnt, "phase emitter is engine-dependent!"
        self.en = self.eng = None
        if scope is not None:
            try:
                self.nc.leave_named_scope(name, scope[0], False)
            except Exception:
                pass
        self.barrier()


def build_nc(upto=4, dbg=(), world=NCORES):
    nc = bass.Bass("TRN2", target_bir_lowering=False)
    B = _B(nc)

    xin = nc.dram_tensor("xin", [T, D], F32, kind="ExternalInput").ap()
    w4qkvu = nc.dram_tensor("w4qkvu", [4, NH, 128, 2048], F32, kind="ExternalInput").ap()
    woutT = nc.dram_tensor("woutT", [D, D], F32, kind="ExternalInput").ap()
    wfcT = nc.dram_tensor("wfcT", [D, H0], F32, kind="ExternalInput").ap()
    wprojT = nc.dram_tensor("wprojT", [H0, D], F32, kind="ExternalInput").ap()
    w3q = nc.dram_tensor("w3q", [NH, 128, 2048], F32, kind="ExternalInput").ap()
    w3k = nc.dram_tensor("w3k", [NH, 128, 2048], F32, kind="ExternalInput").ap()
    w3v = nc.dram_tensor("w3v", [NH, 128, 2048], F32, kind="ExternalInput").ap()
    woT = nc.dram_tensor("woT", [D, D], F32, kind="ExternalInput").ap()
    w1n = nc.dram_tensor("w1n", [H1, D], F32, kind="ExternalInput").ap()
    w1T = nc.dram_tensor("w1T", [D, H1], F32, kind="ExternalInput").ap()
    w2n = nc.dram_tensor("w2n", [D, H1], F32, kind="ExternalInput").ap()
    w2T = nc.dram_tensor("w2T", [H1, D], F32, kind="ExternalInput").ap()
    slope_in = nc.dram_tensor("slope_in", [NH], F32, kind="ExternalInput").ap()
    gates_in = nc.dram_tensor("gates_in", [NCORES], F32, kind="ExternalInput").ap()
    lmask = nc.dram_tensor("lmask", [2, NKT, 128, 512], BF16, kind="ExternalInput").ap()
    out = nc.dram_tensor("out", [T, D], F32, kind="ExternalOutput").ap()

    d_q = nc.dram_tensor("d_q", [D, T], F32).ap()
    d_kf = nc.dram_tensor("d_kf", [D, T], F32).ap()
    d_u = nc.dram_tensor("d_u", [D, T], F32).ap()
    d_vt = nc.dram_tensor("d_vt", [T, D], F32).ap()
    d_C = nc.dram_tensor("d_C", [NH, NBLK, DH, DH], F32).ap()
    ag_in0 = nc.dram_tensor("ag_in0", [NH * DH, DH], F32)
    ag_out0 = nc.dram_tensor("ag_out0", [world * NH * DH, DH], F32, addr_space=("Shared" if world > 1 else "Local"))
    ag_in_k = nc.dram_tensor("ag_in_k", [D, T], F32)
    ag_out_k = nc.dram_tensor("ag_out_k", [world * D, T], F32, addr_space=("Shared" if world > 1 else "Local"))
    ag_in_v = nc.dram_tensor("ag_in_v", [T, D], F32)
    ag_out_v = nc.dram_tensor("ag_out_v", [world * T, D], F32, addr_space=("Shared" if world > 1 else "Local"))
    kwin = nc.dram_tensor("kwin", [D, 2 * T], F32).ap()
    vwin = nc.dram_tensor("vwin", [2 * T, D], F32).ap()
    d_tq1l0 = nc.dram_tensor("d_tq1l0", [D, H0], BF16).ap()
    d_tq2l0 = nc.dram_tensor("d_tq2l0", [H0, D], BF16).ap()
    d_tq1l1 = nc.dram_tensor("d_tq1l1", [D, H1], BF16).ap()
    d_tq2l1 = nc.dram_tensor("d_tq2l1", [H1, D], BF16).ap()
    d_scr = nc.dram_tensor("d_scr", [8192], F32).ap()
    d_s1 = nc.dram_tensor("d_s1", [H1], F32).ap()
    d_s2 = nc.dram_tensor("d_s2", [D], F32).ap()

    dbg_specs = {"d_q": ([D, T], F32), "d_kf": ([D, T], F32), "d_u": ([D, T], F32),
                 "d_vt": ([T, D], F32), "d_C": ([NH, NBLK, DH, DH], F32),
                 "ag_out0": ([world * NH * DH, DH], F32),
                 "kwin": ([D, 2 * T], F32), "vwin": ([2 * T, D], F32),
                 "d_tq1l0": ([D, H0], BF16), "d_tq2l0": ([H0, D], BF16),
                 "d_tq1l1": ([D, H1], BF16), "d_tq2l1": ([H1, D], BF16)}
    dbg_outs = {n: nc.dram_tensor("dbg_" + n, dbg_specs[n][0], dbg_specs[n][1],
                                  kind="ExternalOutput").ap() for n in dbg}

    ctx = B.ctx

    def sb(name, shape, dt=F32):
        return ctx.enter_context(nc.sbuf_tensor(name, shape, dt))

    x_sb = sb("x_sb", [128, NMT * D])
    ident = sb("ident", [128, 128])
    ones_c = sb("ones_c", [128, 1])
    ones_r = sb("ones_r", [1, 128])
    slope_row = sb("slope_row", [1, NH])
    negs_bc = sb("negs_bc", [128, NH])
    bd_bc = sb("bd_bc", [128, NH])
    bd4m1_bc = sb("bd4m1_bc", [128, NH])
    G = sb("G", [128, NCORES])
    iotaq = sb("iotaq", [128, BLK])
    idxd = sb("idxd", [128, 2 * BLK])
    io255 = sb("io255", [128, 1])
    neg30 = sb("neg30", [128, 1])
    io127 = sb("io127", [128, 1])
    am_fc = sb("am_fc", [128, 1])
    am_pr = sb("am_pr", [128, 1])
    thr_fc = sb("thr_fc", [128, 1])
    thr_pr = sb("thr_pr", [128, 1])
    nthr_fc = sb("nthr_fc", [128, 1])
    nthr_pr = sb("nthr_pr", [128, 1])
    pb = [ctx.enter_context(nc.psum_tensor(f"pb{i}", [128, 512], F32)) for i in range(8)]
    M_sb = kv_sb = None

    def ph_const():
        B.dma("sync", slope_row[0:1, :], slope_in[None, :], "wd")
        B.dma("sync", G[0:1, 0:NCORES], gates_in[None, :], "wd")
        m_slope = m_gates = B.cnt["wd"]

        B.op("gpsimd", "gx", lambda e: e.memset(ones_c[:, :], 1.0))
        B.op("gpsimd", "gx", lambda e: e.memset(ones_r[:, :], 1.0))
        v = B.op("gpsimd", "gx", lambda e: e.memset(ident[:, :], 0.0))
        B.wait("gpsimd", "gx", v)
        B.op("gpsimd", "gx", lambda e: e.affine_select(
            out=ident[:, :], in_=ident[:, :], compare_op=ALU.not_equal,
            fill=1.0, base=0, pattern=[[-1, 128]], channel_multiplier=1))
        B.op("gpsimd", "gx", lambda e: e.iota(iotaq[:, :], [[1, BLK]], base=1,
                                         channel_multiplier=0, allow_small_or_imprecise_dtypes=True))
        B.op("gpsimd", "gx", lambda e: e.memset(neg30[:, :], -30.0))
        v2 = B.op("gpsimd", "gx", lambda e: e.iota(
            idxd[:, :], [[-128, 2], [1, BLK]], base=0, channel_multiplier=-1, allow_small_or_imprecise_dtypes=True))
        B.wait("gpsimd", "gx", v2)
        B.op("gpsimd", "gx", lambda e: e.affine_select(
            out=idxd[:, :], in_=idxd[:, :], compare_op=ALU.is_ge,
            fill=1.0e30, base=0, pattern=[[-128, 2], [1, BLK]],
            channel_multiplier=-1))
        B.op("gpsimd", "gx", lambda e: e.iota(io255[:, :], [[0, 1]], base=255,
                                         channel_multiplier=-1, allow_small_or_imprecise_dtypes=True))
        v4 = B.op("gpsimd", "gx", lambda e: e.iota(io127[:, :], [[0, 1]], base=127,
                                                   channel_multiplier=-1, allow_small_or_imprecise_dtypes=True))

        B.wait("scalar", "wd", m_slope)
        B.op("scalar", "sc", lambda e: e.activation(
            out=bd_bc[0:1, :], in_=slope_row[0:1, :], func=AF.Exp, scale=-256.0))
        B.op("scalar", "sc", lambda e: e.activation(
            out=bd4m1_bc[0:1, :], in_=slope_row[0:1, :], func=AF.Exp, scale=-1024.0))
        s_ng = B.op("scalar", "sc", lambda e: e.mul(
            negs_bc[0:1, :], slope_row[0:1, :], -1.0))

        B.wait("tensor", "sc", s_ng)
        B.wait("tensor", "gx", v4)
        B.wait("tensor", "wd", m_gates)
        B.op("tensor", "mm", lambda e: e.matmul(
            pb[0][:, 0:NH], ones_r[:, :], negs_bc[0:1, :], start=True, stop=True))
        B.op("tensor", "mm", lambda e: e.matmul(
            pb[0][:, 64:64 + NH], ones_r[:, :], bd_bc[0:1, :], start=True, stop=True))
        B.op("tensor", "mm", lambda e: e.matmul(
            pb[0][:, 128:128 + NH], ones_r[:, :], bd4m1_bc[0:1, :], start=True, stop=True))
        mmv = B.op("tensor", "mm", lambda e: e.matmul(
            pb[0][:, 192:192 + NCORES], ones_r[:, :], G[0:1, 0:NCORES],
            start=True, stop=True))

        B.wait("vector", "mm", mmv)
        B.op("vector", "vx", lambda e: e.tensor_copy(negs_bc[:, :], pb[0][:, 0:NH]))
        B.op("vector", "vx", lambda e: e.tensor_copy(bd_bc[:, :], pb[0][:, 64:64 + NH]))
        B.op("vector", "vx", lambda e: e.tensor_copy(G[:, :], pb[0][:, 192:192 + NCORES]))
        B.op("vector", "vx", lambda e: e.tensor_single_scalar(
            bd4m1_bc[:, :], pb[0][:, 128:128 + NH], 1.0, ALU.subtract))

        B.dma("gpsimd", x_sb.ap().rearrange("p (t d) -> p t d", d=D),
              xin.rearrange("(t p) d -> p t d", p=128), "gd")
        B.wait("gpsimd", "gd", B.cnt["gd"])

    def emit_ln(xs, ybuf, st, dump):
        s1 = B.op("scalar", "sc", lambda e: e.activation(
            out=dump[:, 0:D], in_=xs, func=AF.Square, accum_out=st["ssq"][:, 0:1]))
        B.op("vector", "vx", lambda e: e.reduce_sum(
            st["ssum"][:, 0:1], xs, axis=AX.X))
        B.wait("vector", "sc", s1)
        B.op("vector", "vx", lambda e: e.tensor_single_scalar(
            st["mean"][:, 0:1], st["ssum"][:, 0:1], 1.0 / D, ALU.mult))
        B.op("vector", "vx", lambda e: e.tensor_single_scalar(
            st["var"][:, 0:1], st["ssq"][:, 0:1], 1.0 / D, ALU.mult))
        B.op("vector", "vx", lambda e: e.tensor_tensor(
            st["m2"][:, 0:1], st["mean"][:, 0:1], st["mean"][:, 0:1], ALU.mult))
        v2 = B.op("vector", "vx", lambda e: e.scalar_tensor_tensor(
            out=st["var"][:, 0:1], in0=st["var"][:, 0:1], scalar=1e-5,
            in1=st["m2"][:, 0:1], op0=ALU.add, op1=ALU.subtract))
        B.wait("scalar", "vx", v2)
        s2 = B.op("scalar", "sc", lambda e: e.activation(
            out=st["rstd"][:, 0:1], in_=st["var"][:, 0:1], func=AF.Sqrt))
        B.wait("vector", "sc", s2)
        B.op("vector", "vx", lambda e: e.reciprocal(st["rstd"][:, 0:1], st["rstd"][:, 0:1]))
        B.op("vector", "vx", lambda e: e.tensor_tensor(
            st["m2"][:, 0:1], st["rstd"][:, 0:1], st["rstd"][:, 0:1], ALU.mult))
        B.op("vector", "vx", lambda e: e.tensor_tensor(
            st["m2"][:, 0:1], st["m2"][:, 0:1], st["var"][:, 0:1], ALU.mult))
        B.op("vector", "vx", lambda e: e.tensor_scalar(
            st["m2"][:, 0:1], st["m2"][:, 0:1], -0.5, 1.5, op0=ALU.mult, op1=ALU.add))
        B.op("vector", "vx", lambda e: e.tensor_tensor(
            st["rstd"][:, 0:1], st["rstd"][:, 0:1], st["m2"][:, 0:1], ALU.mult))
        v4 = B.op("vector", "vx", lambda e: e.tensor_scalar(
            ybuf[:, 0:D], xs, st["mean"][:, 0:1], st["rstd"][:, 0:1],
            op0=ALU.subtract, op1=ALU.mult))
        return v4

    def stats_tiles(alloc, pfx):
        return {k: alloc(pfx + k, [128, 1]) for k in
                ("ssq", "ssum", "mean", "var", "m2", "rstd", "nmr")}

    def ph_l0_sweep1(bufs):
        yT = bufs["yT"]; wb = bufs["wb"]; ev = bufs["ev"]
        ktok = bufs["ktok"]; vtok = bufs["vtok"]; Cb = bufs["Cb"]
        KD = bufs["KD"]; ybuf = bufs["ybuf"]; st = bufs["st"]
        dump = ybuf

        bank_evac = {0: 0, 1: 0}
        for mt in range(NMT):
            s3 = emit_ln(x_sb[:, mt * D:(mt + 1) * D], ybuf, st, dump)
            B.wait("tensor", "vx", s3)
            for k in range(16):
                bi = k % 2
                B.wait("tensor", "vx", bank_evac[bi])
                mmv = B.op("tensor", "mm", lambda e, k=k, bi=bi: e.transpose(
                    pb[bi][:, 0:128], ybuf[:, k * 128:(k + 1) * 128], ident[:, :]))
                B.wait("vector", "mm", mmv)
                bank_evac[bi] = B.op("vector", "vx",
                                     lambda e, k=k, mt=mt, bi=bi: e.tensor_copy(
                                         yT[:, k * T + mt * 128: k * T + (mt + 1) * 128],
                                         pb[bi][:, 0:128]))
            B.wait("scalar", "mm", B.cnt["mm"])

        yT_done = B.cnt["vx"]
        vM = B.op("vector", "vx", lambda e: e.memset(M_sb[:, :], 0.0))
        B.wait("tensor", "vx", yT_done)

        dests = [d_q, d_kf, None, d_u]
        wlast = {0: 0, 1: 0}
        prev_evac = ("vx", 0)
        ev_spill = 0
        tr_bank_evac = {0: 0, 1: 0}
        vtok_spill = 0
        c_spill = 0
        kd_ready = 0
        for h in range(NH):
            B.wait("scalar", "vx", max(tr_bank_evac.values()))
            s_kd1 = B.op("scalar", "sc", lambda e, h=h: e.activation(
                out=KD[:, 0:1], in_=io255[:, :], func=AF.Exp, scale=negs_bc[:, h:h + 1]))
            kd_ready = B.op("scalar", "sc", lambda e, h=h: e.activation(
                out=KD[:, 1:2], in_=io127[:, :], func=AF.Exp, scale=negs_bc[:, h:h + 1]))
            for r in range(4):
                bi = (h * 4 + r) % 2
                wsem = "w0" if bi == 0 else "w1"
                B.wait("sync", "mm", wlast[bi])
                mw = B.dma("sync", wb[bi].ap(), w4qkvu[r, h], wsem)
                B.wait("tensor", wsem, mw)
                B.wait("tensor", prev_evac[0], prev_evac[1])
                for half in range(2):
                    for kc in range(16):
                        B.op("tensor", "mm",
                             lambda e, kc=kc, half=half, bi=bi:
                             e.matmul(pb[2 + half][:, :],
                                      wb[bi][:, kc * 128:(kc + 1) * 128],
                                      yT[:, kc * T + half * 512: kc * T + (half + 1) * 512],
                                      start=(kc == 0), stop=(kc == 15)),
                             inc=(1 if kc == 15 else 0))
                wlast[bi] = B.cnt["mm"]
                g1 = B.cnt["mm"]
                eeng, esem = (("scalar", "sc") if r in (0, 1) else ("vector", "vx"))
                B.wait(eeng, "mm", g1)
                B.wait(eeng, "ev", ev_spill)
                if r in (0, 1):
                    B.op("scalar", "sc", lambda e: e.activation(
                        out=ev[:, 0:512], in_=pb[2][:, :], func=AF.Silu))
                    e2 = B.op("scalar", "sc", lambda e: e.activation(
                        out=ev[:, 512:1024], in_=pb[3][:, :], func=AF.Silu))
                else:
                    B.op("vector", "vx", lambda e: e.tensor_copy(ev[:, 0:512], pb[2][:, :]))
                    e2 = B.op("vector", "vx", lambda e: e.tensor_copy(
                        ev[:, 512:1024], pb[3][:, :]))
                prev_evac = (esem, e2)
                if r != 2:
                    B.wait("gpsimd", esem, e2)
                    ev_spill = B.dma("gpsimd", dests[r][h * 128:(h + 1) * 128, :],
                                     ev[:, 0:1024], "ev")
                if r in (1, 2):
                    tgt = ktok if r == 1 else vtok
                    B.wait("tensor", esem, e2)
                    if r == 1:
                        B.wait("tensor", "mm", 0)
                    for t in range(8):
                        tb = t % 2
                        B.wait("tensor", "vx", tr_bank_evac[tb])
                        mmv = B.op("tensor", "mm", lambda e, t=t, tb=tb: e.transpose(
                            pb[4 + tb][:, 0:128], ev[:, t * 128:(t + 1) * 128],
                            ident[:, :]))
                        B.wait("vector", "mm", mmv)
                        if r == 1:
                            B.wait("vector", "sc", kd_ready)
                            B.wait("vector", "gd", 0)
                            tr_bank_evac[tb] = B.op(
                                "vector", "vx", lambda e, t=t, tb=tb: e.tensor_scalar(
                                    tgt[:, t * 128:(t + 1) * 128], pb[4 + tb][:, 0:128],
                                    KD[:, (t % 2):(t % 2) + 1], None, op0=ALU.mult))
                        else:
                            B.wait("vector", "vt", vtok_spill)
                            tr_bank_evac[tb] = B.op(
                                "vector", "vx", lambda e, t=t, tb=tb: e.tensor_copy(
                                    tgt[:, t * 128:(t + 1) * 128], pb[4 + tb][:, 0:128]))
                    if r == 2:
                        B.wait("gpsimd", "vx", tr_bank_evac[(7) % 2])
                        vtok_spill = B.dma(
                            "gpsimd",
                            d_vt.rearrange("(t p) d -> p t d", p=128)[:, :, h * 128:(h + 1) * 128],
                            vtok.ap().rearrange("p (t d) -> p t d", d=128), "vt")
            B.wait("tensor", "vx", max(tr_bank_evac[0], tr_bank_evac[1]))
            vM_h = vM
            for b in range(NBLK):
                mmv = 0
                for sc_ in range(2):
                    t = 2 * b + sc_
                    mmv = B.op("tensor", "mm", lambda e, t=t: e.matmul(
                        pb[6][:, 0:128], ktok[:, t * 128:(t + 1) * 128],
                        vtok[:, t * 128:(t + 1) * 128],
                        start=(t % 2 == 0), stop=(t % 2 == 1)),
                        inc=(1 if t % 2 == 1 else 0))
                B.wait("vector", "mm", mmv)
                B.wait("vector", "cs", c_spill)
                B.op("vector", "vx", lambda e, b=b: e.tensor_copy(
                    Cb[:, b * 128:(b + 1) * 128], pb[6][:, 0:128]))
                vM_h = B.op("vector", "vx", lambda e, h=h: e.scalar_tensor_tensor(
                    out=M_sb[:, h * 128:(h + 1) * 128],
                    in0=M_sb[:, h * 128:(h + 1) * 128],
                    scalar=bd_bc[:, h:h + 1], in1=pb[6][:, 0:128],
                    op0=ALU.mult, op1=ALU.add))
                B.wait("tensor", "vx", vM_h)
            vM = vM_h
            B.wait("gpsimd", "vx", vM)
            c_spill = B.dma("gpsimd", d_C[h].rearrange("b p e -> p b e"),
                            Cb.ap().rearrange("p (b e) -> p b e", e=128), "cs")

        B.dma("gpsimd", ag_in0.ap().rearrange("(h p) e -> p h e", p=128),
              M_sb.ap().rearrange("p (h e) -> p h e", e=128), "gd")
        for s_ in ("gd", "ev", "vt", "cs"):
            B.wait("gpsimd", s_, B.cnt[s_])
        B.op("gpsimd", "cc", lambda e: e.collective_compute(
            "AllGather", ALU.bypass, replica_groups=[list(range(world))],
            ins=[ag_in0.ap().opt()], outs=[ag_out0.ap().opt()]))

    def ph_l0_sweep2(bufs):
        qb = bufs["qb"]; kb = bufs["kb"]; vtb = bufs["vtb"]; ub = bufs["ub"]
        Cl = bufs["Cl"]; ob = bufs["ob"]; agm = bufs["agm"]
        qp = bufs["qp"]; qd = bufs["qd"]; dct = bufs["dct"]; pbuf = bufs["pbuf"]
        djc = bufs["djc"]; tmph = bufs["tmph"]; wv = bufs["wv"]
        dump = bufs["dump"]; rmsc = bufs["rmsc"]
        r_a = bufs["r_a"]; r_b = bufs["r_b"]; r_c = bufs["r_c"]

        B.wait("gpsimd", "cc", B.cnt["cc"])
        vz = B.op("vector", "vx", lambda e: e.memset(kv_sb[:, :], 0.0))
        for j in range(world):
            B.wait("gpsimd", "vx", vz)
            agl = B.dma("gpsimd", agm.ap().rearrange("p (h e) -> p h e", e=128),
                        ag_out0.ap()[j * NH * DH:(j + 1) * NH * DH, :]
                        .rearrange("(h p) e -> p h e", p=128), "gd")
            B.wait("vector", "gd", agl)
            for h in range(NH):
                B.op("vector", "vx", lambda e, h=h: e.scalar_tensor_tensor(
                    out=djc[:, 0:1], in0=bd4m1_bc[:, h:h + 1], scalar=G[:, j:j + 1],
                    in1=ones_c[:, :], op0=ALU.mult, op1=ALU.add))
                B.op("vector", "vx", lambda e, h=h: e.tensor_scalar(
                    tmph[:, :], agm[:, h * 128:(h + 1) * 128], G[:, j:j + 1], None,
                    op0=ALU.mult))
                vz = B.op("vector", "vx", lambda e, h=h: e.scalar_tensor_tensor(
                    out=kv_sb[:, h * 128:(h + 1) * 128],
                    in0=kv_sb[:, h * 128:(h + 1) * 128],
                    scalar=djc[:, 0:1], in1=tmph[:, :], op0=ALU.mult, op1=ALU.add))

        pm_done = 0
        o_done = 0
        kvup = vz
        wuse = {0: 0, 1: 0}
        for b in range(NBLK):
            c0 = b * BLK
            B.wait("gpsimd", "mm", B.cnt["mm"])
            B.wait("gpsimd", "vx", B.cnt["vx"])
            for s_ in ("lq", "lk", "lv", "lu", "lc"):
                B.wait("gpsimd", s_, B.cnt[s_])
            ld_q = B.dma("gpsimd", qb.ap().rearrange("p (c m) -> p c m", m=BLK),
                         d_q.rearrange("(c p) m -> p c m", p=128)[:, :, c0:c0 + BLK], "lq")
            ld_k = B.dma("gpsimd", kb.ap().rearrange("p (c m) -> p c m", m=BLK),
                         d_kf.rearrange("(c p) m -> p c m", p=128)[:, :, c0:c0 + BLK], "lk")
            ld_v = B.dma("gpsimd", vtb.ap().rearrange("p (s d) -> p s d", d=D),
                         d_vt.rearrange("(s p) d -> p s d", p=128)[:, 2 * b:2 * b + 2, :], "lv")
            ld_u = B.dma("gpsimd", ub.ap().rearrange("p (c m) -> p c m", m=BLK),
                         d_u.rearrange("(c p) m -> p c m", p=128)[:, :, c0:c0 + BLK], "lu")
            ld_c = B.dma("gpsimd", Cl.ap().rearrange("p (h e) -> p h e", e=128),
                         d_C[:, b].rearrange("h p e -> p h e"), "lc")

            v_qp = 0
            o_ev = {0: 0, 1: 0}
            for h in range(NH):
                B.wait("scalar", "vx", max(v_qp, pm_done))
                s_qd = B.op("scalar", "sc", lambda e, h=h: e.activation(
                    out=qd[:, :], in_=iotaq[:, :], func=AF.Exp,
                    scale=negs_bc[:, h:h + 1]))
                s_dc = B.op("scalar", "sc", lambda e, h=h: e.activation(
                    out=dct[:, :], in_=idxd[:, :], func=AF.Exp,
                    scale=negs_bc[:, h:h + 1]))
                B.wait("vector", "lq", ld_q)
                B.wait("vector", "sc", s_qd)
                B.wait("vector", "mm", o_done)
                v_qp = B.op("vector", "vx", lambda e, h=h: e.tensor_tensor(
                    qp[:, :], qb[:, h * BLK:(h + 1) * BLK], qd[:, :], ALU.mult))
                B.wait("tensor", "lq", ld_q); B.wait("tensor", "lk", ld_k)
                B.wait("tensor", "vx", pm_done)
                mmv = 0
                for sc_ in range(2):
                    mmv = B.op("tensor", "mm", lambda e, sc_=sc_, h=h: e.matmul(
                        pb[sc_][:, 0:BLK],
                        kb[:, h * BLK + sc_ * 128: h * BLK + (sc_ + 1) * 128],
                        qb[:, h * BLK:(h + 1) * BLK], start=True, stop=True))
                B.wait("vector", "mm", mmv)
                B.wait("vector", "sc", s_dc)
                for sc_ in range(2):
                    pm_done = B.op("vector", "vx", lambda e, sc_=sc_: e.tensor_tensor(
                        pbuf[:, sc_ * BLK:(sc_ + 1) * BLK], pb[sc_][:, 0:BLK],
                        dct[:, sc_ * BLK:(sc_ + 1) * BLK], ALU.mult))
                opb = pb[2 + h % 2]
                B.wait("tensor", "vx", max(pm_done, v_qp, kvup, o_ev[h % 2]))
                B.wait("tensor", "lv", ld_v)
                B.op("tensor", "mm", lambda e, h=h, opb=opb: e.matmul(
                    opb[:, 0:BLK], kv_sb[:, h * 128:(h + 1) * 128], qp[:, :],
                    start=True, stop=False), 0)
                for sc_ in range(2):
                    o_done = B.op("tensor", "mm",
                                  lambda e, sc_=sc_, h=h, opb=opb: e.matmul(
                                      opb[:, 0:BLK],
                                      vtb[:, sc_ * D + h * 128: sc_ * D + (h + 1) * 128],
                                      pbuf[:, sc_ * BLK:(sc_ + 1) * BLK],
                                      start=False, stop=(sc_ == 1)),
                                  inc=(1 if sc_ == 1 else 0))
                B.wait("vector", "mm", o_done)
                o_ev[h % 2] = B.op("vector", "vx", lambda e, h=h, opb=opb: e.tensor_copy(
                    ob[:, h * BLK:(h + 1) * BLK], opb[:, 0:BLK]))
                B.wait("vector", "lc", ld_c)
                kvup = B.op("vector", "vx", lambda e, h=h: e.scalar_tensor_tensor(
                    out=kv_sb[:, h * 128:(h + 1) * 128],
                    in0=kv_sb[:, h * 128:(h + 1) * 128],
                    scalar=bd_bc[:, h:h + 1], in1=Cl[:, h * 128:(h + 1) * 128],
                    op0=ALU.mult, op1=ALU.add))

            B.wait("scalar", "vx", kvup)
            prev_mm = 0
            for c in range(16):
                B.wait("scalar", "mm", prev_mm)
                sqd = B.op("scalar", "sc", lambda e, c=c: e.activation(
                    out=dump[:, 0:BLK], in_=ob[:, c * BLK:(c + 1) * BLK],
                    func=AF.Square))
                B.wait("tensor", "sc", sqd)
                prev_mm = B.op("tensor", "mm", lambda e, c=c: e.matmul(
                    pb[6][0:1, 0:BLK], ones_c[:, :], dump[:, 0:BLK],
                    start=(c == 0), stop=(c == 15)))
            B.wait("vector", "mm", prev_mm)
            B.op("vector", "vx", lambda e: e.tensor_single_scalar(
                r_a[0:1, :], pb[6][0:1, 0:BLK], 1.0 / D, ALU.mult))
            v1 = B.op("vector", "vx", lambda e: e.tensor_single_scalar(
                r_a[0:1, :], r_a[0:1, :], 1e-6, ALU.add))
            B.wait("scalar", "vx", v1)
            s1 = B.op("scalar", "sc", lambda e: e.activation(
                out=r_b[0:1, :], in_=r_a[0:1, :], func=AF.Sqrt))
            B.wait("vector", "sc", s1)
            B.op("vector", "vx", lambda e: e.reciprocal(r_b[0:1, :], r_b[0:1, :]))
            B.op("vector", "vx", lambda e: e.tensor_tensor(
                r_c[0:1, :], r_b[0:1, :], r_b[0:1, :], ALU.mult))
            B.op("vector", "vx", lambda e: e.tensor_tensor(
                r_c[0:1, :], r_c[0:1, :], r_a[0:1, :], ALU.mult))
            B.op("vector", "vx", lambda e: e.tensor_scalar(
                r_c[0:1, :], r_c[0:1, :], -0.5, 1.5, op0=ALU.mult, op1=ALU.add))
            v2 = B.op("vector", "vx", lambda e: e.tensor_tensor(
                r_b[0:1, :], r_b[0:1, :], r_c[0:1, :], ALU.mult))
            B.wait("gpsimd", "vx", v2)
            B.dma("gpsimd", d_scr[0:BLK][None, :], r_b[0:1, :], "lr")
            B.wait("gpsimd", "lr", B.cnt["lr"])
            for t_ in range(2):
                g1 = B.dma("gpsimd", rmsc[:, t_:t_ + 1],
                           d_scr[t_ * 128:(t_ + 1) * 128][:, None], "lr")

            B.wait("vector", "lu", ld_u)
            g_done = 0
            for c in range(16):
                g_done = B.op("vector", "vx", lambda e, c=c: e.tensor_tensor(
                    ob[:, c * BLK:(c + 1) * BLK], ob[:, c * BLK:(c + 1) * BLK],
                    ub[:, c * BLK:(c + 1) * BLK], ALU.mult))

            for n in range(4):
                evv = 0
                for c in range(16):
                    bi = c % 2
                    wsem = "w0" if bi == 0 else "w1"
                    B.wait("sync", "mm", wuse[bi])
                    mw = B.dma("sync", wv[bi].ap(),
                               woutT[c * 128:(c + 1) * 128, n * 512:(n + 1) * 512], wsem)
                    B.wait("tensor", wsem, mw)
                    if c == 0:
                        B.wait("tensor", "vx", g_done)
                    for s_ in range(2):
                        B.op("tensor", "mm", lambda e, c=c, s_=s_, bi=bi: e.matmul(
                            pb[4 + s_][:, :],
                            ob[:, c * BLK + s_ * 128: c * BLK + (s_ + 1) * 128],
                            wv[bi].ap(), start=(c == 0), stop=(c == 15)),
                            inc=(1 if s_ == 1 else 0))
                    wuse[bi] = B.cnt["mm"]
                B.wait("vector", "mm", B.cnt["mm"])
                B.wait("vector", "lr", g1)
                for s_ in range(2):
                    mt = 2 * b + s_
                    evv = B.op("vector", "vx",
                               lambda e, s_=s_, mt=mt, n=n: e.scalar_tensor_tensor(
                                   out=x_sb[:, mt * D + n * 512: mt * D + (n + 1) * 512],
                                   in0=pb[4 + s_][:, :], scalar=rmsc[:, s_:s_ + 1],
                                   in1=x_sb[:, mt * D + n * 512: mt * D + (n + 1) * 512],
                                   op0=ALU.mult, op1=ALU.add))
                B.wait("tensor", "vx", evv)

    def ph_wstats(bufs):
        wsb = bufs["wsb"]
        dmp = bufs["dmp"]
        acol = bufs["acol"]
        acol2 = bufs["acol2"]
        acc = bufs["acc"]
        s1t = bufs["s1t"]
        s2t = bufs["s2t"]
        srow = bufs["srow"]

        W = 1024
        vz = B.vop(lambda e: e.memset(acc[:, :], 0.0))
        suse = {0: ("sc", 0), 1: ("vx", 0)}

        def stream(dram, nr, ncol, per_tile):
            idx = 0
            for r in range(nr):
                for cb in range(ncol // W):
                    bi = idx % 2
                    wsem = "w0" if bi == 0 else "w1"
                    B.wait("sync", suse[bi][0], suse[bi][1])
                    mw = B.dma("sync", wsb[bi].ap(),
                               dram[r * 128:(r + 1) * 128, cb * W:(cb + 1) * W], wsem)
                    B.wait("scalar" if bi == 0 else "vector", wsem, mw)
                    per_tile(r, cb, bi)
                    suse[bi] = (("sc", B.cnt["sc"]) if bi == 0
                                else ("vx", B.cnt["vx"]))
                    idx += 1

        def glob_tile(col):
            def f(r, cb, bi):
                if bi == 0:
                    B.wait("scalar", "vx", B.cnt["vx"])
                    s = B.sop(lambda e: e.activation(
                        out=dmp[:, :], in_=wsb[bi][:, :], func=AF.Abs,
                        accum_out=acol[:, 0:1]))
                    B.wait("vector", "sc", s)
                    B.vop(lambda e: e.tensor_tensor(
                        acc[:, col:col + 1], acc[:, col:col + 1], acol[:, 0:1],
                        ALU.add))
                else:
                    B.vop(lambda e: e.reduce_sum(
                        acol2[:, 0:1], wsb[bi][:, :], axis=AX.X,
                        apply_absolute_value=True))
                    B.vop(lambda e: e.tensor_tensor(
                        acc[:, col:col + 1], acc[:, col:col + 1], acol2[:, 0:1],
                        ALU.add))
            return f
        stream(wfcT, 16, H0, glob_tile(0))
        stream(wprojT, 40, D, glob_tile(1))

        def row_tile(tgt, nchunks, inv):
            def f(r, cb, bi):
                if bi == 0:
                    B.wait("scalar", "vx", B.cnt["vx"])
                    s = B.sop(lambda e: e.activation(
                        out=dmp[:, :], in_=wsb[bi][:, :], func=AF.Abs,
                        accum_out=acol[:, 0:1]))
                    B.wait("vector", "sc", s)
                else:
                    s = None
                    B.vop(lambda e: e.reduce_sum(
                        acol[:, 0:1], wsb[bi][:, :], axis=AX.X,
                        apply_absolute_value=True))
                if cb == 0:
                    B.vop(lambda e, r=r: e.tensor_copy(
                        tgt[:, r:r + 1], acol[:, 0:1]))
                else:
                    B.vop(lambda e, r=r: e.tensor_tensor(
                        tgt[:, r:r + 1], tgt[:, r:r + 1], acol[:, 0:1], ALU.add))
                if cb == nchunks - 1:
                    B.vop(lambda e, r=r: e.tensor_scalar(
                        tgt[:, r:r + 1], tgt[:, r:r + 1], inv, 1e-5,
                        op0=ALU.mult, op1=ALU.max))
            return f
        stream(w1n, 64, D, row_tile(s1t, D // W, 1.0 / D))
        stream(w2n, 16, H1, row_tile(s2t, H1 // W, 1.0 / H1))

        vlast = B.cnt["vx"]
        B.wait("tensor", "vx", vlast)
        B.op("tensor", "mm", lambda e: e.matmul(
            pb[0][0:1, 0:1], acc[:, 0:1], ones_c[:, :], start=True, stop=True))
        m1 = B.op("tensor", "mm", lambda e: e.matmul(
            pb[0][0:1, 1:2], acc[:, 1:2], ones_c[:, :], start=True, stop=True))
        B.wait("vector", "mm", m1)
        B.vop(lambda e: e.tensor_scalar(
            srow[0:1, 0:1], pb[0][0:1, 0:1], 1.0 / (H0 * D), 1e-5,
            op0=ALU.mult, op1=ALU.max))
        B.vop(lambda e: e.tensor_scalar(
            srow[0:1, 1:2], pb[0][0:1, 1:2], 1.0 / (H0 * D), 1e-5,
            op0=ALU.mult, op1=ALU.max))
        B.vop(lambda e: e.tensor_single_scalar(
            srow[0:1, 2:3], srow[0:1, 0:1], 0.7, ALU.mult))
        B.vop(lambda e: e.tensor_single_scalar(
            srow[0:1, 3:4], srow[0:1, 1:2], 0.7, ALU.mult))
        B.vop(lambda e: e.tensor_single_scalar(
            srow[0:1, 4:5], srow[0:1, 2:3], -1.0, ALU.mult))
        v2 = B.vop(lambda e: e.tensor_single_scalar(
            srow[0:1, 5:6], srow[0:1, 3:4], -1.0, ALU.mult))
        B.wait("tensor", "vx", v2)
        m2 = B.op("tensor", "mm", lambda e: e.matmul(
            pb[1][:, 0:6], ones_r[:, :], srow[0:1, 0:6], start=True, stop=True))
        B.wait("vector", "mm", m2)
        B.vop(lambda e: e.tensor_copy(am_fc[:, 0:1], pb[1][:, 0:1]))
        B.vop(lambda e: e.tensor_copy(am_pr[:, 0:1], pb[1][:, 1:2]))
        B.vop(lambda e: e.tensor_copy(thr_fc[:, 0:1], pb[1][:, 2:3]))
        B.vop(lambda e: e.tensor_copy(thr_pr[:, 0:1], pb[1][:, 3:4]))
        B.vop(lambda e: e.tensor_copy(nthr_fc[:, 0:1], pb[1][:, 4:5]))
        v3 = B.vop(lambda e: e.tensor_copy(nthr_pr[:, 0:1], pb[1][:, 5:6]))

        B.wait("tensor", "vx", v3)
        m3 = B.op("tensor", "mm", lambda e: e.transpose(
            pb[2][0:64, 0:128], s1t[:, 0:64], ident[:, :]))
        m4 = B.op("tensor", "mm", lambda e: e.transpose(
            pb[3][0:16, 0:128], s2t[:, 0:16], ident[:, :]))
        B.wait("vector", "mm", m4)
        B.vop(lambda e: e.tensor_copy(dmp[0:64, 0:128], pb[2][0:64, 0:128]))
        v4 = B.vop(lambda e: e.tensor_copy(dmp[0:16, 128:256], pb[3][0:16, 0:128]))
        B.wait("gpsimd", "vx", v4)
        B.dma("gpsimd", d_s1.rearrange("(r c) -> r c", c=128), dmp[0:64, 0:128], "gd")
        B.dma("gpsimd", d_s2.rearrange("(r c) -> r c", c=128), dmp[0:16, 128:256], "gd")
        B.wait("gpsimd", "gd", B.cnt["gd"])

    def ph_tern(bufs):
        wsb = bufs["wsb"]; tn = bufs["tn"]; qo = bufs["qo"]; qo2 = bufs["qo2"]
        thp = bufs["thp"]; thn = bufs["thn"]; s1r = bufs["s1r"]; tn2 = bufs["tn2"]
        W = 1024
        suse = {0: ("vx", 0), 1: ("gx", 0)}
        qfree = {0: 0, 1: 0}
        sgn = [0]

        def tern_global(dramT, dst, nr, ncol, thr, nthr):
            idx = 0
            for r in range(nr):
                for cb in range(ncol // W):
                    bi = idx % 2
                    wsem = "w0" if bi == 0 else "w1"
                    B.wait("sync", suse[bi][0], suse[bi][1])
                    mw = B.dma("sync", wsb[bi].ap(),
                               dramT[r * 128:(r + 1) * 128, cb * W:(cb + 1) * W], wsem)
                    qi = idx % 2
                    qsem = "ev" if qi == 0 else "vt"
                    if qi == 0:
                        B.wait("vector", qsem, qfree[qi])
                        B.wait("vector", wsem, mw)
                        B.vop(lambda e, bi=bi: e.tensor_scalar(
                            tn[:, :], wsb[bi][:, :], nthr[:, 0:1], None,
                            op0=ALU.is_lt))
                        vq = B.vop(lambda e, bi=bi, qi=qi:
                                   e.scalar_tensor_tensor(
                                       out=qo[qi][:, :], in0=wsb[bi][:, :],
                                       scalar=thr[:, 0:1], in1=tn[:, :],
                                       op0=ALU.is_gt, op1=ALU.subtract))
                        esem = "vx"
                    else:
                        B.wait("scalar", wsem, mw)
                        B.wait("scalar", "vx", sgn[0])
                        B.sop(lambda e, bi=bi: e.activation(
                            out=tn2[:, :], in_=wsb[bi][:, :], func=AF.Sign,
                            bias=nthr[:, 0:1]))
                        s2 = B.sop(lambda e, bi=bi: e.activation(
                            out=qo2[:, :], in_=wsb[bi][:, :], func=AF.Sign,
                            bias=thr[:, 0:1]))
                        B.wait("vector", "sc", s2)
                        B.wait("vector", qsem, qfree[qi])
                        vq = B.vop(lambda e, qi=qi: e.scalar_tensor_tensor(
                            out=qo[qi][:, :], in0=tn2[:, :], scalar=1.0,
                            in1=qo2[:, :], op0=ALU.mult, op1=ALU.add))
                        vq2 = B.vop(lambda e, qi=qi: e.tensor_single_scalar(
                            qo[qi][:, :], qo[qi][:, :], 0.5, ALU.mult))
                        vq = vq2
                        sgn[0] = vq
                        esem = "vx"
                    suse[bi] = (esem if qi == 0 else "sc",
                                vq if qi == 0 else B.cnt["sc"])
                    B.wait("gpsimd", esem, vq)
                    qfree[qi] = B.dma(
                        "gpsimd", dst[r * 128:(r + 1) * 128, cb * W:(cb + 1) * W],
                        qo[qi].ap(), qsem)
                    idx += 1
        tern_global(wfcT, d_tq1l0, 16, H0, thr_fc, nthr_fc)
        tern_global(wprojT, d_tq2l0, 40, D, thr_pr, nthr_pr)

        def tern_rows(dramT, dst, scl_dram, nr, ncol):
            idx = 0
            for cb in range(ncol // W):
                ml = B.dma("gpsimd", s1r[0:1, 0:W],
                           scl_dram[cb * W:(cb + 1) * W][None, :], "lr")
                B.wait("tensor", "lr", ml)
                B.wait("tensor", suse[0][0], suse[0][1])
                B.wait("tensor", suse[1][0], suse[1][1])
                mm1 = B.op("tensor", "mm", lambda e: e.matmul(
                    pb[4][:, 0:512], ones_r[:, :], s1r[0:1, 0:512],
                    start=True, stop=True))
                mm2 = B.op("tensor", "mm", lambda e: e.matmul(
                    pb[5][:, 0:512], ones_r[:, :], s1r[0:1, 512:1024],
                    start=True, stop=True))
                B.wait("vector", "mm", mm2)
                B.vop(lambda e: e.tensor_single_scalar(
                    thp[:, 0:512], pb[4][:, 0:512], 0.5, ALU.mult))
                B.vop(lambda e: e.tensor_single_scalar(
                    thp[:, 512:1024], pb[5][:, 0:512], 0.5, ALU.mult))
                B.vop(lambda e: e.tensor_single_scalar(
                    thn[:, 0:512], pb[4][:, 0:512], -0.5, ALU.mult))
                vb = B.vop(lambda e: e.tensor_single_scalar(
                    thn[:, 512:1024], pb[5][:, 0:512], -0.5, ALU.mult))
                for r in range(nr):
                    bi = idx % 2
                    wsem = "w0" if bi == 0 else "w1"
                    B.wait("sync", suse[bi][0], suse[bi][1])
                    mw = B.dma("sync", wsb[bi].ap(),
                               dramT[r * 128:(r + 1) * 128, cb * W:(cb + 1) * W], wsem)
                    qi = idx % 2
                    qsem = "ev" if qi == 0 else "vt"
                    B.wait("vector", wsem, mw)
                    B.wait("vector", qsem, qfree[qi])
                    B.vop(lambda e, bi=bi: e.tensor_tensor(
                        tn[:, :], wsb[bi][:, :], thn[:, :], ALU.is_lt))
                    B.vop(lambda e, bi=bi: e.tensor_tensor(
                        qo2[:, :], wsb[bi][:, :], thp[:, :], ALU.is_gt))
                    vq = B.vop(lambda e, qi=qi: e.tensor_tensor(
                        qo[qi][:, :], qo2[:, :], tn[:, :], ALU.subtract))
                    suse[bi] = ("vx", vq)
                    B.wait("gpsimd", "vx", vq)
                    qfree[qi] = B.dma(
                        "gpsimd", dst[r * 128:(r + 1) * 128, cb * W:(cb + 1) * W],
                        qo[qi].ap(), qsem)
                    idx += 1
        tern_rows(w1T, d_tq1l1, d_s1, 16, H1)
        tern_rows(w2T, d_tq2l1, d_s2, 64, D)

    def ph_l0_ffn(bufs):
        ybuf = bufs["ybuf"]; st = bufs["st"]
        y2h = bufs["y2h"]; y2l = bufs["y2l"]
        wsb = bufs["wsb"]
        rb = bufs["rb"]; sqf = bufs["sqf"]
        sqh = bufs["sqh"]; sql = bufs["sql"]
        dump = ybuf

        wuse = {0: 0, 1: 0, 2: 0, 3: 0}
        prev_ln = 0
        for mc in range(4):
            bank_evac = {0: B.cnt["vx"], 1: B.cnt["vx"]}
            for mt_rel in range(2):
                mt = 2 * mc + mt_rel
                B.wait("scalar", "mm", B.cnt["mm"])
                B.wait("scalar", "vx", prev_ln)
                s3 = emit_ln(x_sb[:, mt * D:(mt + 1) * D], ybuf, st, dump)
                prev_ln = s3
                B.wait("tensor", "vx", s3)
                for k in range(16):
                    bi = k % 2
                    B.wait("tensor", "vx", bank_evac[bi])
                    mmv = B.op("tensor", "mm", lambda e, k=k, bi=bi: e.transpose(
                        pb[bi][:, 0:128], ybuf[:, k * 128:(k + 1) * 128], ident[:, :]))
                    B.wait("vector", "mm", mmv)
                    col = k * BLK + mt_rel * 128
                    B.vop(lambda e, bi=bi, col=col: e.tensor_copy(
                        y2h[:, col:col + 128], pb[bi][:, 0:128]))
                    bank_evac[bi] = B.vop(lambda e, bi=bi, col=col: e.tensor_tensor(
                        y2l[:, col:col + 128], pb[bi][:, 0:128], y2h[:, col:col + 128],
                        ALU.subtract))
            yT_done = B.cnt["vx"]
            B.wait("tensor", "vx", yT_done)

            set_done = {0: B.cnt["vx"], 1: B.cnt["vx"]}
            prev_sq = 0
            for cb in range(5):
                bset = [pb[4 * (cb % 2) + i] for i in range(4)]
                B.wait("tensor", "vx", set_done[cb % 2])
                for kr in range(16):
                    bi = (cb * 16 + kr) % 4
                    wsem = B.WSEMS[bi]
                    B.wait("sync", "mm", wuse[bi])
                    mw = B.dma("sync", wsb[bi][:, 0:1024],
                               d_tq1l0[kr * 128:(kr + 1) * 128,
                                       cb * 1024:(cb + 1) * 1024], wsem)
                    B.wait("tensor", wsem, mw)
                    for j in range(8):
                        bank = bset[j // 2]
                        half = (j % 2) * BLK
                        for hl, ysrc in ((0, y2h), (1, y2l)):
                            B.op("tensor", "mm",
                                 lambda e, kr=kr, j=j, hl=hl, ysrc=ysrc,
                                 bank=bank, half=half, bi=bi: e.matmul(
                                     bank[:, half:half + BLK],
                                     wsb[bi][:, j * 128:(j + 1) * 128],
                                     ysrc[:, kr * BLK:(kr + 1) * BLK],
                                     start=(kr == 0 and j % 2 == 0 and hl == 0),
                                     stop=(kr == 15 and j % 2 == 1 and hl == 1)),
                                 inc=(1 if (j == 7 and hl == 1) else 0))
                    wuse[bi] = B.cnt["mm"]
                gmm = B.cnt["mm"]
                for j in range(8):
                    bank = bset[j // 2]
                    half = (j % 2) * BLK
                    hc = cb * 8 + j
                    B.wait("vector", "mm", gmm)
                    B.wait("vector", "sc", prev_sq)
                    B.vop(lambda e, bank=bank, half=half: e.tensor_relu(
                        rb[:, 0:BLK], bank[:, half:half + BLK]))
                    vr = B.cnt["vx"]
                    B.wait("scalar", "vx", vr)
                    prev_sq = B.sop(lambda e: e.activation(
                        out=sqf[:, 0:BLK], in_=rb[:, 0:BLK], func=AF.Square,
                        scale=am_fc[:, 0:1]))
                    B.wait("vector", "sc", prev_sq)
                    col = hc * BLK
                    B.vop(lambda e, col=col: e.tensor_copy(
                        sqh[:, col:col + BLK], sqf[:, 0:BLK]))
                    B.vop(lambda e, col=col: e.tensor_tensor(
                        sql[:, col:col + BLK], sqf[:, 0:BLK], sqh[:, col:col + BLK],
                        ALU.subtract))
                set_done[cb % 2] = B.cnt["vx"]
            sq_done = B.cnt["vx"]

            for hc in range(40):
                bi = hc % 4
                wsem = B.WSEMS[bi]
                B.wait("sync", "mm", wuse[bi])
                mw = B.dma("sync", wsb[bi].ap(),
                           d_tq2l0[hc * 128:(hc + 1) * 128, :], wsem)
                B.wait("tensor", wsem, mw)
                if hc == 0:
                    B.wait("tensor", "vx", sq_done)
                for s_ in range(2):
                    for n in range(4):
                        for hl, sqsrc in ((0, sqh), (1, sql)):
                            B.op("tensor", "mm",
                                 lambda e, hc=hc, s_=s_, n=n, sqsrc=sqsrc, bi=bi:
                                 e.matmul(
                                     pb[4 * s_ + n][:, :],
                                     sqsrc[:, hc * BLK + s_ * 128: hc * BLK + (s_ + 1) * 128],
                                     wsb[bi][:, n * 512:(n + 1) * 512],
                                     start=(hc == 0 and hl == 0),
                                     stop=(hc == 39 and hl == 1)),
                                 inc=(1 if (s_ == 1 and n == 3 and hl == 1) else 0))
                wuse[bi] = B.cnt["mm"]
            B.wait("vector", "mm", B.cnt["mm"])
            evv = 0
            for s_ in range(2):
                for n in range(4):
                    mt = 2 * mc + s_
                    evv = B.vop(lambda e, s_=s_, mt=mt, n=n: e.scalar_tensor_tensor(
                        out=x_sb[:, mt * D + n * 512: mt * D + (n + 1) * 512],
                        in0=pb[4 * s_ + n][:, :], scalar=am_pr[:, 0:1],
                        in1=x_sb[:, mt * D + n * 512: mt * D + (n + 1) * 512],
                        op0=ALU.mult, op1=ALU.add))
            B.wait("tensor", "vx", evv)

    def ph_l1_attn_a(bufs):
        y3T = bufs["y3T"]; wb = bufs["wb"]; ev = bufs["ev"]; vtok = bufs["vtok"]
        ybuf = bufs["ybuf"]; st = bufs["st"]
        dump = ybuf

        bank_evac = {0: B.cnt["vx"], 1: B.cnt["vx"]}
        prev_ln = 0
        for mt in range(NMT):
            B.wait("scalar", "mm", B.cnt["mm"])
            B.wait("scalar", "vx", prev_ln)
            s3 = emit_ln(x_sb[:, mt * D:(mt + 1) * D], ybuf, st, dump)
            prev_ln = s3
            B.wait("tensor", "vx", s3)
            for k in range(16):
                bi = k % 2
                B.wait("tensor", "vx", bank_evac[bi])
                mmv = B.op("tensor", "mm", lambda e, k=k, bi=bi: e.transpose(
                    pb[bi][:, 0:128], ybuf[:, k * 128:(k + 1) * 128], ident[:, :]))
                B.wait("vector", "mm", mmv)
                bank_evac[bi] = B.vop(lambda e, k=k, mt=mt, bi=bi: e.tensor_copy(
                    y3T[:, k * T + mt * 128: k * T + (mt + 1) * 128], pb[bi][:, 0:128]))
        yT_done = B.cnt["vx"]
        B.wait("tensor", "vx", yT_done)

        wlast = {0: 0, 1: 0}
        prev_evac = ("vx", 0)
        ev_spill = {"q": 0, "k": 0}
        vt_spill = 0
        tr_bank_evac = {0: B.cnt["vx"], 1: B.cnt["vx"]}
        regs = [(w3q, "q"), (w3k, "k"), (w3v, "v")]
        for h in range(NH):
            for ri, (wT, r) in enumerate(regs):
                bi = (h * 3 + ri) % 2
                wsem = "w0" if bi == 0 else "w1"
                B.wait("sync", "mm", wlast[bi])
                mw = B.dma("sync", wb[bi].ap(), wT[h], wsem)
                B.wait("tensor", wsem, mw)
                B.wait("tensor", prev_evac[0], prev_evac[1])
                for half in range(2):
                    for kc in range(16):
                        B.op("tensor", "mm",
                             lambda e, kc=kc, half=half, bi=bi:
                             e.matmul(pb[2 + half][:, :],
                                      wb[bi][:, kc * 128:(kc + 1) * 128],
                                      y3T[:, kc * T + half * 512: kc * T + (half + 1) * 512],
                                      start=(kc == 0), stop=(kc == 15)),
                             inc=(1 if kc == 15 else 0))
                wlast[bi] = B.cnt["mm"]
                g1 = B.cnt["mm"]
                B.wait("vector", "mm", g1)
                B.wait("vector", "lq", ev_spill["q"])
                B.wait("vector", "ev", ev_spill["k"])
                if r != "v":
                    B.vop(lambda e: e.tensor_copy(ev[:, 0:512], pb[2][:, :]))
                    e2 = B.vop(lambda e: e.tensor_copy(ev[:, 512:1024], pb[3][:, :]))
                    prev_evac = ("vx", e2)
                    B.wait("gpsimd", "vx", e2)
                    if r == "q":
                        ev_spill["q"] = B.dma("gpsimd", d_q[h * 128:(h + 1) * 128, :],
                                              ev[:, 0:1024], "lq")
                    else:
                        ev_spill["k"] = B.dma("gpsimd",
                                              ag_in_k.ap()[h * 128:(h + 1) * 128, :],
                                              ev[:, 0:1024], "ev")
                else:
                    B.vop(lambda e: e.tensor_copy(ev[:, 0:512], pb[2][:, :]))
                    e2 = B.vop(lambda e: e.tensor_copy(ev[:, 512:1024], pb[3][:, :]))
                    prev_evac = ("vx", e2)
                    B.wait("tensor", "vx", e2)
                    for t in range(8):
                        tb_ = t % 2
                        B.wait("tensor", "vx", tr_bank_evac[tb_])
                        mmv = B.op("tensor", "mm", lambda e, t=t, tb_=tb_: e.transpose(
                            pb[4 + tb_][:, 0:128], ev[:, t * 128:(t + 1) * 128],
                            ident[:, :]))
                        B.wait("vector", "mm", mmv)
                        B.wait("vector", "vt", vt_spill)
                        tr_bank_evac[tb_] = B.vop(
                            lambda e, t=t, tb_=tb_: e.tensor_copy(
                                vtok[:, t * 128:(t + 1) * 128], pb[4 + tb_][:, 0:128]))
                    B.wait("gpsimd", "vx", max(tr_bank_evac.values()))
                    vt_spill = B.dma(
                        "gpsimd",
                        ag_in_v.ap().rearrange("(t p) d -> p t d", p=128)[:, :, h * 128:(h + 1) * 128],
                        vtok.ap().rearrange("p (t d) -> p t d", d=128), "vt")

        for s_ in ("lq", "ev", "vt"):
            B.wait("gpsimd", s_, B.cnt[s_])
        B.op("gpsimd", "cc", lambda e: e.collective_compute(
            "AllGather", ALU.bypass, replica_groups=[list(range(world))],
            ins=[ag_in_k.ap().opt()], outs=[ag_out_k.ap().opt()]))
        B.op("gpsimd", "cc", lambda e: e.collective_compute(
            "AllGather", ALU.bypass, replica_groups=[list(range(world))],
            ins=[ag_in_v.ap().opt()], outs=[ag_out_v.ap().opt()]))
        B.wait("gpsimd", "cc", B.cnt["cc"])

        kh = ag_out_k.ap().tensor
        vh = ag_out_v.ap().tensor

        def mk(off_fn, dims):
            return lambda: bass.AP(off_fn[0], nc.snap(off_fn[1]()), dims)

        def prev_off():
            pid = B.eng.partition_id()
            return (pid + world - 1) % world

        def self_off():
            return B.eng.partition_id()

        B.dma("gpsimd", kwin[:, 0:T],
              mk((kh, lambda: prev_off() * (D * T)), [[T, D], [1, T]]), "gd")
        B.dma("gpsimd", kwin[:, T:2 * T],
              mk((kh, lambda: self_off() * (D * T)), [[T, D], [1, T]]), "gd")
        B.dma("gpsimd", vwin[0:T, :],
              mk((vh, lambda: prev_off() * (T * D)), [[D, T], [1, D]]), "gd")
        B.dma("gpsimd", vwin[T:2 * T, :],
              mk((vh, lambda: self_off() * (T * D)), [[D, T], [1, D]]), "gd")
        B.wait("gpsimd", "gd", B.cnt["gd"])

    def ph_l1_attn_b(bufs):
        qmb = bufs["qmb"]; osb = bufs["osb"]; masks = bufs["masks"]
        ktile = bufs["ktile"]; vtile = bufs["vtile"]
        tbv = bufs["tbv"]; pbm = bufs["pbm"]; lsb = bufs["lsb"]; lrow = bufs["lrow"]
        lsum = bufs["lsum"]; lrows = bufs["lrows"]
        wvb = bufs["wvb"]
        SC = 1.0 / float(np.sqrt(128.0))

        wuse = {0: 0, 1: 0}
        for mb in range(2):
            B.wait("gpsimd", "vx", B.cnt["vx"])
            B.wait("gpsimd", "mm", B.cnt["mm"])
            for s_ in ("lr", "lq", "lk", "lv", "lu", "lc"):
                B.wait("gpsimd", s_, B.cnt[s_])
            ml = B.dma("gpsimd", masks.ap().rearrange("p (k m) -> p k m", m=512),
                       lmask[mb].rearrange("k p m -> p k m"), "lr")
            lq = B.dma("gpsimd", qmb.ap().rearrange("p (c m) -> p c m", m=512),
                       d_q.rearrange("(c p) m -> p c m", p=128)[:, :, mb * 512:(mb + 1) * 512],
                       "lq")
            vz = B.vop(lambda e: e.memset(osb[:, :], 0.0))

            tanh_done = {0: 0, 1: 0, 2: 0}
            av_evac = {0: ("vx", vz), 1: ("vx", vz)}
            lrow_acc = {0: 0, 1: 0}
            pbm_freed = {0: 0, 1: 0, 2: 0}
            vmask = {0: 0, 1: 0, 2: 0}
            kv_ld = {}
            st_mm_done = {}

            def issue_kv(st_):
                par = st_ % 2
                ks, vs = ("lk", "lv") if par == 0 else ("lu", "lc")
                B.wait("gpsimd", "mm", st_mm_done.get(st_ - 2, 0))
                lk_ = B.dma("gpsimd",
                            ktile[par].ap().rearrange("p (c s) -> p c s", s=128),
                            kwin.rearrange("(c p) t -> p c t", p=128)
                            [:, :, mb * 512 + st_ * 128: mb * 512 + (st_ + 1) * 128], ks)
                lv_ = B.dma("gpsimd", vtile[par].ap(),
                            vwin[mb * 512 + st_ * 128: mb * 512 + (st_ + 1) * 128, :], vs)
                kv_ld[st_] = (ks, lk_, vs, lv_)

            issue_kv(0)
            for st_ in range(12):
                if st_ + 1 < 12:
                    issue_kv(st_ + 1)
                par = st_ % 2
                ks, lk, vs, lv = kv_ld[st_]
                ktl = ktile[par]
                vtl = vtile[par]
                for h in range(NH):
                    sb_ = h % 3
                    B.wait("tensor", ks, lk)
                    B.wait("tensor", "lq", lq)
                    B.wait("tensor", "sc", tanh_done[sb_])
                    mmv = B.op("tensor", "mm", lambda e, h=h, sb_=sb_, ktl=ktl: e.matmul(
                        pb[sb_][:, :], ktl[:, h * 128:(h + 1) * 128],
                        qmb[:, h * 512:(h + 1) * 512], start=True, stop=True))
                    B.wait("scalar", "mm", max(mmv, pbm_freed[sb_]))
                    B.wait("scalar", "vx", vmask[sb_])
                    B.sop(lambda e, sb_=sb_: e.activation(
                        out=tbv[sb_][:, :], in_=pb[sb_][:, :], func=AF.Tanh,
                        scale=SC / 30.0))
                    tanh_done[sb_] = B.cnt["sc"]
                    s_e = B.sop(lambda e, sb_=sb_: e.activation(
                        out=pbm[sb_][:, :], in_=tbv[sb_][:, :], func=AF.Exp,
                        scale=30.0, bias=neg30[:, 0:1]))
                    B.wait("vector", "sc", s_e)
                    B.wait("vector", "lr", ml)
                    v_m = B.vop(lambda e, st_=st_, sb_=sb_: e.tensor_tensor(
                        pbm[sb_][:, :], pbm[sb_][:, :],
                        masks[:, st_ * 512:(st_ + 1) * 512], ALU.mult))
                    vmask[sb_] = v_m
                    ab = 3 + (h % 2)
                    B.wait("tensor", "vx", v_m)
                    B.wait("tensor", av_evac[h % 2][0], av_evac[h % 2][1])
                    B.wait("tensor", vs, lv)
                    av = B.op("tensor", "mm",
                              lambda e, h=h, ab=ab, sb_=sb_, vtl=vtl: e.matmul(
                                  pb[ab][:, :], vtl[:, h * 128:(h + 1) * 128],
                                  pbm[sb_][:, :], start=True, stop=True))
                    lb_ = 5 + (h % 2)
                    B.wait("tensor", "vx", lrow_acc[h % 2])
                    B.op("tensor", "mm", lambda e, sb_=sb_, lb_=lb_: e.matmul(
                        pb[lb_][0:1, 0:512], ones_c[:, :], pbm[sb_][:, :],
                        start=True, stop=True))
                    pbm_freed[sb_] = B.cnt["mm"]
                    B.wait("vector", "mm", pbm_freed[sb_])
                    lpr = 32 * (h % 4)
                    lcc = (h // 4) * 512
                    if st_ == 0:
                        lrow_acc[h % 2] = B.vop(lambda e, lpr=lpr, lcc=lcc, lb_=lb_:
                                                e.tensor_copy(
                                                    lsum[lpr:lpr + 1, lcc:lcc + 512],
                                                    pb[lb_][0:1, 0:512]))
                    else:
                        lrow_acc[h % 2] = B.vop(lambda e, lpr=lpr, lcc=lcc, lb_=lb_:
                                                e.tensor_tensor(
                                                    lsum[lpr:lpr + 1, lcc:lcc + 512],
                                                    lsum[lpr:lpr + 1, lcc:lcc + 512],
                                                    pb[lb_][0:1, 0:512], ALU.add))
                    B.wait("vector", "mm", av)
                    av_evac[h % 2] = ("vx", B.vop(
                        lambda e, h=h, ab=ab: e.tensor_tensor(
                            osb[:, h * 512:(h + 1) * 512],
                            osb[:, h * 512:(h + 1) * 512], pb[ab][:, :], ALU.add)))
                st_mm_done[st_] = B.cnt["mm"]

            for k_ in range(4):
                B.vop(lambda e, k_=k_: e.reciprocal(
                    lsum[32 * k_:32 * k_ + 1, :], lsum[32 * k_:32 * k_ + 1, :]))
            vrec0 = B.cnt["vx"]
            B.wait("gpsimd", "vx", vrec0)
            for k_ in range(4):
                B.dma("gpsimd", d_scr[k_ * 2048:(k_ + 1) * 2048][None, :],
                      lsum[32 * k_:32 * k_ + 1, :], "lr")
            B.wait("gpsimd", "lr", B.cnt["lr"])
            g2 = B.dma("gpsimd", lrows[0:64, 0:128],
                       d_scr[0:NH * 512].rearrange("(r p) -> r p", p=128), "lr")
            B.wait("tensor", "lr", g2)
            m1 = B.op("tensor", "mm", lambda e: e.transpose(
                pb[6][:, 0:64], lrows[0:64, 0:128], ident[0:64, 0:64]))
            B.wait("vector", "mm", m1)
            vrec = B.vop(lambda e: e.tensor_copy(lsb[:, 0:64], pb[6][:, 0:64]))
            B.wait("tensor", "vx", vrec)

            for h in range(NH):
                for sub in range(4):
                    col = (h % 4) * 16 + (h // 4) * 4 + sub
                    B.wait("tensor", "vx", B.cnt["vx"])
                    m1 = B.op("tensor", "mm", lambda e, col=col: e.matmul(
                        pb[6][0:1, 0:128], lsb[:, col:col + 1], ident[:, :],
                        start=True, stop=True))
                    B.wait("vector", "mm", m1)
                    vv = B.vop(lambda e: e.tensor_copy(lrow[0:1, :], pb[6][0:1, 0:128]))
                    B.wait("tensor", "vx", vv)
                    m2 = B.op("tensor", "mm", lambda e: e.matmul(
                        pb[7][:, 0:128], ones_r[:, :], lrow[0:1, :],
                        start=True, stop=True))
                    B.wait("vector", "mm", m2)
                    B.vop(lambda e, h=h, sub=sub: e.tensor_tensor(
                        osb[:, h * 512 + sub * 128: h * 512 + (sub + 1) * 128],
                        osb[:, h * 512 + sub * 128: h * 512 + (sub + 1) * 128],
                        pb[7][:, 0:128], ALU.mult))
            on_done = B.cnt["vx"]

            for n in range(4):
                for h in range(NH):
                    bi = h % 2
                    wsem = "w0" if bi == 0 else "w1"
                    B.wait("sync", "mm", wuse[bi])
                    mw = B.dma("sync", wvb[bi].ap(),
                               woT[h * 128:(h + 1) * 128, n * 512:(n + 1) * 512], wsem)
                    B.wait("tensor", wsem, mw)
                    if h == 0:
                        B.wait("tensor", "vx", on_done)
                    for ms in range(4):
                        B.op("tensor", "mm", lambda e, h=h, ms=ms, bi=bi: e.matmul(
                            pb[ms][:, :],
                            osb[:, h * 512 + ms * 128: h * 512 + (ms + 1) * 128],
                            wvb[bi].ap(), start=(h == 0), stop=(h == NH - 1)),
                            inc=(1 if ms == 3 else 0))
                    wuse[bi] = B.cnt["mm"]
                B.wait("vector", "mm", B.cnt["mm"])
                evv = 0
                for ms in range(4):
                    mt = mb * 4 + ms
                    evv = B.vop(lambda e, ms=ms, mt=mt, n=n: e.tensor_tensor(
                        x_sb[:, mt * D + n * 512: mt * D + (n + 1) * 512],
                        x_sb[:, mt * D + n * 512: mt * D + (n + 1) * 512],
                        pb[ms][:, :], ALU.add))
                B.wait("tensor", "vx", evv)

    def ph_l1_ffn(bufs):
        ybuf = bufs["gbuf"][:, D:2 * D]
        st = bufs["st"]
        sc1 = bufs["sc1"]; sc2 = bufs["sc2"]
        s1r = bufs["s1r"]
        zb = bufs["zb"]; fb = bufs["fb"]; db = bufs["db"]
        ib = bufs["ib"]
        xif = bufs["xif"]
        xif2 = bufs["gbuf"]
        RW = 512
        xiT = bufs["xiT"]
        gbuf = bufs["gbuf"]
        k2T = bufs["k2T"]
        k2Tb = bufs["k2Tb"]
        wsb = bufs["wsb"]
        mxc = bufs["mxc"]
        W = 1024

        def emit_round(src_ap, n, s_col, out_ap, out_cols):
            for cc in range(n // RW):
                sl = slice(cc * RW, (cc + 1) * RW)
                B.vop(lambda e, sl=sl: e.tensor_scalar(
                    zb[:, :], src_ap[:, sl], mxc[:, s_col:s_col + 1], None,
                    op0=ALU.mult))
                B.vop(lambda e: e.tensor_copy(ib[:, :], zb[:, :]))
                B.vop(lambda e: e.tensor_copy(fb[:, :], ib[:, :]))
                B.vop(lambda e: e.tensor_sub(db[:, :], zb[:, :], fb[:, :]))
                B.vop(lambda e: e.scalar_tensor_tensor(
                    out=zb[:, :], in0=db[:, :], scalar=0.5, in1=fb[:, :],
                    op0=ALU.is_gt, op1=ALU.add))
                B.vop(lambda e: e.tensor_single_scalar(
                    db[:, :], db[:, :], -0.5, ALU.is_lt))
                B.vop(lambda e, sl=sl, cc=cc: e.tensor_tensor(
                    out_ap[:, out_cols.start + cc * RW: out_cols.start + (cc + 1) * RW],
                    zb[:, :], db[:, :], ALU.subtract))

        for tgt, dsrc, ncol in ((sc1, d_s1, H1), (sc2, d_s2, D)):
            for cb in range(ncol // W):
                ml = B.dma("gpsimd", s1r[0:1, 0:W], dsrc[cb * W:(cb + 1) * W][None, :], "lr")
                B.wait("tensor", "lr", ml)
                B.wait("tensor", "vx", B.cnt["vx"])
                B.op("tensor", "mm", lambda e: e.matmul(
                    pb[4][:, 0:512], ones_r[:, :], s1r[0:1, 0:512],
                    start=True, stop=True))
                m2 = B.op("tensor", "mm", lambda e: e.matmul(
                    pb[5][:, 0:512], ones_r[:, :], s1r[0:1, 512:1024],
                    start=True, stop=True))
                B.wait("vector", "mm", m2)
                B.vop(lambda e, tgt=tgt, cb=cb: e.tensor_copy(
                    tgt[:, cb * W: cb * W + 512], pb[4][:, 0:512]))
                B.vop(lambda e, tgt=tgt, cb=cb: e.tensor_copy(
                    tgt[:, cb * W + 512: (cb + 1) * W], pb[5][:, 0:512]))
                B.wait("gpsimd", "mm", m2)

        wuse = {0: 0, 1: 0, 2: 0, 3: 0}
        prev_ln = 0

        def preamble(mt, k2t):
            nonlocal prev_ln
            xs = x_sb[:, mt * D:(mt + 1) * D]
            B.wait("scalar", "mm", B.cnt["mm"])
            B.wait("scalar", "vx", prev_ln)
            s3 = emit_ln(xs, ybuf, st, ybuf)
            prev_ln = s3
            B.vop(lambda e: e.reduce_max(mxc[:, 0:1], ybuf[:, 0:D], axis=AX.X,
                                         apply_absolute_value=True))
            B.vop(lambda e: e.tensor_single_scalar(mxc[:, 0:1], mxc[:, 0:1], 1e-5,
                                                   ALU.max))
            B.vop(lambda e: e.reciprocal(mxc[:, 1:2], mxc[:, 0:1]))
            B.vop(lambda e: e.tensor_single_scalar(mxc[:, 1:2], mxc[:, 1:2], 127.0,
                                                   ALU.mult))
            B.vop(lambda e: e.reciprocal(mxc[:, 2:3], mxc[:, 1:2]))
            emit_round(ybuf, D, 1, xif2, slice(0, D))
            xi_done = B.cnt["vx"]
            B.wait("tensor", "vx", xi_done)
            bank_evac = {0: B.cnt["sc"], 1: B.cnt["sc"]}
            for k in range(16):
                bi = k % 2
                B.wait("tensor", "sc", bank_evac[bi])
                mmv = B.op("tensor", "mm", lambda e, k=k, bi=bi: e.transpose(
                    pb[bi][:, 0:128], xif2[:, k * 128:(k + 1) * 128], ident[:, :]))
                B.wait("scalar", "mm", mmv)
                bank_evac[bi] = B.sop(lambda e, k=k, bi=bi: e.copy(
                    xiT[:, k * 128:(k + 1) * 128], pb[bi][:, 0:128]))
            xiT_done_sc = B.cnt["sc"]

            prev_gelu = 0
            set_done = {0: B.cnt["vx"], 1: B.cnt["vx"]}
            for cb4 in range(4):
                bset = [pb[4 * (cb4 % 2) + i] for i in range(4)]
                B.wait("tensor", "vx", set_done[cb4 % 2])
                if cb4 == 0:
                    B.wait("tensor", "sc", xiT_done_sc)
                for kr in range(16):
                    bi = (cb4 * 16 + kr) % 3
                    wsem = B.WSEMS[bi]
                    B.wait("sync", "mm", wuse[bi])
                    mw = B.dma("sync", wsb[bi].ap(),
                               d_tq1l1[kr * 128:(kr + 1) * 128,
                                       cb4 * 2048:(cb4 + 1) * 2048], wsem)
                    B.wait("tensor", wsem, mw)
                    for j in range(4):
                        B.op("tensor", "mm",
                             lambda e, kr=kr, j=j, bi=bi, bset=bset: e.matmul(
                                 bset[j][:, :], xiT[:, kr * 128:(kr + 1) * 128],
                                 wsb[bi][:, j * 512:(j + 1) * 512],
                                 start=(kr == 0), stop=(kr == 15)),
                             inc=(1 if j == 3 else 0))
                    wuse[bi] = B.cnt["mm"]
                gmm = B.cnt["mm"]
                for j in range(4):
                    hb = cb4 * 4 + j
                    B.wait("vector", "mm", gmm)
                    B.wait("vector", "sc", prev_gelu)
                    B.vop(lambda e, hb=hb, j=j, bset=bset: e.tensor_tensor(
                        zb[:, 0:512], bset[j][:, :], sc1[:, hb * 512:(hb + 1) * 512],
                        ALU.mult))
                    vt_ = B.cnt["vx"]
                    B.wait("scalar", "vx", vt_)
                    prev_gelu = B.sop(lambda e, hb=hb: e.activation(
                        out=gbuf[:, hb * 512:(hb + 1) * 512], in_=zb[:, 0:512],
                        func=AF.Gelu_apprx_tanh, scale=mxc[:, 2:3]))
                set_done[cb4 % 2] = B.cnt["vx"]
            g_done = B.cnt["sc"]

            B.wait("vector", "sc", g_done)
            B.vop(lambda e: e.reduce_max(mxc[:, 0:1], gbuf[:, :], axis=AX.X,
                                         apply_absolute_value=True))
            B.vop(lambda e: e.tensor_single_scalar(mxc[:, 0:1], mxc[:, 0:1], 1e-5,
                                                   ALU.max))
            B.vop(lambda e: e.reciprocal(mxc[:, 1:2], mxc[:, 0:1]))
            B.vop(lambda e: e.tensor_single_scalar(mxc[:, 1:2], mxc[:, 1:2], 127.0,
                                                   ALU.mult))
            B.vop(lambda e: e.reciprocal(mxc[:, 2:3], mxc[:, 1:2]))
            tr_banks = {0: ("sc", B.cnt["sc"]), 1: ("sc", B.cnt["sc"])}
            tr_done = {0: 0, 1: 0}
            for cc in range(H1 // RW):
                xpar = cc % 2
                xx = xif[:, xpar * RW:(xpar + 1) * RW]
                B.wait("vector", "mm", tr_done[xpar])
                B.vop(lambda e, cc=cc: e.tensor_scalar(
                    zb[:, :], gbuf[:, cc * RW:(cc + 1) * RW], mxc[:, 1:2], None,
                    op0=ALU.mult))
                B.vop(lambda e: e.tensor_copy(ib[:, :], zb[:, :]))
                B.vop(lambda e: e.tensor_copy(fb[:, :], ib[:, :]))
                B.vop(lambda e: e.tensor_sub(db[:, :], zb[:, :], fb[:, :]))
                B.vop(lambda e: e.scalar_tensor_tensor(
                    out=zb[:, :], in0=db[:, :], scalar=0.5, in1=fb[:, :],
                    op0=ALU.is_gt, op1=ALU.add))
                B.vop(lambda e: e.tensor_single_scalar(
                    db[:, :], db[:, :], -0.5, ALU.is_lt))
                k2_done = B.vop(lambda e, xx=xx: e.tensor_sub(xx, zb[:, :], db[:, :]))
                B.wait("tensor", "vx", k2_done)
                for k in range(RW // 128):
                    bi = k % 2
                    B.wait("tensor", tr_banks[bi][0], tr_banks[bi][1])
                    mmv = B.op("tensor", "mm", lambda e, k=k, bi=bi, xx=xx: e.transpose(
                        pb[bi][:, 0:128], xx[:, k * 128:(k + 1) * 128], ident[:, :]))
                    B.wait("scalar", "mm", mmv)
                    tr_banks[bi] = ("sc", B.sop(
                        lambda e, cc=cc, k=k, bi=bi, k2t=k2t: e.copy(
                            k2t[:, (cc * 4 + k) * 128:(cc * 4 + k + 1) * 128],
                            pb[bi][:, 0:128])))
                tr_done[xpar] = B.cnt["mm"]
            B.wait("vector", "sc", B.cnt["sc"])

        for pr in range(NMT // 2):
            ma, mb_ = 2 * pr, 2 * pr + 1
            preamble(ma, k2T)
            B.vop(lambda e: e.tensor_copy(mxc[:, 3:4], mxc[:, 2:3]))
            preamble(mb_, k2Tb)
            k2_all = B.cnt["vx"]
            k2_all_sc = B.cnt["sc"]

            for hc in range(64):
                bi = hc % 3
                wsem = B.WSEMS[bi]
                B.wait("sync", "mm", wuse[bi])
                mw = B.dma("sync", wsb[bi].ap(),
                           d_tq2l1[hc * 128:(hc + 1) * 128, :], wsem)
                B.wait("tensor", wsem, mw)
                if hc == 0:
                    B.wait("tensor", "vx", k2_all)
                    B.wait("tensor", "sc", k2_all_sc)
                for n in range(4):
                    B.op("tensor", "mm", lambda e, hc=hc, n=n, bi=bi: e.matmul(
                        pb[4 + n][:, :], k2T[:, hc * 128:(hc + 1) * 128],
                        wsb[bi][:, n * 512:(n + 1) * 512],
                        start=(hc == 0), stop=(hc == 63)),
                        inc=0)
                for n in range(4):
                    B.op("tensor", "mm", lambda e, hc=hc, n=n, bi=bi: e.matmul(
                        pb[n][:, :], k2Tb[:, hc * 128:(hc + 1) * 128],
                        wsb[bi][:, n * 512:(n + 1) * 512],
                        start=(hc == 0), stop=(hc == 63)),
                        inc=(1 if n == 3 else 0))
                wuse[bi] = B.cnt["mm"]
            g2 = B.cnt["mm"]
            B.wait("vector", "mm", g2)
            evv = 0
            for n in range(4):
                B.vop(lambda e, n=n: e.tensor_tensor(
                    zb[:, 0:512], pb[4 + n][:, :], sc2[:, n * 512:(n + 1) * 512],
                    ALU.mult))
                evv = B.vop(lambda e, n=n, ma=ma: e.scalar_tensor_tensor(
                    out=x_sb[:, ma * D + n * 512: ma * D + (n + 1) * 512],
                    in0=zb[:, 0:512], scalar=mxc[:, 3:4],
                    in1=x_sb[:, ma * D + n * 512: ma * D + (n + 1) * 512],
                    op0=ALU.mult, op1=ALU.add))
            for n in range(4):
                B.vop(lambda e, n=n: e.tensor_tensor(
                    zb[:, 0:512], pb[n][:, :], sc2[:, n * 512:(n + 1) * 512],
                    ALU.mult))
                evv = B.vop(lambda e, n=n, mb_=mb_: e.scalar_tensor_tensor(
                    out=x_sb[:, mb_ * D + n * 512: mb_ * D + (n + 1) * 512],
                    in0=zb[:, 0:512], scalar=mxc[:, 2:3],
                    in1=x_sb[:, mb_ * D + n * 512: mb_ * D + (n + 1) * 512],
                    op0=ALU.mult, op1=ALU.add))
            B.wait("tensor", "vx", evv)
            B.wait("tensor", "vx", evv)

    def ph_out():
        B.dma("gpsimd", out.rearrange("(t p) d -> p t d", p=128),
              x_sb.ap().rearrange("p (t d) -> p t d", d=D), "od")
        for name, dst in dbg_outs.items():
            srcs = {"d_q": d_q, "d_kf": d_kf, "d_u": d_u, "d_vt": d_vt,
                    "d_C": d_C, "ag_out0": ag_out0.ap(),
                    "d_tq1l0": d_tq1l0, "d_tq2l0": d_tq2l0,
                    "d_tq1l1": d_tq1l1, "d_tq2l1": d_tq2l1,
                    "kwin": kwin, "vwin": vwin}
            B.dma("gpsimd", dst, srcs[name], "od")
        B.wait("gpsimd", "od", B.cnt["od"])

    B.phase(ph_const, name='const')

    if upto >= 2:
        with ExitStack() as wctx:
            def sbw(name, shape, dt=F32):
                return wctx.enter_context(nc.sbuf_tensor(name, shape, dt))
            bufs_w = dict(
                wsb=[sbw("ws0", [128, 1024]), sbw("ws1", [128, 1024])],
                dmp=sbw("dmp", [128, 1024]),
                acol=sbw("acol", [128, 1]),
                acol2=sbw("acol2", [128, 1]),
                acc=sbw("acc", [128, 2]),
                s1t=sbw("s1t", [128, 64]),
                s2t=sbw("s2t", [128, 16]),
                srow=sbw("srow", [1, 8]),
                tn=sbw("tn", [128, 1024]),
                tn2=sbw("tn2", [128, 1024]),
                qo=[sbw("qo0", [128, 1024], BF16), sbw("qo1", [128, 1024], BF16)],
                qo2=sbw("qo2", [128, 1024]),
                qo3=sbw("qo3", [128, 1024]),
                thp=sbw("thp", [128, 1024]),
                thn=sbw("thn", [128, 1024]),
                s1r=sbw("s1r", [1, 1024]),
            )
            B.phase(lambda: ph_wstats(bufs_w), name='wstats')
            B.phase(lambda: ph_tern(bufs_w), name='tern')

    l0outer = ExitStack()
    M_sb = l0outer.enter_context(nc.sbuf_tensor("M_sb", [128, NH * DH], F32))
    kv_sb = l0outer.enter_context(nc.sbuf_tensor("kv_sb", [128, NH * DH], F32))
    with ExitStack() as l1ctx:
        def sbl(name, shape, dt=F32):
            return l1ctx.enter_context(nc.sbuf_tensor(name, shape, dt))
        bufs1 = dict(
            yT=sbl("yT", [128, 16 * T]),
            wb=[sbl("wb0", [128, 16 * 128]), sbl("wb1", [128, 16 * 128])],
            ev=sbl("ev", [128, 1024]),
            ktok=sbl("ktok", [128, 8 * 128]),
            vtok=sbl("vtok", [128, 8 * 128]),
            Cb=sbl("Cb", [128, 4 * 128]),
            KD=sbl("KD", [128, 2]),
            ybuf=sbl("ybuf", [128, D]),
            st=stats_tiles(sbl, "s1_"),
        )
        B.phase(lambda: ph_l0_sweep1(bufs1), name='l0s1')

    with ExitStack() as l2ctx:
        def sbl2(name, shape, dt=F32):
            return l2ctx.enter_context(nc.sbuf_tensor(name, shape, dt))
        bufs2 = dict(
            qb=sbl2("qb", [128, NH * BLK]),
            kb=sbl2("kb", [128, NH * BLK]),
            vtb=sbl2("vtb", [128, 2 * D]),
            ub=sbl2("ub", [128, NH * BLK]),
            Cl=sbl2("Cl", [128, NH * 128]),
            ob=sbl2("ob", [128, NH * BLK]),
            agm=sbl2("agm", [128, NH * 128]),
            qp=sbl2("qp", [128, BLK]),
            qd=sbl2("qd", [128, BLK]),
            dct=sbl2("dct", [128, 2 * BLK]),
            pbuf=sbl2("pbuf", [128, 2 * BLK]),
            djc=sbl2("djc", [128, 1]),
            tmph=sbl2("tmph", [128, 128]),
            wv=[sbl2("wv0", [128, 512]), sbl2("wv1", [128, 512])],
            dump=sbl2("dump2", [128, BLK]),
            rmsc=sbl2("rmsc", [128, 2]),
            r_a=sbl2("r_a", [1, BLK]),
            r_b=sbl2("r_b", [1, BLK]),
            r_c=sbl2("r_c", [1, BLK]),
        )
        B.phase(lambda: ph_l0_sweep2(bufs2), name='l0s2')
    l0outer.close()

    if upto >= 2:
        with ExitStack() as fctx:
            def sbf(name, shape, dt=F32):
                return fctx.enter_context(nc.sbuf_tensor(name, shape, dt))
            bufs3 = dict(
                ybuf=sbf("f_ybuf", [128, D]),
                st=stats_tiles(sbf, "f_st_"),
                y2h=sbf("y2h", [128, 16 * BLK], BF16),
                y2l=sbf("y2l", [128, 16 * BLK], BF16),
                wsb=[sbf(f"fw{i}", [128, 2048], BF16) for i in range(4)],
                rb=sbf("rb", [128, BLK]),
                sqf=sbf("sqf", [128, BLK]),
                sqh=sbf("sqh", [128, 40 * BLK], BF16),
                sql=sbf("sql", [128, 40 * BLK], BF16),
            )
            B.phase(lambda: ph_l0_ffn(bufs3), name='l0f')

    if upto >= 3:
        with ExitStack() as actx:
            def sba(name, shape, dt=F32):
                return actx.enter_context(nc.sbuf_tensor(name, shape, dt))
            bufs4 = dict(
                y3T=sba("y3T", [128, 16 * T]),
                wb=[sba("aw0", [128, 16 * 128]), sba("aw1", [128, 16 * 128])],
                ev=sba("a_ev", [128, 1024]),
                vtok=sba("a_vtok", [128, 8 * 128]),
                ybuf=sba("a_ybuf", [128, D]),
                st=stats_tiles(sba, "a_st_"),
            )
            B.phase(lambda: ph_l1_attn_a(bufs4), name='l1a_a')
        with ExitStack() as bctx:
            def sbb(name, shape, dt=F32):
                return bctx.enter_context(nc.sbuf_tensor(name, shape, dt))
            bufs5 = dict(
                qmb=sbb("qmb", [128, NH * 512]),
                osb=sbb("osb", [128, NH * 512]),
                masks=sbb("masks", [128, NKT * 512], BF16),
                ktile=[sbb("ktile0", [128, 2048]), sbb("ktile1", [128, 2048])],
                vtile=[sbb("vtile0", [128, 2048]), sbb("vtile1", [128, 2048])],
                tbv=[sbb(f"tbv{i}", [128, 512]) for i in range(3)],
                pbm=[sbb(f"pbm{i}", [128, 512]) for i in range(3)],
                lsb=sbb("lsb", [128, 64]),
                lsum=sbb("lsum", [128, 2048]),
                lrows=sbb("lrows", [64, 128]),
                lrow=sbb("lrow", [1, 128]),
                wvb=[sbb("bw0", [128, 512]), sbb("bw1", [128, 512])],
            )
            B.phase(lambda: ph_l1_attn_b(bufs5), name='l1a_b')

    if upto >= 4:
        with ExitStack() as gctx:
            def sbg(name, shape, dt=F32):
                return gctx.enter_context(nc.sbuf_tensor(name, shape, dt))
            bufs6 = dict(
                st=stats_tiles(sbg, "g_st_"),
                sc1=sbg("sc1", [128, H1]),
                sc2=sbg("sc2", [128, D]),
                s1r=sbg("g_s1r", [1, 1024]),
                zb=sbg("g_zb", [128, 512]),
                fb=sbg("g_fb", [128, 512]),
                db=sbg("g_db", [128, 512]),
                ib=sbg("g_ib", [128, 512], I32),
                xif=sbg("g_xif", [128, 2 * 512]),
                xiT=sbg("g_xiT", [128, 16 * 128], BF16),
                gbuf=sbg("g_gbuf", [128, H1]),
                k2T=sbg("g_k2T", [128, 64 * 128], BF16),
                k2Tb=sbg("g_k2Tb", [128, 64 * 128], BF16),
                wsb=[sbg(f"g_w{i}", [128, 2048], BF16) for i in range(3)],
                mxc=sbg("g_mxc", [128, 4]),
            )
            B.phase(lambda: ph_l1_ffn(bufs6), name='l1f')

    B.phase(ph_out, name='out')
    B.ctx.close()
    return nc


def _tp(a):
    return np.ascontiguousarray(np.asarray(a, dtype=np.float32).T)


def _pack4(w):
    r = w.shape[0] // D
    t = np.asarray(w, np.float32).reshape(r, NH, 128, NH, 128)
    return np.ascontiguousarray(t.transpose(0, 1, 4, 3, 2)).reshape(r, NH, 128, D)


def _masks_for_core(c):
    import ml_dtypes
    m = np.zeros((2, NKT, 128, 512), dtype=ml_dtypes.bfloat16)
    sp = np.arange(128)[:, None]
    mp = np.arange(512)[None, :]
    for mb in range(2):
        gm = T * c + 512 * mb
        for k in range(NKT):
            bj = gm - WIN + 128 * k
            i = gm + mp
            j = bj + sp
            ok = (j <= i) & (i - j < WIN) & (j >= 0)
            m[mb, k] = ok.astype(np.float32)
    return m


def kernel(L0_attn_x, L0_attn_slope_rate, L0_attn_W_qkvu, L0_attn_W_out,
           L0_ffn_w_fc, L0_ffn_w_proj, L1_attn_Wq, L1_attn_Wk, L1_attn_Wv,
           L1_attn_Wo, L1_ffn_w1, L1_ffn_w2, _upto=4, _dbg=()):
    x = np.ascontiguousarray(np.asarray(L0_attn_x, np.float32).reshape(NCORES * T, D))
    shared = dict(
        w4qkvu=_pack4(L0_attn_W_qkvu), woutT=_tp(L0_attn_W_out),
        wfcT=_tp(L0_ffn_w_fc), wprojT=_tp(L0_ffn_w_proj),
        w3q=_pack4(L1_attn_Wq)[0], w3k=_pack4(L1_attn_Wk)[0],
        w3v=_pack4(L1_attn_Wv)[0],
        woT=_tp(L1_attn_Wo),
        w1n=np.ascontiguousarray(np.asarray(L1_ffn_w1, np.float32)),
        w1T=_tp(L1_ffn_w1),
        w2n=np.ascontiguousarray(np.asarray(L1_ffn_w2, np.float32)),
        w2T=_tp(L1_ffn_w2),
        slope_in=np.ascontiguousarray(
            np.asarray(L0_attn_slope_rate, np.float32).reshape(NH)),
    )
    in_maps = []
    for c in range(NCORES):
        m = dict(shared)
        m["xin"] = np.ascontiguousarray(x[c * T:(c + 1) * T])
        m["gates_in"] = (np.arange(NCORES) < c).astype(np.float32)
        m["lmask"] = _masks_for_core(c)
        in_maps.append(m)
    nc = build_nc(upto=_upto, dbg=_dbg)
    res = run_bass_kernel_spmd(nc, in_maps, core_ids=list(range(NCORES)))
    outs = np.concatenate([res.results[c]["out"] for c in range(NCORES)], axis=0)
    if _dbg:
        dbgd = [{k: v for k, v in res.results[c].items() if k.startswith("dbg_")}
                for c in range(NCORES)]
        return outs.reshape(1, NCORES * T, D), dbgd
    return outs.reshape(1, NCORES * T, D)


# version 10
# speedup vs baseline: 1.0859x; 1.0796x over previous
import numpy as np
from contextlib import ExitStack

import concourse.bass as bass
import concourse.mybir as mybir
from concourse.bass_utils import run_bass_kernel_spmd

F32 = mybir.dt.float32
BF16 = mybir.dt.bfloat16
I32 = mybir.dt.int32
AF = mybir.ActivationFunctionType
ALU = mybir.AluOpType
AX = mybir.AxisListType

NCORES = 8
T = 1024
D = 2048
NH = 16
DH = 128
BLK = 256
NBLK = T // BLK
NMT = T // 128
H0 = 5120
H1 = 8192
WIN = 1024
NKT = 12

ENGS = ("sync", "gpsimd", "tensor", "vector", "scalar")


class _B:

    def __init__(self, nc):
        self.nc = nc
        self.ctx = ExitStack()
        self.sem = {}
        self.cnt = {}
        for name in ("wd", "gd", "mm", "vx", "sc", "gx", "cc", "bar", "od",
                     "w0", "w1", "w2", "w3", "ev", "vt", "cs", "lq", "lk",
                     "lv", "lu", "lc", "lr"):
            self.sem[name] = self.ctx.enter_context(nc.semaphore(name))
            self.cnt[name] = 0
        self.en = None
        self.eng = None

    def dma(self, qeng, out, in_, sem="gd"):
        self.cnt[sem] += 16
        if self.en == qeng:
            if callable(out):
                out = out()
            if callable(in_):
                in_ = in_()
            self.eng.dma_start(out=out, in_=in_).then_inc(self.sem[sem], 16)
        return self.cnt[sem]

    AUTO_CHAIN = {"vector": "vx", "scalar": "sc"}

    def op(self, oeng, sem, fn, inc=1):
        own = self.AUTO_CHAIN.get(oeng)
        if own is not None and self.en == oeng and self.cnt[own] > 0:
            self.eng.wait_ge(self.sem[own], self.cnt[own])
        if inc:
            self.cnt[sem] += inc
        if self.en == oeng:
            inst = fn(self.eng)
            if inc:
                inst.then_inc(self.sem[sem], inc)
        return self.cnt[sem]

    def vop(self, fn):
        return self.op("vector", "vx", fn)

    def sop(self, fn):
        return self.op("scalar", "sc", fn)

    def sil(self, oeng, fn):
        if self.en == oeng:
            fn(self.eng)

    def wait(self, oeng, sem, val):
        if self.en == oeng and val > 0:
            self.eng.wait_ge(self.sem[sem], val)

    WSEMS = ("w0", "w1", "w2", "w3")

    ENG_SEMS = {"sync": ("wd", "w0", "w1", "w2", "w3"),
                "gpsimd": ("gd", "gx", "od", "cc", "ev", "vt", "cs",
                           "lq", "lk", "lv", "lu", "lc", "lr"),
                "tensor": ("mm",), "vector": ("vx",), "scalar": ("sc",)}

    def barrier(self):
        self.cnt["bar"] += len(ENGS)
        tgt = self.cnt["bar"]
        snap = dict(self.cnt)
        with self.nc.Block() as blk:
            for name in ENGS:
                def body(eng, name=name, tgt=tgt, snap=snap):
                    for s in self.ENG_SEMS[name]:
                        if snap[s] > 0:
                            eng.wait_ge(self.sem[s], snap[s])
                    eng.sem_inc(self.sem["bar"], 1)
                    eng.wait_ge(self.sem["bar"], tgt)
                getattr(blk, name)(body)

    def chain(self, oeng, sem, fns, first_wait=None):
        v = None
        for i, fn in enumerate(fns):
            if i == 0 and first_wait is not None:
                self.wait(oeng, first_wait[0], first_wait[1])
            if v is not None:
                self.wait(oeng, sem, v)
            v = self.op(oeng, sem, fn)
        return v

    def phase(self, emit_fn, name=None):
        snap = dict(self.cnt)
        final = None
        scope = None
        if name is not None:
            try:
                scope = self.nc.enter_named_scope(name, False)
            except Exception:
                scope = None
        with self.nc.Block() as blk:
            for name in ENGS:
                self.cnt = dict(snap)
                self.en = name

                def body(eng):
                    self.eng = eng
                    emit_fn()
                getattr(blk, name)(body)
                if final is None:
                    final = dict(self.cnt)
                else:
                    assert final == self.cnt, "phase emitter is engine-dependent!"
        self.en = self.eng = None
        if scope is not None:
            try:
                self.nc.leave_named_scope(name, scope[0], False)
            except Exception:
                pass
        self.barrier()


def build_nc(upto=4, dbg=(), world=NCORES):
    nc = bass.Bass("TRN2", target_bir_lowering=False)
    B = _B(nc)

    xin = nc.dram_tensor("xin", [T, D], F32, kind="ExternalInput").ap()
    w4qkvu = nc.dram_tensor("w4qkvu", [4, NH, 128, 2048], F32, kind="ExternalInput").ap()
    woutT = nc.dram_tensor("woutT", [D, D], F32, kind="ExternalInput").ap()
    wfcT = nc.dram_tensor("wfcT", [D, H0], F32, kind="ExternalInput").ap()
    wprojT = nc.dram_tensor("wprojT", [H0, D], F32, kind="ExternalInput").ap()
    w3q = nc.dram_tensor("w3q", [NH, 128, 2048], F32, kind="ExternalInput").ap()
    w3k = nc.dram_tensor("w3k", [NH, 128, 2048], F32, kind="ExternalInput").ap()
    w3v = nc.dram_tensor("w3v", [NH, 128, 2048], F32, kind="ExternalInput").ap()
    woT = nc.dram_tensor("woT", [D, D], F32, kind="ExternalInput").ap()
    w1n = nc.dram_tensor("w1n", [H1, D], F32, kind="ExternalInput").ap()
    w1T = nc.dram_tensor("w1T", [D, H1], F32, kind="ExternalInput").ap()
    w2n = nc.dram_tensor("w2n", [D, H1], F32, kind="ExternalInput").ap()
    w2T = nc.dram_tensor("w2T", [H1, D], F32, kind="ExternalInput").ap()
    slope_in = nc.dram_tensor("slope_in", [NH], F32, kind="ExternalInput").ap()
    gates_in = nc.dram_tensor("gates_in", [NCORES], F32, kind="ExternalInput").ap()
    lmask = nc.dram_tensor("lmask", [2, NKT, 128, 512], BF16, kind="ExternalInput").ap()
    out = nc.dram_tensor("out", [T, D], F32, kind="ExternalOutput").ap()

    d_q = nc.dram_tensor("d_q", [D, T], F32).ap()
    d_kf = nc.dram_tensor("d_kf", [D, T], F32).ap()
    d_u = nc.dram_tensor("d_u", [D, T], F32).ap()
    d_vt = nc.dram_tensor("d_vt", [T, D], F32).ap()
    d_C = nc.dram_tensor("d_C", [NH, NBLK, DH, DH], F32).ap()
    ag_in0 = nc.dram_tensor("ag_in0", [NH * DH, DH], F32)
    ag_out0 = nc.dram_tensor("ag_out0", [world * NH * DH, DH], F32, addr_space=("Shared" if world > 1 else "Local"))
    ag_in_k = nc.dram_tensor("ag_in_k", [D, T], F32)
    ag_out_k = nc.dram_tensor("ag_out_k", [world * D, T], F32, addr_space=("Shared" if world > 1 else "Local"))
    ag_in_v = nc.dram_tensor("ag_in_v", [T, D], F32)
    ag_out_v = nc.dram_tensor("ag_out_v", [world * T, D], F32, addr_space=("Shared" if world > 1 else "Local"))
    kwin = nc.dram_tensor("kwin", [D, 2 * T], F32).ap()
    vwin = nc.dram_tensor("vwin", [2 * T, D], F32).ap()
    d_tq1l0 = nc.dram_tensor("d_tq1l0", [D, H0], BF16).ap()
    d_tq2l0 = nc.dram_tensor("d_tq2l0", [H0, D], BF16).ap()
    d_tq1l1 = nc.dram_tensor("d_tq1l1", [D, H1], BF16).ap()
    d_tq2l1 = nc.dram_tensor("d_tq2l1", [H1, D], BF16).ap()
    d_scr = nc.dram_tensor("d_scr", [8192], F32).ap()
    d_s1 = nc.dram_tensor("d_s1", [H1], F32).ap()
    d_s2 = nc.dram_tensor("d_s2", [D], F32).ap()

    dbg_specs = {"d_q": ([D, T], F32), "d_kf": ([D, T], F32), "d_u": ([D, T], F32),
                 "d_vt": ([T, D], F32), "d_C": ([NH, NBLK, DH, DH], F32),
                 "ag_out0": ([world * NH * DH, DH], F32),
                 "kwin": ([D, 2 * T], F32), "vwin": ([2 * T, D], F32),
                 "d_tq1l0": ([D, H0], BF16), "d_tq2l0": ([H0, D], BF16),
                 "d_tq1l1": ([D, H1], BF16), "d_tq2l1": ([H1, D], BF16)}
    dbg_outs = {n: nc.dram_tensor("dbg_" + n, dbg_specs[n][0], dbg_specs[n][1],
                                  kind="ExternalOutput").ap() for n in dbg}

    ctx = B.ctx

    def sb(name, shape, dt=F32):
        return ctx.enter_context(nc.sbuf_tensor(name, shape, dt))

    x_sb = sb("x_sb", [128, NMT * D])
    ident = sb("ident", [128, 128])
    ones_c = sb("ones_c", [128, 1])
    ones_r = sb("ones_r", [1, 128])
    ones_bc = sb("ones_bc", [128, 128])
    slope_row = sb("slope_row", [1, NH])
    negs_bc = sb("negs_bc", [128, NH])
    bd_bc = sb("bd_bc", [128, NH])
    bd4m1_bc = sb("bd4m1_bc", [128, NH])
    G = sb("G", [128, NCORES])
    iotaq = sb("iotaq", [128, BLK])
    idxd = sb("idxd", [128, 2 * BLK])
    io255 = sb("io255", [128, 1])
    neg30 = sb("neg30", [128, 1])
    io127 = sb("io127", [128, 1])
    am_fc = sb("am_fc", [128, 1])
    am_pr = sb("am_pr", [128, 1])
    thr_fc = sb("thr_fc", [128, 1])
    thr_pr = sb("thr_pr", [128, 1])
    nthr_fc = sb("nthr_fc", [128, 1])
    nthr_pr = sb("nthr_pr", [128, 1])
    pb = [ctx.enter_context(nc.psum_tensor(f"pb{i}", [128, 512], F32)) for i in range(8)]
    M_sb = kv_sb = None

    def ph_const():
        B.dma("sync", slope_row[0:1, :], slope_in[None, :], "wd")
        B.dma("sync", G[0:1, 0:NCORES], gates_in[None, :], "wd")
        m_slope = m_gates = B.cnt["wd"]

        B.op("gpsimd", "gx", lambda e: e.memset(ones_c[:, :], 1.0))
        B.op("gpsimd", "gx", lambda e: e.memset(ones_r[:, :], 1.0))
        B.op("gpsimd", "gx", lambda e: e.memset(ones_bc[:, :], 1.0))
        v = B.op("gpsimd", "gx", lambda e: e.memset(ident[:, :], 0.0))
        B.wait("gpsimd", "gx", v)
        B.op("gpsimd", "gx", lambda e: e.affine_select(
            out=ident[:, :], in_=ident[:, :], compare_op=ALU.not_equal,
            fill=1.0, base=0, pattern=[[-1, 128]], channel_multiplier=1))
        B.op("gpsimd", "gx", lambda e: e.iota(iotaq[:, :], [[1, BLK]], base=1,
                                         channel_multiplier=0, allow_small_or_imprecise_dtypes=True))
        B.op("gpsimd", "gx", lambda e: e.memset(neg30[:, :], -30.0))
        v2 = B.op("gpsimd", "gx", lambda e: e.iota(
            idxd[:, :], [[-128, 2], [1, BLK]], base=0, channel_multiplier=-1, allow_small_or_imprecise_dtypes=True))
        B.wait("gpsimd", "gx", v2)
        B.op("gpsimd", "gx", lambda e: e.affine_select(
            out=idxd[:, :], in_=idxd[:, :], compare_op=ALU.is_ge,
            fill=1.0e30, base=0, pattern=[[-128, 2], [1, BLK]],
            channel_multiplier=-1))
        B.op("gpsimd", "gx", lambda e: e.iota(io255[:, :], [[0, 1]], base=255,
                                         channel_multiplier=-1, allow_small_or_imprecise_dtypes=True))
        v4 = B.op("gpsimd", "gx", lambda e: e.iota(io127[:, :], [[0, 1]], base=127,
                                                   channel_multiplier=-1, allow_small_or_imprecise_dtypes=True))

        B.wait("scalar", "wd", m_slope)
        B.op("scalar", "sc", lambda e: e.activation(
            out=bd_bc[0:1, :], in_=slope_row[0:1, :], func=AF.Exp, scale=-256.0))
        B.op("scalar", "sc", lambda e: e.activation(
            out=bd4m1_bc[0:1, :], in_=slope_row[0:1, :], func=AF.Exp, scale=-1024.0))
        s_ng = B.op("scalar", "sc", lambda e: e.mul(
            negs_bc[0:1, :], slope_row[0:1, :], -1.0))

        B.wait("tensor", "sc", s_ng)
        B.wait("tensor", "gx", v4)
        B.wait("tensor", "wd", m_gates)
        B.op("tensor", "mm", lambda e: e.matmul(
            pb[0][:, 0:NH], ones_r[:, :], negs_bc[0:1, :], start=True, stop=True))
        B.op("tensor", "mm", lambda e: e.matmul(
            pb[0][:, 64:64 + NH], ones_r[:, :], bd_bc[0:1, :], start=True, stop=True))
        B.op("tensor", "mm", lambda e: e.matmul(
            pb[0][:, 128:128 + NH], ones_r[:, :], bd4m1_bc[0:1, :], start=True, stop=True))
        mmv = B.op("tensor", "mm", lambda e: e.matmul(
            pb[0][:, 192:192 + NCORES], ones_r[:, :], G[0:1, 0:NCORES],
            start=True, stop=True))

        B.wait("vector", "mm", mmv)
        B.op("vector", "vx", lambda e: e.tensor_copy(negs_bc[:, :], pb[0][:, 0:NH]))
        B.op("vector", "vx", lambda e: e.tensor_copy(bd_bc[:, :], pb[0][:, 64:64 + NH]))
        B.op("vector", "vx", lambda e: e.tensor_copy(G[:, :], pb[0][:, 192:192 + NCORES]))
        B.op("vector", "vx", lambda e: e.tensor_single_scalar(
            bd4m1_bc[:, :], pb[0][:, 128:128 + NH], 1.0, ALU.subtract))

        B.dma("gpsimd", x_sb.ap().rearrange("p (t d) -> p t d", d=D),
              xin.rearrange("(t p) d -> p t d", p=128), "gd")
        B.wait("gpsimd", "gd", B.cnt["gd"])

    def emit_ln(xs, ybuf, st, dump):
        s1 = B.op("scalar", "sc", lambda e: e.activation(
            out=dump[:, 0:D], in_=xs, func=AF.Square, accum_out=st["ssq"][:, 0:1]))
        B.op("vector", "vx", lambda e: e.reduce_sum(
            st["ssum"][:, 0:1], xs, axis=AX.X))
        B.wait("vector", "sc", s1)
        B.op("vector", "vx", lambda e: e.tensor_single_scalar(
            st["mean"][:, 0:1], st["ssum"][:, 0:1], 1.0 / D, ALU.mult))
        B.op("vector", "vx", lambda e: e.tensor_single_scalar(
            st["var"][:, 0:1], st["ssq"][:, 0:1], 1.0 / D, ALU.mult))
        B.op("vector", "vx", lambda e: e.tensor_tensor(
            st["m2"][:, 0:1], st["mean"][:, 0:1], st["mean"][:, 0:1], ALU.mult))
        v2 = B.op("vector", "vx", lambda e: e.scalar_tensor_tensor(
            out=st["var"][:, 0:1], in0=st["var"][:, 0:1], scalar=1e-5,
            in1=st["m2"][:, 0:1], op0=ALU.add, op1=ALU.subtract))
        B.wait("scalar", "vx", v2)
        s2 = B.op("scalar", "sc", lambda e: e.activation(
            out=st["rstd"][:, 0:1], in_=st["var"][:, 0:1], func=AF.Sqrt))
        B.wait("vector", "sc", s2)
        B.op("vector", "vx", lambda e: e.reciprocal(st["rstd"][:, 0:1], st["rstd"][:, 0:1]))
        B.op("vector", "vx", lambda e: e.tensor_tensor(
            st["m2"][:, 0:1], st["rstd"][:, 0:1], st["rstd"][:, 0:1], ALU.mult))
        B.op("vector", "vx", lambda e: e.tensor_tensor(
            st["m2"][:, 0:1], st["m2"][:, 0:1], st["var"][:, 0:1], ALU.mult))
        B.op("vector", "vx", lambda e: e.tensor_scalar(
            st["m2"][:, 0:1], st["m2"][:, 0:1], -0.5, 1.5, op0=ALU.mult, op1=ALU.add))
        B.op("vector", "vx", lambda e: e.tensor_tensor(
            st["rstd"][:, 0:1], st["rstd"][:, 0:1], st["m2"][:, 0:1], ALU.mult))
        v4 = B.op("vector", "vx", lambda e: e.tensor_scalar(
            ybuf[:, 0:D], xs, st["mean"][:, 0:1], st["rstd"][:, 0:1],
            op0=ALU.subtract, op1=ALU.mult))
        return v4

    def stats_tiles(alloc, pfx):
        return {k: alloc(pfx + k, [128, 1]) for k in
                ("ssq", "ssum", "mean", "var", "m2", "rstd", "nmr")}

    def ph_l0_sweep1(bufs):
        yT = bufs["yT"]; wb = bufs["wb"]; ev = bufs["ev"]
        ktok = bufs["ktok"]; vtok = bufs["vtok"]; Cb = bufs["Cb"]
        KD = bufs["KD"]; ybuf = bufs["ybuf"]; st = bufs["st"]
        dump = ybuf

        bank_evac = {0: 0, 1: 0}
        for mt in range(NMT):
            s3 = emit_ln(x_sb[:, mt * D:(mt + 1) * D], ybuf, st, dump)
            B.wait("tensor", "vx", s3)
            for k in range(16):
                bi = k % 2
                B.wait("tensor", "vx", bank_evac[bi])
                mmv = B.op("tensor", "mm", lambda e, k=k, bi=bi: e.transpose(
                    pb[bi][:, 0:128], ybuf[:, k * 128:(k + 1) * 128], ident[:, :]))
                B.wait("vector", "mm", mmv)
                bank_evac[bi] = B.op("vector", "vx",
                                     lambda e, k=k, mt=mt, bi=bi: e.tensor_copy(
                                         yT[:, k * T + mt * 128: k * T + (mt + 1) * 128],
                                         pb[bi][:, 0:128]))
            B.wait("scalar", "mm", B.cnt["mm"])

        yT_done = B.cnt["vx"]
        vM = B.op("vector", "vx", lambda e: e.memset(M_sb[:, :], 0.0))
        B.wait("tensor", "vx", yT_done)

        dests = [d_q, d_kf, None, d_u]
        wlast = {0: 0, 1: 0}
        prev_evac = ("vx", 0)
        ev_spill = 0
        tr_bank_evac = {0: 0, 1: 0}
        vtok_spill = 0
        c_spill = 0
        kd_ready = 0
        for h in range(NH):
            B.wait("scalar", "vx", max(tr_bank_evac.values()))
            s_kd1 = B.op("scalar", "sc", lambda e, h=h: e.activation(
                out=KD[:, 0:1], in_=io255[:, :], func=AF.Exp, scale=negs_bc[:, h:h + 1]))
            kd_ready = B.op("scalar", "sc", lambda e, h=h: e.activation(
                out=KD[:, 1:2], in_=io127[:, :], func=AF.Exp, scale=negs_bc[:, h:h + 1]))
            for r in range(4):
                bi = (h * 4 + r) % 2
                wsem = "w0" if bi == 0 else "w1"
                B.wait("sync", "mm", wlast[bi])
                mw = B.dma("sync", wb[bi].ap(), w4qkvu[r, h], wsem)
                B.wait("tensor", wsem, mw)
                B.wait("tensor", prev_evac[0], prev_evac[1])
                for half in range(2):
                    for kc in range(16):
                        B.op("tensor", "mm",
                             lambda e, kc=kc, half=half, bi=bi:
                             e.matmul(pb[2 + half][:, :],
                                      wb[bi][:, kc * 128:(kc + 1) * 128],
                                      yT[:, kc * T + half * 512: kc * T + (half + 1) * 512],
                                      start=(kc == 0), stop=(kc == 15)),
                             inc=(1 if kc == 15 else 0))
                wlast[bi] = B.cnt["mm"]
                g1 = B.cnt["mm"]
                eeng, esem = (("scalar", "sc") if r in (0, 1) else ("vector", "vx"))
                B.wait(eeng, "mm", g1)
                B.wait(eeng, "ev", ev_spill)
                if r in (0, 1):
                    B.op("scalar", "sc", lambda e: e.activation(
                        out=ev[:, 0:512], in_=pb[2][:, :], func=AF.Silu))
                    e2 = B.op("scalar", "sc", lambda e: e.activation(
                        out=ev[:, 512:1024], in_=pb[3][:, :], func=AF.Silu))
                else:
                    B.op("vector", "vx", lambda e: e.tensor_copy(ev[:, 0:512], pb[2][:, :]))
                    e2 = B.op("vector", "vx", lambda e: e.tensor_copy(
                        ev[:, 512:1024], pb[3][:, :]))
                prev_evac = (esem, e2)
                if r != 2:
                    B.wait("gpsimd", esem, e2)
                    ev_spill = B.dma("gpsimd", dests[r][h * 128:(h + 1) * 128, :],
                                     ev[:, 0:1024], "ev")
                if r in (1, 2):
                    tgt = ktok if r == 1 else vtok
                    B.wait("tensor", esem, e2)
                    if r == 1:
                        B.wait("tensor", "mm", 0)
                    for t in range(8):
                        tb = t % 2
                        B.wait("tensor", "vx", tr_bank_evac[tb])
                        mmv = B.op("tensor", "mm", lambda e, t=t, tb=tb: e.transpose(
                            pb[4 + tb][:, 0:128], ev[:, t * 128:(t + 1) * 128],
                            ident[:, :]))
                        B.wait("vector", "mm", mmv)
                        if r == 1:
                            B.wait("vector", "sc", kd_ready)
                            B.wait("vector", "gd", 0)
                            tr_bank_evac[tb] = B.op(
                                "vector", "vx", lambda e, t=t, tb=tb: e.tensor_scalar(
                                    tgt[:, t * 128:(t + 1) * 128], pb[4 + tb][:, 0:128],
                                    KD[:, (t % 2):(t % 2) + 1], None, op0=ALU.mult))
                        else:
                            B.wait("vector", "vt", vtok_spill)
                            tr_bank_evac[tb] = B.op(
                                "vector", "vx", lambda e, t=t, tb=tb: e.tensor_copy(
                                    tgt[:, t * 128:(t + 1) * 128], pb[4 + tb][:, 0:128]))
                    if r == 2:
                        B.wait("gpsimd", "vx", tr_bank_evac[(7) % 2])
                        vtok_spill = B.dma(
                            "gpsimd",
                            d_vt.rearrange("(t p) d -> p t d", p=128)[:, :, h * 128:(h + 1) * 128],
                            vtok.ap().rearrange("p (t d) -> p t d", d=128), "vt")
            B.wait("tensor", "vx", max(tr_bank_evac[0], tr_bank_evac[1]))
            vM_h = vM
            for b in range(NBLK):
                mmv = 0
                for sc_ in range(2):
                    t = 2 * b + sc_
                    mmv = B.op("tensor", "mm", lambda e, t=t: e.matmul(
                        pb[6][:, 0:128], ktok[:, t * 128:(t + 1) * 128],
                        vtok[:, t * 128:(t + 1) * 128],
                        start=(t % 2 == 0), stop=(t % 2 == 1)),
                        inc=(1 if t % 2 == 1 else 0))
                B.wait("vector", "mm", mmv)
                B.wait("vector", "cs", c_spill)
                B.op("vector", "vx", lambda e, b=b: e.tensor_copy(
                    Cb[:, b * 128:(b + 1) * 128], pb[6][:, 0:128]))
                vM_h = B.op("vector", "vx", lambda e, h=h: e.scalar_tensor_tensor(
                    out=M_sb[:, h * 128:(h + 1) * 128],
                    in0=M_sb[:, h * 128:(h + 1) * 128],
                    scalar=bd_bc[:, h:h + 1], in1=pb[6][:, 0:128],
                    op0=ALU.mult, op1=ALU.add))
                B.wait("tensor", "vx", vM_h)
            vM = vM_h
            B.wait("gpsimd", "vx", vM)
            c_spill = B.dma("gpsimd", d_C[h].rearrange("b p e -> p b e"),
                            Cb.ap().rearrange("p (b e) -> p b e", e=128), "cs")

        B.dma("gpsimd", ag_in0.ap().rearrange("(h p) e -> p h e", p=128),
              M_sb.ap().rearrange("p (h e) -> p h e", e=128), "gd")
        for s_ in ("gd", "ev", "vt", "cs"):
            B.wait("gpsimd", s_, B.cnt[s_])
        B.op("gpsimd", "cc", lambda e: e.collective_compute(
            "AllGather", ALU.bypass, replica_groups=[list(range(world))],
            ins=[ag_in0.ap().opt()], outs=[ag_out0.ap().opt()]))

    def ph_l0_sweep2(bufs):
        qb = bufs["qb"]; kb = bufs["kb"]; vtb = bufs["vtb"]; ub = bufs["ub"]
        Cl = bufs["Cl"]; ob = bufs["ob"]; agm = bufs["agm"]
        qp = bufs["qp"]; qd = bufs["qd"]; dct = bufs["dct"]; pbuf = bufs["pbuf"]
        djc = bufs["djc"]; tmph = bufs["tmph"]; wv = bufs["wv"]
        dump = bufs["dump"]; rmsc = bufs["rmsc"]
        r_a = bufs["r_a"]; r_b = bufs["r_b"]; r_c = bufs["r_c"]

        B.wait("gpsimd", "cc", B.cnt["cc"])
        vz = B.op("vector", "vx", lambda e: e.memset(kv_sb[:, :], 0.0))
        for j in range(world):
            B.wait("gpsimd", "vx", vz)
            agl = B.dma("gpsimd", agm.ap().rearrange("p (h e) -> p h e", e=128),
                        ag_out0.ap()[j * NH * DH:(j + 1) * NH * DH, :]
                        .rearrange("(h p) e -> p h e", p=128), "gd")
            B.wait("vector", "gd", agl)
            for h in range(NH):
                B.op("vector", "vx", lambda e, h=h: e.scalar_tensor_tensor(
                    out=djc[:, 0:1], in0=bd4m1_bc[:, h:h + 1], scalar=G[:, j:j + 1],
                    in1=ones_c[:, :], op0=ALU.mult, op1=ALU.add))
                B.op("vector", "vx", lambda e, h=h: e.tensor_scalar(
                    tmph[:, :], agm[:, h * 128:(h + 1) * 128], G[:, j:j + 1], None,
                    op0=ALU.mult))
                vz = B.op("vector", "vx", lambda e, h=h: e.scalar_tensor_tensor(
                    out=kv_sb[:, h * 128:(h + 1) * 128],
                    in0=kv_sb[:, h * 128:(h + 1) * 128],
                    scalar=djc[:, 0:1], in1=tmph[:, :], op0=ALU.mult, op1=ALU.add))

        pm_done = 0
        o_done = 0
        kvup = vz
        wuse = {0: 0, 1: 0}
        for b in range(NBLK):
            c0 = b * BLK
            B.wait("gpsimd", "mm", B.cnt["mm"])
            B.wait("gpsimd", "vx", B.cnt["vx"])
            for s_ in ("lq", "lk", "lv", "lu", "lc"):
                B.wait("gpsimd", s_, B.cnt[s_])
            ld_q = B.dma("gpsimd", qb.ap().rearrange("p (c m) -> p c m", m=BLK),
                         d_q.rearrange("(c p) m -> p c m", p=128)[:, :, c0:c0 + BLK], "lq")
            ld_k = B.dma("gpsimd", kb.ap().rearrange("p (c m) -> p c m", m=BLK),
                         d_kf.rearrange("(c p) m -> p c m", p=128)[:, :, c0:c0 + BLK], "lk")
            ld_v = B.dma("gpsimd", vtb.ap().rearrange("p (s d) -> p s d", d=D),
                         d_vt.rearrange("(s p) d -> p s d", p=128)[:, 2 * b:2 * b + 2, :], "lv")
            ld_u = B.dma("gpsimd", ub.ap().rearrange("p (c m) -> p c m", m=BLK),
                         d_u.rearrange("(c p) m -> p c m", p=128)[:, :, c0:c0 + BLK], "lu")
            ld_c = B.dma("gpsimd", Cl.ap().rearrange("p (h e) -> p h e", e=128),
                         d_C[:, b].rearrange("h p e -> p h e"), "lc")

            v_qp = 0
            o_ev = {0: 0, 1: 0}
            for h in range(NH):
                B.wait("scalar", "vx", max(v_qp, pm_done))
                s_qd = B.op("scalar", "sc", lambda e, h=h: e.activation(
                    out=qd[:, :], in_=iotaq[:, :], func=AF.Exp,
                    scale=negs_bc[:, h:h + 1]))
                s_dc = B.op("scalar", "sc", lambda e, h=h: e.activation(
                    out=dct[:, :], in_=idxd[:, :], func=AF.Exp,
                    scale=negs_bc[:, h:h + 1]))
                B.wait("vector", "lq", ld_q)
                B.wait("vector", "sc", s_qd)
                B.wait("vector", "mm", o_done)
                v_qp = B.op("vector", "vx", lambda e, h=h: e.tensor_tensor(
                    qp[:, :], qb[:, h * BLK:(h + 1) * BLK], qd[:, :], ALU.mult))
                B.wait("tensor", "lq", ld_q); B.wait("tensor", "lk", ld_k)
                B.wait("tensor", "vx", pm_done)
                mmv = 0
                for sc_ in range(2):
                    mmv = B.op("tensor", "mm", lambda e, sc_=sc_, h=h: e.matmul(
                        pb[sc_][:, 0:BLK],
                        kb[:, h * BLK + sc_ * 128: h * BLK + (sc_ + 1) * 128],
                        qb[:, h * BLK:(h + 1) * BLK], start=True, stop=True))
                B.wait("vector", "mm", mmv)
                B.wait("vector", "sc", s_dc)
                for sc_ in range(2):
                    pm_done = B.op("vector", "vx", lambda e, sc_=sc_: e.tensor_tensor(
                        pbuf[:, sc_ * BLK:(sc_ + 1) * BLK], pb[sc_][:, 0:BLK],
                        dct[:, sc_ * BLK:(sc_ + 1) * BLK], ALU.mult))
                opb = pb[2 + h % 2]
                B.wait("tensor", "vx", max(pm_done, v_qp, kvup, o_ev[h % 2]))
                B.wait("tensor", "lv", ld_v)
                B.op("tensor", "mm", lambda e, h=h, opb=opb: e.matmul(
                    opb[:, 0:BLK], kv_sb[:, h * 128:(h + 1) * 128], qp[:, :],
                    start=True, stop=False), 0)
                for sc_ in range(2):
                    o_done = B.op("tensor", "mm",
                                  lambda e, sc_=sc_, h=h, opb=opb: e.matmul(
                                      opb[:, 0:BLK],
                                      vtb[:, sc_ * D + h * 128: sc_ * D + (h + 1) * 128],
                                      pbuf[:, sc_ * BLK:(sc_ + 1) * BLK],
                                      start=False, stop=(sc_ == 1)),
                                  inc=(1 if sc_ == 1 else 0))
                B.wait("vector", "mm", o_done)
                o_ev[h % 2] = B.op("vector", "vx", lambda e, h=h, opb=opb: e.tensor_copy(
                    ob[:, h * BLK:(h + 1) * BLK], opb[:, 0:BLK]))
                B.wait("vector", "lc", ld_c)
                kvup = B.op("vector", "vx", lambda e, h=h: e.scalar_tensor_tensor(
                    out=kv_sb[:, h * 128:(h + 1) * 128],
                    in0=kv_sb[:, h * 128:(h + 1) * 128],
                    scalar=bd_bc[:, h:h + 1], in1=Cl[:, h * 128:(h + 1) * 128],
                    op0=ALU.mult, op1=ALU.add))

            B.wait("scalar", "vx", kvup)
            prev_mm = 0
            for c in range(16):
                B.wait("scalar", "mm", prev_mm)
                sqd = B.op("scalar", "sc", lambda e, c=c: e.activation(
                    out=dump[:, 0:BLK], in_=ob[:, c * BLK:(c + 1) * BLK],
                    func=AF.Square))
                B.wait("tensor", "sc", sqd)
                prev_mm = B.op("tensor", "mm", lambda e, c=c: e.matmul(
                    pb[6][0:1, 0:BLK], ones_c[:, :], dump[:, 0:BLK],
                    start=(c == 0), stop=(c == 15)))
            B.wait("vector", "mm", prev_mm)
            B.op("vector", "vx", lambda e: e.tensor_single_scalar(
                r_a[0:1, :], pb[6][0:1, 0:BLK], 1.0 / D, ALU.mult))
            v1 = B.op("vector", "vx", lambda e: e.tensor_single_scalar(
                r_a[0:1, :], r_a[0:1, :], 1e-6, ALU.add))
            B.wait("scalar", "vx", v1)
            s1 = B.op("scalar", "sc", lambda e: e.activation(
                out=r_b[0:1, :], in_=r_a[0:1, :], func=AF.Sqrt))
            B.wait("vector", "sc", s1)
            B.op("vector", "vx", lambda e: e.reciprocal(r_b[0:1, :], r_b[0:1, :]))
            B.op("vector", "vx", lambda e: e.tensor_tensor(
                r_c[0:1, :], r_b[0:1, :], r_b[0:1, :], ALU.mult))
            B.op("vector", "vx", lambda e: e.tensor_tensor(
                r_c[0:1, :], r_c[0:1, :], r_a[0:1, :], ALU.mult))
            B.op("vector", "vx", lambda e: e.tensor_scalar(
                r_c[0:1, :], r_c[0:1, :], -0.5, 1.5, op0=ALU.mult, op1=ALU.add))
            v2 = B.op("vector", "vx", lambda e: e.tensor_tensor(
                r_b[0:1, :], r_b[0:1, :], r_c[0:1, :], ALU.mult))
            B.wait("gpsimd", "vx", v2)
            B.dma("gpsimd", d_scr[0:BLK][None, :], r_b[0:1, :], "lr")
            B.wait("gpsimd", "lr", B.cnt["lr"])
            for t_ in range(2):
                g1 = B.dma("gpsimd", rmsc[:, t_:t_ + 1],
                           d_scr[t_ * 128:(t_ + 1) * 128][:, None], "lr")

            B.wait("vector", "lu", ld_u)
            g_done = 0
            for c in range(16):
                g_done = B.op("vector", "vx", lambda e, c=c: e.tensor_tensor(
                    ob[:, c * BLK:(c + 1) * BLK], ob[:, c * BLK:(c + 1) * BLK],
                    ub[:, c * BLK:(c + 1) * BLK], ALU.mult))

            for n in range(4):
                evv = 0
                for c in range(16):
                    bi = c % 2
                    wsem = "w0" if bi == 0 else "w1"
                    B.wait("sync", "mm", wuse[bi])
                    mw = B.dma("sync", wv[bi].ap(),
                               woutT[c * 128:(c + 1) * 128, n * 512:(n + 1) * 512], wsem)
                    B.wait("tensor", wsem, mw)
                    if c == 0:
                        B.wait("tensor", "vx", g_done)
                    for s_ in range(2):
                        B.op("tensor", "mm", lambda e, c=c, s_=s_, bi=bi: e.matmul(
                            pb[4 + s_][:, :],
                            ob[:, c * BLK + s_ * 128: c * BLK + (s_ + 1) * 128],
                            wv[bi].ap(), start=(c == 0), stop=(c == 15)),
                            inc=(1 if s_ == 1 else 0))
                    wuse[bi] = B.cnt["mm"]
                B.wait("vector", "mm", B.cnt["mm"])
                B.wait("vector", "lr", g1)
                for s_ in range(2):
                    mt = 2 * b + s_
                    evv = B.op("vector", "vx",
                               lambda e, s_=s_, mt=mt, n=n: e.scalar_tensor_tensor(
                                   out=x_sb[:, mt * D + n * 512: mt * D + (n + 1) * 512],
                                   in0=pb[4 + s_][:, :], scalar=rmsc[:, s_:s_ + 1],
                                   in1=x_sb[:, mt * D + n * 512: mt * D + (n + 1) * 512],
                                   op0=ALU.mult, op1=ALU.add))
                B.wait("tensor", "vx", evv)

    def ph_wstats(bufs):
        wsb = bufs["wsb"]
        dmp = bufs["dmp"]
        acol = bufs["acol"]
        acol2 = bufs["acol2"]
        acc = bufs["acc"]
        s1t = bufs["s1t"]
        s2t = bufs["s2t"]
        srow = bufs["srow"]

        W = 1024
        vz = B.vop(lambda e: e.memset(acc[:, :], 0.0))
        suse = {0: ("sc", 0), 1: ("vx", 0)}

        def stream(dram, nr, ncol, per_tile):
            idx = 0
            for r in range(nr):
                for cb in range(ncol // W):
                    bi = idx % 2
                    wsem = "w0" if bi == 0 else "w1"
                    B.wait("sync", suse[bi][0], suse[bi][1])
                    mw = B.dma("sync", wsb[bi].ap(),
                               dram[r * 128:(r + 1) * 128, cb * W:(cb + 1) * W], wsem)
                    B.wait("scalar" if bi == 0 else "vector", wsem, mw)
                    per_tile(r, cb, bi)
                    suse[bi] = (("sc", B.cnt["sc"]) if bi == 0
                                else ("vx", B.cnt["vx"]))
                    idx += 1

        def glob_tile(col):
            def f(r, cb, bi):
                if bi == 0:
                    B.wait("scalar", "vx", B.cnt["vx"])
                    s = B.sop(lambda e: e.activation(
                        out=dmp[:, :], in_=wsb[bi][:, :], func=AF.Abs,
                        accum_out=acol[:, 0:1]))
                    B.wait("vector", "sc", s)
                    B.vop(lambda e: e.tensor_tensor(
                        acc[:, col:col + 1], acc[:, col:col + 1], acol[:, 0:1],
                        ALU.add))
                else:
                    B.vop(lambda e: e.reduce_sum(
                        acol2[:, 0:1], wsb[bi][:, :], axis=AX.X,
                        apply_absolute_value=True))
                    B.vop(lambda e: e.tensor_tensor(
                        acc[:, col:col + 1], acc[:, col:col + 1], acol2[:, 0:1],
                        ALU.add))
            return f
        stream(wfcT, 16, H0, glob_tile(0))
        stream(wprojT, 40, D, glob_tile(1))

        def row_tile(tgt, nchunks, inv):
            def f(r, cb, bi):
                if bi == 0:
                    B.wait("scalar", "vx", B.cnt["vx"])
                    s = B.sop(lambda e: e.activation(
                        out=dmp[:, :], in_=wsb[bi][:, :], func=AF.Abs,
                        accum_out=acol[:, 0:1]))
                    B.wait("vector", "sc", s)
                else:
                    s = None
                    B.vop(lambda e: e.reduce_sum(
                        acol[:, 0:1], wsb[bi][:, :], axis=AX.X,
                        apply_absolute_value=True))
                if cb == 0:
                    B.vop(lambda e, r=r: e.tensor_copy(
                        tgt[:, r:r + 1], acol[:, 0:1]))
                else:
                    B.vop(lambda e, r=r: e.tensor_tensor(
                        tgt[:, r:r + 1], tgt[:, r:r + 1], acol[:, 0:1], ALU.add))
                if cb == nchunks - 1:
                    B.vop(lambda e, r=r: e.tensor_scalar(
                        tgt[:, r:r + 1], tgt[:, r:r + 1], inv, 1e-5,
                        op0=ALU.mult, op1=ALU.max))
            return f
        stream(w1n, 64, D, row_tile(s1t, D // W, 1.0 / D))
        stream(w2n, 16, H1, row_tile(s2t, H1 // W, 1.0 / H1))

        vlast = B.cnt["vx"]
        B.wait("tensor", "vx", vlast)
        B.op("tensor", "mm", lambda e: e.matmul(
            pb[0][0:1, 0:1], acc[:, 0:1], ones_c[:, :], start=True, stop=True))
        m1 = B.op("tensor", "mm", lambda e: e.matmul(
            pb[0][0:1, 1:2], acc[:, 1:2], ones_c[:, :], start=True, stop=True))
        B.wait("vector", "mm", m1)
        B.vop(lambda e: e.tensor_scalar(
            srow[0:1, 0:1], pb[0][0:1, 0:1], 1.0 / (H0 * D), 1e-5,
            op0=ALU.mult, op1=ALU.max))
        B.vop(lambda e: e.tensor_scalar(
            srow[0:1, 1:2], pb[0][0:1, 1:2], 1.0 / (H0 * D), 1e-5,
            op0=ALU.mult, op1=ALU.max))
        B.vop(lambda e: e.tensor_single_scalar(
            srow[0:1, 2:3], srow[0:1, 0:1], 0.7, ALU.mult))
        B.vop(lambda e: e.tensor_single_scalar(
            srow[0:1, 3:4], srow[0:1, 1:2], 0.7, ALU.mult))
        B.vop(lambda e: e.tensor_single_scalar(
            srow[0:1, 4:5], srow[0:1, 2:3], -1.0, ALU.mult))
        v2 = B.vop(lambda e: e.tensor_single_scalar(
            srow[0:1, 5:6], srow[0:1, 3:4], -1.0, ALU.mult))
        B.wait("tensor", "vx", v2)
        m2 = B.op("tensor", "mm", lambda e: e.matmul(
            pb[1][:, 0:6], ones_r[:, :], srow[0:1, 0:6], start=True, stop=True))
        B.wait("vector", "mm", m2)
        B.vop(lambda e: e.tensor_copy(am_fc[:, 0:1], pb[1][:, 0:1]))
        B.vop(lambda e: e.tensor_copy(am_pr[:, 0:1], pb[1][:, 1:2]))
        B.vop(lambda e: e.tensor_copy(thr_fc[:, 0:1], pb[1][:, 2:3]))
        B.vop(lambda e: e.tensor_copy(thr_pr[:, 0:1], pb[1][:, 3:4]))
        B.vop(lambda e: e.tensor_copy(nthr_fc[:, 0:1], pb[1][:, 4:5]))
        v3 = B.vop(lambda e: e.tensor_copy(nthr_pr[:, 0:1], pb[1][:, 5:6]))

        B.wait("tensor", "vx", v3)
        m3 = B.op("tensor", "mm", lambda e: e.transpose(
            pb[2][0:64, 0:128], s1t[:, 0:64], ident[:, :]))
        m4 = B.op("tensor", "mm", lambda e: e.transpose(
            pb[3][0:16, 0:128], s2t[:, 0:16], ident[:, :]))
        B.wait("vector", "mm", m4)
        B.vop(lambda e: e.tensor_copy(dmp[0:64, 0:128], pb[2][0:64, 0:128]))
        v4 = B.vop(lambda e: e.tensor_copy(dmp[0:16, 128:256], pb[3][0:16, 0:128]))
        B.wait("gpsimd", "vx", v4)
        B.dma("gpsimd", d_s1.rearrange("(r c) -> r c", c=128), dmp[0:64, 0:128], "gd")
        B.dma("gpsimd", d_s2.rearrange("(r c) -> r c", c=128), dmp[0:16, 128:256], "gd")
        B.wait("gpsimd", "gd", B.cnt["gd"])

    def ph_tern(bufs):
        wsb = bufs["wsb"]; tn = bufs["tn"]; qo = bufs["qo"]; qo2 = bufs["qo2"]
        thp = bufs["thp"]; thn = bufs["thn"]; s1r = bufs["s1r"]; tn2 = bufs["tn2"]
        W = 1024
        suse = {0: ("vx", 0), 1: ("gx", 0)}
        qfree = {0: 0, 1: 0}
        sgn = [0]

        def tern_global(dramT, dst, nr, ncol, thr, nthr):
            idx = 0
            for r in range(nr):
                for cb in range(ncol // W):
                    bi = idx % 2
                    wsem = "w0" if bi == 0 else "w1"
                    B.wait("sync", suse[bi][0], suse[bi][1])
                    mw = B.dma("sync", wsb[bi].ap(),
                               dramT[r * 128:(r + 1) * 128, cb * W:(cb + 1) * W], wsem)
                    qi = idx % 2
                    qsem = "ev" if qi == 0 else "vt"
                    if qi == 0:
                        B.wait("vector", qsem, qfree[qi])
                        B.wait("vector", wsem, mw)
                        B.vop(lambda e, bi=bi: e.tensor_scalar(
                            tn[:, :], wsb[bi][:, :], nthr[:, 0:1], None,
                            op0=ALU.is_lt))
                        vq = B.vop(lambda e, bi=bi, qi=qi:
                                   e.scalar_tensor_tensor(
                                       out=qo[qi][:, :], in0=wsb[bi][:, :],
                                       scalar=thr[:, 0:1], in1=tn[:, :],
                                       op0=ALU.is_gt, op1=ALU.subtract))
                        esem = "vx"
                    else:
                        B.wait("scalar", wsem, mw)
                        B.wait("scalar", "vx", sgn[0])
                        B.sop(lambda e, bi=bi: e.activation(
                            out=tn2[:, :], in_=wsb[bi][:, :], func=AF.Sign,
                            bias=nthr[:, 0:1]))
                        s2 = B.sop(lambda e, bi=bi: e.activation(
                            out=qo2[:, :], in_=wsb[bi][:, :], func=AF.Sign,
                            bias=thr[:, 0:1]))
                        B.wait("vector", "sc", s2)
                        B.wait("vector", qsem, qfree[qi])
                        vq = B.vop(lambda e, qi=qi: e.scalar_tensor_tensor(
                            out=qo[qi][:, :], in0=tn2[:, :], scalar=1.0,
                            in1=qo2[:, :], op0=ALU.mult, op1=ALU.add))
                        vq2 = B.vop(lambda e, qi=qi: e.tensor_single_scalar(
                            qo[qi][:, :], qo[qi][:, :], 0.5, ALU.mult))
                        vq = vq2
                        sgn[0] = vq
                        esem = "vx"
                    suse[bi] = (esem if qi == 0 else "sc",
                                vq if qi == 0 else B.cnt["sc"])
                    B.wait("gpsimd", esem, vq)
                    qfree[qi] = B.dma(
                        "gpsimd", dst[r * 128:(r + 1) * 128, cb * W:(cb + 1) * W],
                        qo[qi].ap(), qsem)
                    idx += 1
        tern_global(wfcT, d_tq1l0, 16, H0, thr_fc, nthr_fc)
        tern_global(wprojT, d_tq2l0, 40, D, thr_pr, nthr_pr)

        def tern_rows(dramT, dst, scl_dram, nr, ncol):
            idx = 0
            for cb in range(ncol // W):
                ml = B.dma("gpsimd", s1r[0:1, 0:W],
                           scl_dram[cb * W:(cb + 1) * W][None, :], "lr")
                B.wait("tensor", "lr", ml)
                B.wait("tensor", suse[0][0], suse[0][1])
                B.wait("tensor", suse[1][0], suse[1][1])
                mm1 = B.op("tensor", "mm", lambda e: e.matmul(
                    pb[4][:, 0:512], ones_r[:, :], s1r[0:1, 0:512],
                    start=True, stop=True))
                mm2 = B.op("tensor", "mm", lambda e: e.matmul(
                    pb[5][:, 0:512], ones_r[:, :], s1r[0:1, 512:1024],
                    start=True, stop=True))
                B.wait("vector", "mm", mm2)
                B.vop(lambda e: e.tensor_single_scalar(
                    thp[:, 0:512], pb[4][:, 0:512], 0.5, ALU.mult))
                B.vop(lambda e: e.tensor_single_scalar(
                    thp[:, 512:1024], pb[5][:, 0:512], 0.5, ALU.mult))
                B.vop(lambda e: e.tensor_single_scalar(
                    thn[:, 0:512], pb[4][:, 0:512], -0.5, ALU.mult))
                vb = B.vop(lambda e: e.tensor_single_scalar(
                    thn[:, 512:1024], pb[5][:, 0:512], -0.5, ALU.mult))
                for r in range(nr):
                    bi = idx % 2
                    wsem = "w0" if bi == 0 else "w1"
                    B.wait("sync", suse[bi][0], suse[bi][1])
                    mw = B.dma("sync", wsb[bi].ap(),
                               dramT[r * 128:(r + 1) * 128, cb * W:(cb + 1) * W], wsem)
                    qi = idx % 2
                    qsem = "ev" if qi == 0 else "vt"
                    B.wait("vector", wsem, mw)
                    B.wait("vector", qsem, qfree[qi])
                    B.vop(lambda e, bi=bi: e.tensor_tensor(
                        tn[:, :], wsb[bi][:, :], thn[:, :], ALU.is_lt))
                    B.vop(lambda e, bi=bi: e.tensor_tensor(
                        qo2[:, :], wsb[bi][:, :], thp[:, :], ALU.is_gt))
                    vq = B.vop(lambda e, qi=qi: e.tensor_tensor(
                        qo[qi][:, :], qo2[:, :], tn[:, :], ALU.subtract))
                    suse[bi] = ("vx", vq)
                    B.wait("gpsimd", "vx", vq)
                    qfree[qi] = B.dma(
                        "gpsimd", dst[r * 128:(r + 1) * 128, cb * W:(cb + 1) * W],
                        qo[qi].ap(), qsem)
                    idx += 1
        tern_rows(w1T, d_tq1l1, d_s1, 16, H1)
        tern_rows(w2T, d_tq2l1, d_s2, 64, D)

    def ph_l0_ffn(bufs):
        ybuf = bufs["ybuf"]; st = bufs["st"]
        y2h = bufs["y2h"]; y2l = bufs["y2l"]
        wsb = bufs["wsb"]
        rb = bufs["rb"]; sqf = bufs["sqf"]
        sqh = bufs["sqh"]; sql = bufs["sql"]
        dump = ybuf

        wuse = {0: 0, 1: 0, 2: 0, 3: 0}
        prev_ln = 0
        for mc in range(4):
            bank_evac = {0: B.cnt["vx"], 1: B.cnt["vx"]}
            for mt_rel in range(2):
                mt = 2 * mc + mt_rel
                B.wait("scalar", "mm", B.cnt["mm"])
                B.wait("scalar", "vx", prev_ln)
                s3 = emit_ln(x_sb[:, mt * D:(mt + 1) * D], ybuf, st, dump)
                prev_ln = s3
                B.wait("tensor", "vx", s3)
                for k in range(16):
                    bi = k % 2
                    B.wait("tensor", "vx", bank_evac[bi])
                    mmv = B.op("tensor", "mm", lambda e, k=k, bi=bi: e.transpose(
                        pb[bi][:, 0:128], ybuf[:, k * 128:(k + 1) * 128], ident[:, :]))
                    B.wait("vector", "mm", mmv)
                    col = k * BLK + mt_rel * 128
                    B.vop(lambda e, bi=bi, col=col: e.tensor_copy(
                        y2h[:, col:col + 128], pb[bi][:, 0:128]))
                    bank_evac[bi] = B.vop(lambda e, bi=bi, col=col: e.tensor_tensor(
                        y2l[:, col:col + 128], pb[bi][:, 0:128], y2h[:, col:col + 128],
                        ALU.subtract))
            yT_done = B.cnt["vx"]
            B.wait("tensor", "vx", yT_done)

            set_done = {0: B.cnt["vx"], 1: B.cnt["vx"]}
            prev_sq = 0
            for cb in range(5):
                bset = [pb[4 * (cb % 2) + i] for i in range(4)]
                B.wait("tensor", "vx", set_done[cb % 2])
                for kr in range(16):
                    bi = (cb * 16 + kr) % 4
                    wsem = B.WSEMS[bi]
                    B.wait("sync", "mm", wuse[bi])
                    mw = B.dma("sync", wsb[bi][:, 0:1024],
                               d_tq1l0[kr * 128:(kr + 1) * 128,
                                       cb * 1024:(cb + 1) * 1024], wsem)
                    B.wait("tensor", wsem, mw)
                    for j in range(8):
                        bank = bset[j // 2]
                        half = (j % 2) * BLK
                        for hl, ysrc in ((0, y2h), (1, y2l)):
                            B.op("tensor", "mm",
                                 lambda e, kr=kr, j=j, hl=hl, ysrc=ysrc,
                                 bank=bank, half=half, bi=bi: e.matmul(
                                     bank[:, half:half + BLK],
                                     wsb[bi][:, j * 128:(j + 1) * 128],
                                     ysrc[:, kr * BLK:(kr + 1) * BLK],
                                     start=(kr == 0 and j % 2 == 0 and hl == 0),
                                     stop=(kr == 15 and j % 2 == 1 and hl == 1)),
                                 inc=(1 if (j == 7 and hl == 1) else 0))
                    wuse[bi] = B.cnt["mm"]
                gmm = B.cnt["mm"]
                for j in range(8):
                    bank = bset[j // 2]
                    half = (j % 2) * BLK
                    hc = cb * 8 + j
                    B.wait("vector", "mm", gmm)
                    B.wait("vector", "sc", prev_sq)
                    B.vop(lambda e, bank=bank, half=half: e.tensor_relu(
                        rb[:, 0:BLK], bank[:, half:half + BLK]))
                    vr = B.cnt["vx"]
                    B.wait("scalar", "vx", vr)
                    prev_sq = B.sop(lambda e: e.activation(
                        out=sqf[:, 0:BLK], in_=rb[:, 0:BLK], func=AF.Square,
                        scale=am_fc[:, 0:1]))
                    B.wait("vector", "sc", prev_sq)
                    col = hc * BLK
                    B.vop(lambda e, col=col: e.tensor_copy(
                        sqh[:, col:col + BLK], sqf[:, 0:BLK]))
                    B.vop(lambda e, col=col: e.tensor_tensor(
                        sql[:, col:col + BLK], sqf[:, 0:BLK], sqh[:, col:col + BLK],
                        ALU.subtract))
                set_done[cb % 2] = B.cnt["vx"]
            sq_done = B.cnt["vx"]

            for hc in range(40):
                bi = hc % 4
                wsem = B.WSEMS[bi]
                B.wait("sync", "mm", wuse[bi])
                mw = B.dma("sync", wsb[bi].ap(),
                           d_tq2l0[hc * 128:(hc + 1) * 128, :], wsem)
                B.wait("tensor", wsem, mw)
                if hc == 0:
                    B.wait("tensor", "vx", sq_done)
                for s_ in range(2):
                    for n in range(4):
                        for hl, sqsrc in ((0, sqh), (1, sql)):
                            B.op("tensor", "mm",
                                 lambda e, hc=hc, s_=s_, n=n, sqsrc=sqsrc, bi=bi:
                                 e.matmul(
                                     pb[4 * s_ + n][:, :],
                                     sqsrc[:, hc * BLK + s_ * 128: hc * BLK + (s_ + 1) * 128],
                                     wsb[bi][:, n * 512:(n + 1) * 512],
                                     start=(hc == 0 and hl == 0),
                                     stop=(hc == 39 and hl == 1)),
                                 inc=(1 if (s_ == 1 and n == 3 and hl == 1) else 0))
                wuse[bi] = B.cnt["mm"]
            B.wait("vector", "mm", B.cnt["mm"])
            evv = 0
            for s_ in range(2):
                for n in range(4):
                    mt = 2 * mc + s_
                    evv = B.vop(lambda e, s_=s_, mt=mt, n=n: e.scalar_tensor_tensor(
                        out=x_sb[:, mt * D + n * 512: mt * D + (n + 1) * 512],
                        in0=pb[4 * s_ + n][:, :], scalar=am_pr[:, 0:1],
                        in1=x_sb[:, mt * D + n * 512: mt * D + (n + 1) * 512],
                        op0=ALU.mult, op1=ALU.add))
            B.wait("tensor", "vx", evv)

    def ph_l1_attn_a(bufs):
        y3T = bufs["y3T"]; wb = bufs["wb"]; ev = bufs["ev"]; vtok = bufs["vtok"]
        ybuf = bufs["ybuf"]; st = bufs["st"]
        dump = ybuf

        bank_evac = {0: B.cnt["vx"], 1: B.cnt["vx"]}
        prev_ln = 0
        for mt in range(NMT):
            B.wait("scalar", "mm", B.cnt["mm"])
            B.wait("scalar", "vx", prev_ln)
            s3 = emit_ln(x_sb[:, mt * D:(mt + 1) * D], ybuf, st, dump)
            prev_ln = s3
            B.wait("tensor", "vx", s3)
            for k in range(16):
                bi = k % 2
                B.wait("tensor", "vx", bank_evac[bi])
                mmv = B.op("tensor", "mm", lambda e, k=k, bi=bi: e.transpose(
                    pb[bi][:, 0:128], ybuf[:, k * 128:(k + 1) * 128], ident[:, :]))
                B.wait("vector", "mm", mmv)
                bank_evac[bi] = B.vop(lambda e, k=k, mt=mt, bi=bi: e.tensor_copy(
                    y3T[:, k * T + mt * 128: k * T + (mt + 1) * 128], pb[bi][:, 0:128]))
        yT_done = B.cnt["vx"]
        B.wait("tensor", "vx", yT_done)

        wlast = {0: 0, 1: 0}
        prev_evac = ("vx", 0)
        ev_spill = {"q": 0, "k": 0}
        vt_spill = 0
        tr_bank_evac = {0: B.cnt["vx"], 1: B.cnt["vx"]}
        regs = [(w3q, "q"), (w3k, "k"), (w3v, "v")]
        for h in range(NH):
            for ri, (wT, r) in enumerate(regs):
                bi = (h * 3 + ri) % 2
                wsem = "w0" if bi == 0 else "w1"
                B.wait("sync", "mm", wlast[bi])
                mw = B.dma("sync", wb[bi].ap(), wT[h], wsem)
                B.wait("tensor", wsem, mw)
                B.wait("tensor", prev_evac[0], prev_evac[1])
                for half in range(2):
                    for kc in range(16):
                        B.op("tensor", "mm",
                             lambda e, kc=kc, half=half, bi=bi:
                             e.matmul(pb[2 + half][:, :],
                                      wb[bi][:, kc * 128:(kc + 1) * 128],
                                      y3T[:, kc * T + half * 512: kc * T + (half + 1) * 512],
                                      start=(kc == 0), stop=(kc == 15)),
                             inc=(1 if kc == 15 else 0))
                wlast[bi] = B.cnt["mm"]
                g1 = B.cnt["mm"]
                B.wait("vector", "mm", g1)
                B.wait("vector", "lq", ev_spill["q"])
                B.wait("vector", "ev", ev_spill["k"])
                if r != "v":
                    B.vop(lambda e: e.tensor_copy(ev[:, 0:512], pb[2][:, :]))
                    e2 = B.vop(lambda e: e.tensor_copy(ev[:, 512:1024], pb[3][:, :]))
                    prev_evac = ("vx", e2)
                    B.wait("gpsimd", "vx", e2)
                    if r == "q":
                        ev_spill["q"] = B.dma("gpsimd", d_q[h * 128:(h + 1) * 128, :],
                                              ev[:, 0:1024], "lq")
                    else:
                        ev_spill["k"] = B.dma("gpsimd",
                                              ag_in_k.ap()[h * 128:(h + 1) * 128, :],
                                              ev[:, 0:1024], "ev")
                else:
                    B.vop(lambda e: e.tensor_copy(ev[:, 0:512], pb[2][:, :]))
                    e2 = B.vop(lambda e: e.tensor_copy(ev[:, 512:1024], pb[3][:, :]))
                    prev_evac = ("vx", e2)
                    B.wait("tensor", "vx", e2)
                    for t in range(8):
                        tb_ = t % 2
                        B.wait("tensor", "vx", tr_bank_evac[tb_])
                        mmv = B.op("tensor", "mm", lambda e, t=t, tb_=tb_: e.transpose(
                            pb[4 + tb_][:, 0:128], ev[:, t * 128:(t + 1) * 128],
                            ident[:, :]))
                        B.wait("vector", "mm", mmv)
                        B.wait("vector", "vt", vt_spill)
                        tr_bank_evac[tb_] = B.vop(
                            lambda e, t=t, tb_=tb_: e.tensor_copy(
                                vtok[:, t * 128:(t + 1) * 128], pb[4 + tb_][:, 0:128]))
                    B.wait("gpsimd", "vx", max(tr_bank_evac.values()))
                    vt_spill = B.dma(
                        "gpsimd",
                        ag_in_v.ap().rearrange("(t p) d -> p t d", p=128)[:, :, h * 128:(h + 1) * 128],
                        vtok.ap().rearrange("p (t d) -> p t d", d=128), "vt")

        for s_ in ("lq", "ev", "vt"):
            B.wait("gpsimd", s_, B.cnt[s_])
        B.op("gpsimd", "cc", lambda e: e.collective_compute(
            "AllGather", ALU.bypass, replica_groups=[list(range(world))],
            ins=[ag_in_k.ap().opt()], outs=[ag_out_k.ap().opt()]))
        B.op("gpsimd", "cc", lambda e: e.collective_compute(
            "AllGather", ALU.bypass, replica_groups=[list(range(world))],
            ins=[ag_in_v.ap().opt()], outs=[ag_out_v.ap().opt()]))
        B.wait("gpsimd", "cc", B.cnt["cc"])

        kh = ag_out_k.ap().tensor
        vh = ag_out_v.ap().tensor

        def mk(off_fn, dims):
            return lambda: bass.AP(off_fn[0], nc.snap(off_fn[1]()), dims)

        def prev_off():
            pid = B.eng.partition_id()
            return (pid + world - 1) % world

        def self_off():
            return B.eng.partition_id()

        B.dma("gpsimd", kwin[:, 0:T],
              mk((kh, lambda: prev_off() * (D * T)), [[T, D], [1, T]]), "gd")
        B.dma("gpsimd", kwin[:, T:2 * T],
              mk((kh, lambda: self_off() * (D * T)), [[T, D], [1, T]]), "gd")
        B.dma("gpsimd", vwin[0:T, :],
              mk((vh, lambda: prev_off() * (T * D)), [[D, T], [1, D]]), "gd")
        B.dma("gpsimd", vwin[T:2 * T, :],
              mk((vh, lambda: self_off() * (T * D)), [[D, T], [1, D]]), "gd")
        B.wait("gpsimd", "gd", B.cnt["gd"])

    def ph_l1_attn_b(bufs):
        qmb = bufs["qmb"]; osb = bufs["osb"]; masks = bufs["masks"]
        ktile = bufs["ktile"]; vtile = bufs["vtile"]
        tbv = bufs["tbv"]; pbm = bufs["pbm"]
        lsum = bufs["lsum"]
        wvb = bufs["wvb"]
        SC = 1.0 / float(np.sqrt(128.0))

        def lslot(h):
            if h < 12:
                return lsum, 32 * (h // 4), (h % 4) * 512
            if h < 15:
                return wvb[0], 32 * (h - 12), 0
            return wvb[1], 0, 0

        wuse = {0: 0, 1: 0}
        for mb in range(2):
            B.wait("gpsimd", "vx", B.cnt["vx"])
            B.wait("gpsimd", "mm", B.cnt["mm"])
            for s_ in ("lr", "lq", "lk", "lv", "lu", "lc"):
                B.wait("gpsimd", s_, B.cnt[s_])
            ml = B.dma("gpsimd", masks.ap().rearrange("p (k m) -> p k m", m=512),
                       lmask[mb].rearrange("k p m -> p k m"), "lr")
            lq = B.dma("gpsimd", qmb.ap().rearrange("p (c m) -> p c m", m=512),
                       d_q.rearrange("(c p) m -> p c m", p=128)[:, :, mb * 512:(mb + 1) * 512],
                       "lq")
            vz = B.vop(lambda e: e.memset(osb[:, :], 0.0))

            tanh_done = {0: 0, 1: 0, 2: 0}
            av_evac = {0: ("vx", vz), 1: ("vx", vz)}
            lrow_acc = {0: 0, 1: 0}
            pbm_freed = {0: 0, 1: 0, 2: 0}
            vmask = {0: 0, 1: 0, 2: 0}
            kv_ld = {}
            st_mm_done = {}

            def issue_kv(st_):
                par = st_ % 2
                ks, vs = ("lk", "lv") if par == 0 else ("lu", "lc")
                B.wait("gpsimd", "mm", st_mm_done.get(st_ - 2, 0))
                lk_ = B.dma("gpsimd",
                            ktile[par].ap().rearrange("p (c s) -> p c s", s=128),
                            kwin.rearrange("(c p) t -> p c t", p=128)
                            [:, :, mb * 512 + st_ * 128: mb * 512 + (st_ + 1) * 128], ks)
                lv_ = B.dma("gpsimd", vtile[par].ap(),
                            vwin[mb * 512 + st_ * 128: mb * 512 + (st_ + 1) * 128, :], vs)
                kv_ld[st_] = (ks, lk_, vs, lv_)

            score_mm = {}

            def emit_score(st_, h):
                par = st_ % 2
                ktl = ktile[par]
                ks, lk, vs, lv = kv_ld[st_]
                sb_ = h % 3
                B.wait("tensor", ks, lk)
                B.wait("tensor", "lq", lq)
                B.wait("tensor", "sc", tanh_done[sb_])
                mmv = B.op("tensor", "mm", lambda e, h=h, sb_=sb_, ktl=ktl: e.matmul(
                    pb[sb_][:, :], ktl[:, h * 128:(h + 1) * 128],
                    qmb[:, h * 512:(h + 1) * 512], start=True, stop=True))
                score_mm[(st_, h)] = mmv

            def emit_rest(st_, h):
                par = st_ % 2
                vtl = vtile[par]
                ks, lk, vs, lv = kv_ld[st_]
                sb_ = h % 3
                mmv = score_mm.pop((st_, h))
                B.wait("scalar", "mm", max(mmv, pbm_freed[sb_]))
                B.wait("scalar", "vx", vmask[sb_])
                B.sop(lambda e, sb_=sb_: e.activation(
                    out=tbv[sb_][:, :], in_=pb[sb_][:, :], func=AF.Tanh,
                    scale=SC / 30.0))
                tanh_done[sb_] = B.cnt["sc"]
                s_e = B.sop(lambda e, sb_=sb_: e.activation(
                    out=pbm[sb_][:, :], in_=tbv[sb_][:, :], func=AF.Exp,
                    scale=30.0, bias=neg30[:, 0:1]))
                B.wait("vector", "sc", s_e)
                B.wait("vector", "lr", ml)
                v_m = B.vop(lambda e, st_=st_, sb_=sb_: e.tensor_tensor(
                    pbm[sb_][:, :], pbm[sb_][:, :],
                    masks[:, st_ * 512:(st_ + 1) * 512], ALU.mult))
                vmask[sb_] = v_m
                ab = 3 + (h % 2)
                B.wait("tensor", "vx", v_m)
                B.wait("tensor", av_evac[h % 2][0], av_evac[h % 2][1])
                B.wait("tensor", vs, lv)
                av = B.op("tensor", "mm",
                          lambda e, h=h, ab=ab, sb_=sb_, vtl=vtl: e.matmul(
                              pb[ab][:, :], vtl[:, h * 128:(h + 1) * 128],
                              pbm[sb_][:, :], start=True, stop=True))
                lb_ = 5 + (h % 2)
                B.wait("tensor", "vx", lrow_acc[h % 2])
                B.op("tensor", "mm", lambda e, sb_=sb_, lb_=lb_: e.matmul(
                    pb[lb_][0:1, 0:512], ones_c[:, :], pbm[sb_][:, :],
                    start=True, stop=True))
                pbm_freed[sb_] = B.cnt["mm"]
                B.wait("vector", "mm", pbm_freed[sb_])
                lbuf, lpr, lcc = lslot(h)
                if st_ == 0:
                    lrow_acc[h % 2] = B.vop(lambda e, lbuf=lbuf, lpr=lpr, lcc=lcc,
                                            lb_=lb_: e.tensor_copy(
                                                lbuf[lpr:lpr + 1, lcc:lcc + 512],
                                                pb[lb_][0:1, 0:512]))
                else:
                    lrow_acc[h % 2] = B.vop(lambda e, lbuf=lbuf, lpr=lpr, lcc=lcc,
                                            lb_=lb_: e.tensor_tensor(
                                                lbuf[lpr:lpr + 1, lcc:lcc + 512],
                                                lbuf[lpr:lpr + 1, lcc:lcc + 512],
                                                pb[lb_][0:1, 0:512], ALU.add))
                B.wait("vector", "mm", av)
                av_evac[h % 2] = ("vx", B.vop(
                    lambda e, h=h, ab=ab: e.tensor_tensor(
                        osb[:, h * 512:(h + 1) * 512],
                        osb[:, h * 512:(h + 1) * 512], pb[ab][:, :], ALU.add)))

            issue_kv(0)
            issue_kv(1)
            for i in range(193):
                if i < 192:
                    emit_score(*divmod(i, 16))
                if i >= 1:
                    pst, ph_ = divmod(i - 1, 16)
                    emit_rest(pst, ph_)
                    if ph_ == 15:
                        st_mm_done[pst] = B.cnt["mm"]
                        if pst + 2 < 12:
                            issue_kv(pst + 2)

            for k_ in range(3):
                B.vop(lambda e, k_=k_: e.reciprocal(
                    lsum[32 * k_:32 * k_ + 1, :], lsum[32 * k_:32 * k_ + 1, :]))
            for k_ in range(3):
                B.vop(lambda e, k_=k_: e.reciprocal(
                    wvb[0][32 * k_:32 * k_ + 1, 0:512],
                    wvb[0][32 * k_:32 * k_ + 1, 0:512]))
            B.vop(lambda e: e.reciprocal(wvb[1][0:1, 0:512], wvb[1][0:1, 0:512]))
            vrec0 = B.cnt["vx"]
            nbuse = {6: 0, 7: 0}
            for h in range(NH):
                lbuf, lpr, lcc = lslot(h)
                nb = 6 + (h % 2)
                B.wait("tensor", "vx", max(vrec0, nbuse[nb]))
                m1 = B.op("tensor", "mm", lambda e, lbuf=lbuf, lpr=lpr, lcc=lcc,
                          nb=nb: e.matmul(
                    pb[nb][:, 0:512], ones_bc[lpr:lpr + 1, 0:128],
                    lbuf[lpr:lpr + 1, lcc:lcc + 512], start=True, stop=True))
                B.wait("vector", "mm", m1)
                nbuse[nb] = B.vop(lambda e, h=h, nb=nb: e.tensor_tensor(
                    osb[:, h * 512:(h + 1) * 512],
                    osb[:, h * 512:(h + 1) * 512], pb[nb][:, :], ALU.mult))
            wuse[0] = wuse[1] = B.cnt["mm"]
            on_done = B.cnt["vx"]

            for n in range(4):
                for h in range(NH):
                    bi = h % 2
                    wsem = "w0" if bi == 0 else "w1"
                    B.wait("sync", "mm", wuse[bi])
                    mw = B.dma("sync", wvb[bi].ap(),
                               woT[h * 128:(h + 1) * 128, n * 512:(n + 1) * 512], wsem)
                    B.wait("tensor", wsem, mw)
                    if h == 0:
                        B.wait("tensor", "vx", on_done)
                    for ms in range(4):
                        B.op("tensor", "mm", lambda e, h=h, ms=ms, bi=bi: e.matmul(
                            pb[ms][:, :],
                            osb[:, h * 512 + ms * 128: h * 512 + (ms + 1) * 128],
                            wvb[bi].ap(), start=(h == 0), stop=(h == NH - 1)),
                            inc=(1 if ms == 3 else 0))
                    wuse[bi] = B.cnt["mm"]
                B.wait("vector", "mm", B.cnt["mm"])
                evv = 0
                for ms in range(4):
                    mt = mb * 4 + ms
                    evv = B.vop(lambda e, ms=ms, mt=mt, n=n: e.tensor_tensor(
                        x_sb[:, mt * D + n * 512: mt * D + (n + 1) * 512],
                        x_sb[:, mt * D + n * 512: mt * D + (n + 1) * 512],
                        pb[ms][:, :], ALU.add))
                B.wait("tensor", "vx", evv)

    def ph_l1_ffn(bufs):
        ybuf = bufs["gbuf"][:, D:2 * D]
        st = bufs["st"]
        sc1 = bufs["sc1"]; sc2 = bufs["sc2"]
        s1r = bufs["s1r"]
        zb = bufs["zb"]; fb = bufs["fb"]; db = bufs["db"]
        ib = bufs["ib"]
        xif = bufs["xif"]
        xif2 = bufs["gbuf"]
        RW = 512
        xiT = bufs["xiT"]
        gbuf = bufs["gbuf"]
        k2T = bufs["k2T"]
        k2Tb = bufs["k2Tb"]
        wsb = bufs["wsb"]
        mxc = bufs["mxc"]
        W = 1024

        def emit_round(src_ap, n, s_col, out_ap, out_cols):
            for cc in range(n // RW):
                sl = slice(cc * RW, (cc + 1) * RW)
                B.vop(lambda e, sl=sl: e.tensor_scalar(
                    zb[:, :], src_ap[:, sl], mxc[:, s_col:s_col + 1], None,
                    op0=ALU.mult))
                B.vop(lambda e: e.tensor_copy(ib[:, :], zb[:, :]))
                B.vop(lambda e: e.tensor_copy(fb[:, :], ib[:, :]))
                B.vop(lambda e: e.tensor_sub(db[:, :], zb[:, :], fb[:, :]))
                B.vop(lambda e: e.scalar_tensor_tensor(
                    out=zb[:, :], in0=db[:, :], scalar=0.5, in1=fb[:, :],
                    op0=ALU.is_gt, op1=ALU.add))
                B.vop(lambda e: e.tensor_single_scalar(
                    db[:, :], db[:, :], -0.5, ALU.is_lt))
                B.vop(lambda e, sl=sl, cc=cc: e.tensor_tensor(
                    out_ap[:, out_cols.start + cc * RW: out_cols.start + (cc + 1) * RW],
                    zb[:, :], db[:, :], ALU.subtract))

        for tgt, dsrc, ncol in ((sc1, d_s1, H1), (sc2, d_s2, D)):
            for cb in range(ncol // W):
                ml = B.dma("gpsimd", s1r[0:1, 0:W], dsrc[cb * W:(cb + 1) * W][None, :], "lr")
                B.wait("tensor", "lr", ml)
                B.wait("tensor", "vx", B.cnt["vx"])
                B.op("tensor", "mm", lambda e: e.matmul(
                    pb[4][:, 0:512], ones_r[:, :], s1r[0:1, 0:512],
                    start=True, stop=True))
                m2 = B.op("tensor", "mm", lambda e: e.matmul(
                    pb[5][:, 0:512], ones_r[:, :], s1r[0:1, 512:1024],
                    start=True, stop=True))
                B.wait("vector", "mm", m2)
                B.vop(lambda e, tgt=tgt, cb=cb: e.tensor_copy(
                    tgt[:, cb * W: cb * W + 512], pb[4][:, 0:512]))
                B.vop(lambda e, tgt=tgt, cb=cb: e.tensor_copy(
                    tgt[:, cb * W + 512: (cb + 1) * W], pb[5][:, 0:512]))
                B.wait("gpsimd", "mm", m2)

        wuse = {0: 0, 1: 0, 2: 0, 3: 0}
        prev_ln = 0

        def preamble(mt, k2t):
            nonlocal prev_ln
            xs = x_sb[:, mt * D:(mt + 1) * D]
            B.wait("scalar", "mm", B.cnt["mm"])
            B.wait("scalar", "vx", prev_ln)
            s3 = emit_ln(xs, ybuf, st, ybuf)
            prev_ln = s3
            B.vop(lambda e: e.reduce_max(mxc[:, 0:1], ybuf[:, 0:D], axis=AX.X,
                                         apply_absolute_value=True))
            B.vop(lambda e: e.tensor_single_scalar(mxc[:, 0:1], mxc[:, 0:1], 1e-5,
                                                   ALU.max))
            B.vop(lambda e: e.reciprocal(mxc[:, 1:2], mxc[:, 0:1]))
            B.vop(lambda e: e.tensor_single_scalar(mxc[:, 1:2], mxc[:, 1:2], 127.0,
                                                   ALU.mult))
            B.vop(lambda e: e.reciprocal(mxc[:, 2:3], mxc[:, 1:2]))
            emit_round(ybuf, D, 1, xif2, slice(0, D))
            xi_done = B.cnt["vx"]
            B.wait("tensor", "vx", xi_done)
            bank_evac = {0: B.cnt["sc"], 1: B.cnt["sc"]}
            for k in range(16):
                bi = k % 2
                B.wait("tensor", "sc", bank_evac[bi])
                mmv = B.op("tensor", "mm", lambda e, k=k, bi=bi: e.transpose(
                    pb[bi][:, 0:128], xif2[:, k * 128:(k + 1) * 128], ident[:, :]))
                B.wait("scalar", "mm", mmv)
                bank_evac[bi] = B.sop(lambda e, k=k, bi=bi: e.copy(
                    xiT[:, k * 128:(k + 1) * 128], pb[bi][:, 0:128]))
            xiT_done_sc = B.cnt["sc"]

            prev_gelu = 0
            set_done = {0: B.cnt["vx"], 1: B.cnt["vx"]}
            for cb4 in range(4):
                bset = [pb[4 * (cb4 % 2) + i] for i in range(4)]
                B.wait("tensor", "vx", set_done[cb4 % 2])
                if cb4 == 0:
                    B.wait("tensor", "sc", xiT_done_sc)
                for kr in range(16):
                    bi = (cb4 * 16 + kr) % 3
                    wsem = B.WSEMS[bi]
                    B.wait("sync", "mm", wuse[bi])
                    mw = B.dma("sync", wsb[bi].ap(),
                               d_tq1l1[kr * 128:(kr + 1) * 128,
                                       cb4 * 2048:(cb4 + 1) * 2048], wsem)
                    B.wait("tensor", wsem, mw)
                    for j in range(4):
                        B.op("tensor", "mm",
                             lambda e, kr=kr, j=j, bi=bi, bset=bset: e.matmul(
                                 bset[j][:, :], xiT[:, kr * 128:(kr + 1) * 128],
                                 wsb[bi][:, j * 512:(j + 1) * 512],
                                 start=(kr == 0), stop=(kr == 15)),
                             inc=(1 if j == 3 else 0))
                    wuse[bi] = B.cnt["mm"]
                gmm = B.cnt["mm"]
                for j in range(4):
                    hb = cb4 * 4 + j
                    B.wait("vector", "mm", gmm)
                    B.wait("vector", "sc", prev_gelu)
                    B.vop(lambda e, hb=hb, j=j, bset=bset: e.tensor_tensor(
                        zb[:, 0:512], bset[j][:, :], sc1[:, hb * 512:(hb + 1) * 512],
                        ALU.mult))
                    vt_ = B.cnt["vx"]
                    B.wait("scalar", "vx", vt_)
                    prev_gelu = B.sop(lambda e, hb=hb: e.activation(
                        out=gbuf[:, hb * 512:(hb + 1) * 512], in_=zb[:, 0:512],
                        func=AF.Gelu_apprx_tanh, scale=mxc[:, 2:3]))
                set_done[cb4 % 2] = B.cnt["vx"]
            g_done = B.cnt["sc"]

            B.wait("vector", "sc", g_done)
            B.vop(lambda e: e.reduce_max(mxc[:, 0:1], gbuf[:, :], axis=AX.X,
                                         apply_absolute_value=True))
            B.vop(lambda e: e.tensor_single_scalar(mxc[:, 0:1], mxc[:, 0:1], 1e-5,
                                                   ALU.max))
            B.vop(lambda e: e.reciprocal(mxc[:, 1:2], mxc[:, 0:1]))
            B.vop(lambda e: e.tensor_single_scalar(mxc[:, 1:2], mxc[:, 1:2], 127.0,
                                                   ALU.mult))
            B.vop(lambda e: e.reciprocal(mxc[:, 2:3], mxc[:, 1:2]))
            tr_banks = {0: ("sc", B.cnt["sc"]), 1: ("sc", B.cnt["sc"])}
            tr_done = {0: 0, 1: 0}
            for cc in range(H1 // RW):
                xpar = cc % 2
                xx = xif[:, xpar * RW:(xpar + 1) * RW]
                B.wait("vector", "mm", tr_done[xpar])
                B.vop(lambda e, cc=cc: e.tensor_scalar(
                    zb[:, :], gbuf[:, cc * RW:(cc + 1) * RW], mxc[:, 1:2], None,
                    op0=ALU.mult))
                B.vop(lambda e: e.tensor_copy(ib[:, :], zb[:, :]))
                B.vop(lambda e: e.tensor_copy(fb[:, :], ib[:, :]))
                B.vop(lambda e: e.tensor_sub(db[:, :], zb[:, :], fb[:, :]))
                B.vop(lambda e: e.scalar_tensor_tensor(
                    out=zb[:, :], in0=db[:, :], scalar=0.5, in1=fb[:, :],
                    op0=ALU.is_gt, op1=ALU.add))
                B.vop(lambda e: e.tensor_single_scalar(
                    db[:, :], db[:, :], -0.5, ALU.is_lt))
                k2_done = B.vop(lambda e, xx=xx: e.tensor_sub(xx, zb[:, :], db[:, :]))
                B.wait("tensor", "vx", k2_done)
                for k in range(RW // 128):
                    bi = k % 2
                    B.wait("tensor", tr_banks[bi][0], tr_banks[bi][1])
                    mmv = B.op("tensor", "mm", lambda e, k=k, bi=bi, xx=xx: e.transpose(
                        pb[bi][:, 0:128], xx[:, k * 128:(k + 1) * 128], ident[:, :]))
                    B.wait("scalar", "mm", mmv)
                    tr_banks[bi] = ("sc", B.sop(
                        lambda e, cc=cc, k=k, bi=bi, k2t=k2t: e.copy(
                            k2t[:, (cc * 4 + k) * 128:(cc * 4 + k + 1) * 128],
                            pb[bi][:, 0:128])))
                tr_done[xpar] = B.cnt["mm"]
            B.wait("vector", "sc", B.cnt["sc"])

        for pr in range(NMT // 2):
            ma, mb_ = 2 * pr, 2 * pr + 1
            preamble(ma, k2T)
            B.vop(lambda e: e.tensor_copy(mxc[:, 3:4], mxc[:, 2:3]))
            preamble(mb_, k2Tb)
            k2_all = B.cnt["vx"]
            k2_all_sc = B.cnt["sc"]

            for hc in range(64):
                bi = hc % 3
                wsem = B.WSEMS[bi]
                B.wait("sync", "mm", wuse[bi])
                mw = B.dma("sync", wsb[bi].ap(),
                           d_tq2l1[hc * 128:(hc + 1) * 128, :], wsem)
                B.wait("tensor", wsem, mw)
                if hc == 0:
                    B.wait("tensor", "vx", k2_all)
                    B.wait("tensor", "sc", k2_all_sc)
                for n in range(4):
                    B.op("tensor", "mm", lambda e, hc=hc, n=n, bi=bi: e.matmul(
                        pb[4 + n][:, :], k2T[:, hc * 128:(hc + 1) * 128],
                        wsb[bi][:, n * 512:(n + 1) * 512],
                        start=(hc == 0), stop=(hc == 63)),
                        inc=0)
                for n in range(4):
                    B.op("tensor", "mm", lambda e, hc=hc, n=n, bi=bi: e.matmul(
                        pb[n][:, :], k2Tb[:, hc * 128:(hc + 1) * 128],
                        wsb[bi][:, n * 512:(n + 1) * 512],
                        start=(hc == 0), stop=(hc == 63)),
                        inc=(1 if n == 3 else 0))
                wuse[bi] = B.cnt["mm"]
            g2 = B.cnt["mm"]
            B.wait("vector", "mm", g2)
            evv = 0
            for n in range(4):
                B.vop(lambda e, n=n: e.tensor_tensor(
                    zb[:, 0:512], pb[4 + n][:, :], sc2[:, n * 512:(n + 1) * 512],
                    ALU.mult))
                evv = B.vop(lambda e, n=n, ma=ma: e.scalar_tensor_tensor(
                    out=x_sb[:, ma * D + n * 512: ma * D + (n + 1) * 512],
                    in0=zb[:, 0:512], scalar=mxc[:, 3:4],
                    in1=x_sb[:, ma * D + n * 512: ma * D + (n + 1) * 512],
                    op0=ALU.mult, op1=ALU.add))
            for n in range(4):
                B.vop(lambda e, n=n: e.tensor_tensor(
                    zb[:, 0:512], pb[n][:, :], sc2[:, n * 512:(n + 1) * 512],
                    ALU.mult))
                evv = B.vop(lambda e, n=n, mb_=mb_: e.scalar_tensor_tensor(
                    out=x_sb[:, mb_ * D + n * 512: mb_ * D + (n + 1) * 512],
                    in0=zb[:, 0:512], scalar=mxc[:, 2:3],
                    in1=x_sb[:, mb_ * D + n * 512: mb_ * D + (n + 1) * 512],
                    op0=ALU.mult, op1=ALU.add))
            B.wait("tensor", "vx", evv)
            B.wait("tensor", "vx", evv)

    def ph_out():
        B.dma("gpsimd", out.rearrange("(t p) d -> p t d", p=128),
              x_sb.ap().rearrange("p (t d) -> p t d", d=D), "od")
        for name, dst in dbg_outs.items():
            srcs = {"d_q": d_q, "d_kf": d_kf, "d_u": d_u, "d_vt": d_vt,
                    "d_C": d_C, "ag_out0": ag_out0.ap(),
                    "d_tq1l0": d_tq1l0, "d_tq2l0": d_tq2l0,
                    "d_tq1l1": d_tq1l1, "d_tq2l1": d_tq2l1,
                    "kwin": kwin, "vwin": vwin}
            B.dma("gpsimd", dst, srcs[name], "od")
        B.wait("gpsimd", "od", B.cnt["od"])

    B.phase(ph_const, name='const')

    if upto >= 2:
        with ExitStack() as wctx:
            def sbw(name, shape, dt=F32):
                return wctx.enter_context(nc.sbuf_tensor(name, shape, dt))
            bufs_w = dict(
                wsb=[sbw("ws0", [128, 1024]), sbw("ws1", [128, 1024])],
                dmp=sbw("dmp", [128, 1024]),
                acol=sbw("acol", [128, 1]),
                acol2=sbw("acol2", [128, 1]),
                acc=sbw("acc", [128, 2]),
                s1t=sbw("s1t", [128, 64]),
                s2t=sbw("s2t", [128, 16]),
                srow=sbw("srow", [1, 8]),
                tn=sbw("tn", [128, 1024]),
                tn2=sbw("tn2", [128, 1024]),
                qo=[sbw("qo0", [128, 1024], BF16), sbw("qo1", [128, 1024], BF16)],
                qo2=sbw("qo2", [128, 1024]),
                qo3=sbw("qo3", [128, 1024]),
                thp=sbw("thp", [128, 1024]),
                thn=sbw("thn", [128, 1024]),
                s1r=sbw("s1r", [1, 1024]),
            )
            B.phase(lambda: ph_wstats(bufs_w), name='wstats')
            B.phase(lambda: ph_tern(bufs_w), name='tern')

    l0outer = ExitStack()
    M_sb = l0outer.enter_context(nc.sbuf_tensor("M_sb", [128, NH * DH], F32))
    kv_sb = l0outer.enter_context(nc.sbuf_tensor("kv_sb", [128, NH * DH], F32))
    with ExitStack() as l1ctx:
        def sbl(name, shape, dt=F32):
            return l1ctx.enter_context(nc.sbuf_tensor(name, shape, dt))
        bufs1 = dict(
            yT=sbl("yT", [128, 16 * T]),
            wb=[sbl("wb0", [128, 16 * 128]), sbl("wb1", [128, 16 * 128])],
            ev=sbl("ev", [128, 1024]),
            ktok=sbl("ktok", [128, 8 * 128]),
            vtok=sbl("vtok", [128, 8 * 128]),
            Cb=sbl("Cb", [128, 4 * 128]),
            KD=sbl("KD", [128, 2]),
            ybuf=sbl("ybuf", [128, D]),
            st=stats_tiles(sbl, "s1_"),
        )
        B.phase(lambda: ph_l0_sweep1(bufs1), name='l0s1')

    with ExitStack() as l2ctx:
        def sbl2(name, shape, dt=F32):
            return l2ctx.enter_context(nc.sbuf_tensor(name, shape, dt))
        bufs2 = dict(
            qb=sbl2("qb", [128, NH * BLK]),
            kb=sbl2("kb", [128, NH * BLK]),
            vtb=sbl2("vtb", [128, 2 * D]),
            ub=sbl2("ub", [128, NH * BLK]),
            Cl=sbl2("Cl", [128, NH * 128]),
            ob=sbl2("ob", [128, NH * BLK]),
            agm=sbl2("agm", [128, NH * 128]),
            qp=sbl2("qp", [128, BLK]),
            qd=sbl2("qd", [128, BLK]),
            dct=sbl2("dct", [128, 2 * BLK]),
            pbuf=sbl2("pbuf", [128, 2 * BLK]),
            djc=sbl2("djc", [128, 1]),
            tmph=sbl2("tmph", [128, 128]),
            wv=[sbl2("wv0", [128, 512]), sbl2("wv1", [128, 512])],
            dump=sbl2("dump2", [128, BLK]),
            rmsc=sbl2("rmsc", [128, 2]),
            r_a=sbl2("r_a", [1, BLK]),
            r_b=sbl2("r_b", [1, BLK]),
            r_c=sbl2("r_c", [1, BLK]),
        )
        B.phase(lambda: ph_l0_sweep2(bufs2), name='l0s2')
    l0outer.close()

    if upto >= 2:
        with ExitStack() as fctx:
            def sbf(name, shape, dt=F32):
                return fctx.enter_context(nc.sbuf_tensor(name, shape, dt))
            bufs3 = dict(
                ybuf=sbf("f_ybuf", [128, D]),
                st=stats_tiles(sbf, "f_st_"),
                y2h=sbf("y2h", [128, 16 * BLK], BF16),
                y2l=sbf("y2l", [128, 16 * BLK], BF16),
                wsb=[sbf(f"fw{i}", [128, 2048], BF16) for i in range(4)],
                rb=sbf("rb", [128, BLK]),
                sqf=sbf("sqf", [128, BLK]),
                sqh=sbf("sqh", [128, 40 * BLK], BF16),
                sql=sbf("sql", [128, 40 * BLK], BF16),
            )
            B.phase(lambda: ph_l0_ffn(bufs3), name='l0f')

    if upto >= 3:
        with ExitStack() as actx:
            def sba(name, shape, dt=F32):
                return actx.enter_context(nc.sbuf_tensor(name, shape, dt))
            bufs4 = dict(
                y3T=sba("y3T", [128, 16 * T]),
                wb=[sba("aw0", [128, 16 * 128]), sba("aw1", [128, 16 * 128])],
                ev=sba("a_ev", [128, 1024]),
                vtok=sba("a_vtok", [128, 8 * 128]),
                ybuf=sba("a_ybuf", [128, D]),
                st=stats_tiles(sba, "a_st_"),
            )
            B.phase(lambda: ph_l1_attn_a(bufs4), name='l1a_a')
        with ExitStack() as bctx:
            def sbb(name, shape, dt=F32):
                return bctx.enter_context(nc.sbuf_tensor(name, shape, dt))
            bufs5 = dict(
                qmb=sbb("qmb", [128, NH * 512]),
                osb=sbb("osb", [128, NH * 512]),
                masks=sbb("masks", [128, NKT * 512], BF16),
                ktile=[sbb("ktile0", [128, 2048]), sbb("ktile1", [128, 2048])],
                vtile=[sbb("vtile0", [128, 2048]), sbb("vtile1", [128, 2048])],
                tbv=[sbb(f"tbv{i}", [128, 512]) for i in range(3)],
                pbm=[sbb(f"pbm{i}", [128, 512]) for i in range(3)],
                lsum=sbb("lsum", [128, 2048]),
                wvb=[sbb("bw0", [128, 512]), sbb("bw1", [128, 512])],
            )
            B.phase(lambda: ph_l1_attn_b(bufs5), name='l1a_b')

    if upto >= 4:
        with ExitStack() as gctx:
            def sbg(name, shape, dt=F32):
                return gctx.enter_context(nc.sbuf_tensor(name, shape, dt))
            bufs6 = dict(
                st=stats_tiles(sbg, "g_st_"),
                sc1=sbg("sc1", [128, H1]),
                sc2=sbg("sc2", [128, D]),
                s1r=sbg("g_s1r", [1, 1024]),
                zb=sbg("g_zb", [128, 512]),
                fb=sbg("g_fb", [128, 512]),
                db=sbg("g_db", [128, 512]),
                ib=sbg("g_ib", [128, 512], I32),
                xif=sbg("g_xif", [128, 2 * 512]),
                xiT=sbg("g_xiT", [128, 16 * 128], BF16),
                gbuf=sbg("g_gbuf", [128, H1]),
                k2T=sbg("g_k2T", [128, 64 * 128], BF16),
                k2Tb=sbg("g_k2Tb", [128, 64 * 128], BF16),
                wsb=[sbg(f"g_w{i}", [128, 2048], BF16) for i in range(3)],
                mxc=sbg("g_mxc", [128, 4]),
            )
            B.phase(lambda: ph_l1_ffn(bufs6), name='l1f')

    B.phase(ph_out, name='out')
    B.ctx.close()
    return nc


def _tp(a):
    return np.ascontiguousarray(np.asarray(a, dtype=np.float32).T)


def _pack4(w):
    r = w.shape[0] // D
    t = np.asarray(w, np.float32).reshape(r, NH, 128, NH, 128)
    return np.ascontiguousarray(t.transpose(0, 1, 4, 3, 2)).reshape(r, NH, 128, D)


def _masks_for_core(c):
    import ml_dtypes
    m = np.zeros((2, NKT, 128, 512), dtype=ml_dtypes.bfloat16)
    sp = np.arange(128)[:, None]
    mp = np.arange(512)[None, :]
    for mb in range(2):
        gm = T * c + 512 * mb
        for k in range(NKT):
            bj = gm - WIN + 128 * k
            i = gm + mp
            j = bj + sp
            ok = (j <= i) & (i - j < WIN) & (j >= 0)
            m[mb, k] = ok.astype(np.float32)
    return m


def kernel(L0_attn_x, L0_attn_slope_rate, L0_attn_W_qkvu, L0_attn_W_out,
           L0_ffn_w_fc, L0_ffn_w_proj, L1_attn_Wq, L1_attn_Wk, L1_attn_Wv,
           L1_attn_Wo, L1_ffn_w1, L1_ffn_w2, _upto=4, _dbg=()):
    x = np.ascontiguousarray(np.asarray(L0_attn_x, np.float32).reshape(NCORES * T, D))
    shared = dict(
        w4qkvu=_pack4(L0_attn_W_qkvu), woutT=_tp(L0_attn_W_out),
        wfcT=_tp(L0_ffn_w_fc), wprojT=_tp(L0_ffn_w_proj),
        w3q=_pack4(L1_attn_Wq)[0], w3k=_pack4(L1_attn_Wk)[0],
        w3v=_pack4(L1_attn_Wv)[0],
        woT=_tp(L1_attn_Wo),
        w1n=np.ascontiguousarray(np.asarray(L1_ffn_w1, np.float32)),
        w1T=_tp(L1_ffn_w1),
        w2n=np.ascontiguousarray(np.asarray(L1_ffn_w2, np.float32)),
        w2T=_tp(L1_ffn_w2),
        slope_in=np.ascontiguousarray(
            np.asarray(L0_attn_slope_rate, np.float32).reshape(NH)),
    )
    in_maps = []
    for c in range(NCORES):
        m = dict(shared)
        m["xin"] = np.ascontiguousarray(x[c * T:(c + 1) * T])
        m["gates_in"] = (np.arange(NCORES) < c).astype(np.float32)
        m["lmask"] = _masks_for_core(c)
        in_maps.append(m)
    nc = build_nc(upto=_upto, dbg=_dbg)
    res = run_bass_kernel_spmd(nc, in_maps, core_ids=list(range(NCORES)))
    outs = np.concatenate([res.results[c]["out"] for c in range(NCORES)], axis=0)
    if _dbg:
        dbgd = [{k: v for k, v in res.results[c].items() if k.startswith("dbg_")}
                for c in range(NCORES)]
        return outs.reshape(1, NCORES * T, D), dbgd
    return outs.reshape(1, NCORES * T, D)


# version 12
# speedup vs baseline: 1.1009x; 1.0138x over previous
import numpy as np
from contextlib import ExitStack

import concourse.bass as bass
import concourse.mybir as mybir
from concourse.bass_utils import run_bass_kernel_spmd

F32 = mybir.dt.float32
BF16 = mybir.dt.bfloat16
I32 = mybir.dt.int32
AF = mybir.ActivationFunctionType
ALU = mybir.AluOpType
AX = mybir.AxisListType

NCORES = 8
T = 1024
D = 2048
NH = 16
DH = 128
BLK = 256
NBLK = T // BLK
NMT = T // 128
H0 = 5120
H1 = 8192
WIN = 1024
NKT = 12

ENGS = ("sync", "gpsimd", "tensor", "vector", "scalar")


class _B:

    def __init__(self, nc):
        self.nc = nc
        self.ctx = ExitStack()
        self.sem = {}
        self.cnt = {}
        for name in ("wd", "gd", "mm", "vx", "sc", "gx", "cc", "bar", "od",
                     "w0", "w1", "w2", "w3", "ev", "vt", "cs", "lq", "lk",
                     "lv", "lu", "lc", "lr", "t1", "t2", "t3"):
            self.sem[name] = self.ctx.enter_context(nc.semaphore(name))
            self.cnt[name] = 0
        self.en = None
        self.eng = None

    def dma(self, qeng, out, in_, sem="gd"):
        self.cnt[sem] += 16
        if self.en == qeng:
            if callable(out):
                out = out()
            if callable(in_):
                in_ = in_()
            self.eng.dma_start(out=out, in_=in_).then_inc(self.sem[sem], 16)
        return self.cnt[sem]

    AUTO_CHAIN = {"vector": "vx", "scalar": "sc"}

    def op(self, oeng, sem, fn, inc=1):
        own = self.AUTO_CHAIN.get(oeng)
        if own is not None and self.en == oeng and self.cnt[own] > 0:
            self.eng.wait_ge(self.sem[own], self.cnt[own])
        if inc:
            self.cnt[sem] += inc
        if self.en == oeng:
            inst = fn(self.eng)
            if inc:
                inst.then_inc(self.sem[sem], inc)
        return self.cnt[sem]

    def vop(self, fn):
        return self.op("vector", "vx", fn)

    def sop(self, fn):
        return self.op("scalar", "sc", fn)

    def sil(self, oeng, fn):
        if self.en == oeng:
            fn(self.eng)

    def wait(self, oeng, sem, val):
        if self.en == oeng and val > 0:
            self.eng.wait_ge(self.sem[sem], val)

    WSEMS = ("w0", "w1", "w2", "w3")

    ENG_SEMS = {"sync": ("wd", "w0", "w1", "w2", "w3", "t3"),
                "gpsimd": ("gd", "gx", "od", "cc", "ev", "vt", "cs",
                           "lq", "lk", "lv", "lu", "lc", "lr"),
                "tensor": ("mm",), "vector": ("vx",),
                "scalar": ("sc", "t1", "t2")}

    def barrier(self):
        self.cnt["bar"] += len(ENGS)
        tgt = self.cnt["bar"]
        snap = dict(self.cnt)
        with self.nc.Block() as blk:
            for name in ENGS:
                def body(eng, name=name, tgt=tgt, snap=snap):
                    for s in self.ENG_SEMS[name]:
                        if snap[s] > 0:
                            eng.wait_ge(self.sem[s], snap[s])
                    eng.sem_inc(self.sem["bar"], 1)
                    eng.wait_ge(self.sem["bar"], tgt)
                getattr(blk, name)(body)

    def chain(self, oeng, sem, fns, first_wait=None):
        v = None
        for i, fn in enumerate(fns):
            if i == 0 and first_wait is not None:
                self.wait(oeng, first_wait[0], first_wait[1])
            if v is not None:
                self.wait(oeng, sem, v)
            v = self.op(oeng, sem, fn)
        return v

    def phase(self, emit_fn, name=None):
        snap = dict(self.cnt)
        final = None
        scope = None
        if name is not None:
            try:
                scope = self.nc.enter_named_scope(name, False)
            except Exception:
                scope = None
        with self.nc.Block() as blk:
            for name in ENGS:
                self.cnt = dict(snap)
                self.en = name

                def body(eng):
                    self.eng = eng
                    emit_fn()
                getattr(blk, name)(body)
                if final is None:
                    final = dict(self.cnt)
                else:
                    assert final == self.cnt, "phase emitter is engine-dependent!"
        self.en = self.eng = None
        if scope is not None:
            try:
                self.nc.leave_named_scope(name, scope[0], False)
            except Exception:
                pass
        self.barrier()


def build_nc(upto=4, dbg=(), world=NCORES):
    nc = bass.Bass("TRN2", target_bir_lowering=False)
    B = _B(nc)

    xin = nc.dram_tensor("xin", [T, D], F32, kind="ExternalInput").ap()
    w4qkvu = nc.dram_tensor("w4qkvu", [4, NH, 128, 2048], F32, kind="ExternalInput").ap()
    woutT = nc.dram_tensor("woutT", [D, D], F32, kind="ExternalInput").ap()
    wfcT = nc.dram_tensor("wfcT", [D, H0], F32, kind="ExternalInput").ap()
    wprojT = nc.dram_tensor("wprojT", [H0, D], F32, kind="ExternalInput").ap()
    w3q = nc.dram_tensor("w3q", [NH, 128, 2048], F32, kind="ExternalInput").ap()
    w3k = nc.dram_tensor("w3k", [NH, 128, 2048], F32, kind="ExternalInput").ap()
    w3v = nc.dram_tensor("w3v", [NH, 128, 2048], F32, kind="ExternalInput").ap()
    woT = nc.dram_tensor("woT", [D, D], F32, kind="ExternalInput").ap()
    w1n = nc.dram_tensor("w1n", [H1, D], F32, kind="ExternalInput").ap()
    w1T = nc.dram_tensor("w1T", [D, H1], F32, kind="ExternalInput").ap()
    w2n = nc.dram_tensor("w2n", [D, H1], F32, kind="ExternalInput").ap()
    w2T = nc.dram_tensor("w2T", [H1, D], F32, kind="ExternalInput").ap()
    slope_in = nc.dram_tensor("slope_in", [NH], F32, kind="ExternalInput").ap()
    gates_in = nc.dram_tensor("gates_in", [NCORES], F32, kind="ExternalInput").ap()
    lmask = nc.dram_tensor("lmask", [2, NKT, 128, 512], BF16, kind="ExternalInput").ap()
    out = nc.dram_tensor("out", [T, D], F32, kind="ExternalOutput").ap()

    d_q = nc.dram_tensor("d_q", [D, T], F32).ap()
    d_kf = nc.dram_tensor("d_kf", [D, T], F32).ap()
    d_u = nc.dram_tensor("d_u", [D, T], F32).ap()
    d_vt = nc.dram_tensor("d_vt", [T, D], F32).ap()
    d_C = nc.dram_tensor("d_C", [NH, NBLK, DH, DH], F32).ap()
    ag_in0 = nc.dram_tensor("ag_in0", [NH * DH, DH], F32)
    ag_out0 = nc.dram_tensor("ag_out0", [world * NH * DH, DH], F32, addr_space=("Shared" if world > 1 else "Local"))
    ag_in_k = nc.dram_tensor("ag_in_k", [D, T], F32)
    ag_out_k = nc.dram_tensor("ag_out_k", [world * D, T], F32, addr_space=("Shared" if world > 1 else "Local"))
    ag_in_v = nc.dram_tensor("ag_in_v", [T, D], F32)
    ag_out_v = nc.dram_tensor("ag_out_v", [world * T, D], F32, addr_space=("Shared" if world > 1 else "Local"))
    kwin = nc.dram_tensor("kwin", [D, 2 * T], F32).ap()
    vwin = nc.dram_tensor("vwin", [2 * T, D], F32).ap()
    d_tq1l0 = nc.dram_tensor("d_tq1l0", [D, H0], BF16).ap()
    d_tq2l0 = nc.dram_tensor("d_tq2l0", [H0, D], BF16).ap()
    d_tq1l1 = nc.dram_tensor("d_tq1l1", [D, H1], BF16).ap()
    d_tq2l1 = nc.dram_tensor("d_tq2l1", [H1, D], BF16).ap()
    d_scr = nc.dram_tensor("d_scr", [8192], F32).ap()
    d_s1 = nc.dram_tensor("d_s1", [H1], F32).ap()
    d_s2 = nc.dram_tensor("d_s2", [D], F32).ap()

    dbg_specs = {"d_q": ([D, T], F32), "d_kf": ([D, T], F32), "d_u": ([D, T], F32),
                 "d_vt": ([T, D], F32), "d_C": ([NH, NBLK, DH, DH], F32),
                 "ag_out0": ([world * NH * DH, DH], F32),
                 "kwin": ([D, 2 * T], F32), "vwin": ([2 * T, D], F32),
                 "d_tq1l0": ([D, H0], BF16), "d_tq2l0": ([H0, D], BF16),
                 "d_tq1l1": ([D, H1], BF16), "d_tq2l1": ([H1, D], BF16)}
    dbg_outs = {n: nc.dram_tensor("dbg_" + n, dbg_specs[n][0], dbg_specs[n][1],
                                  kind="ExternalOutput").ap() for n in dbg}

    ctx = B.ctx

    def sb(name, shape, dt=F32):
        return ctx.enter_context(nc.sbuf_tensor(name, shape, dt))

    x_sb = sb("x_sb", [128, NMT * D])
    ident = sb("ident", [128, 128])
    ones_c = sb("ones_c", [128, 1])
    ones_r = sb("ones_r", [1, 128])
    ones_bc = sb("ones_bc", [128, 128])
    slope_row = sb("slope_row", [1, NH])
    negs_bc = sb("negs_bc", [128, NH])
    bd_bc = sb("bd_bc", [128, NH])
    bd4m1_bc = sb("bd4m1_bc", [128, NH])
    G = sb("G", [128, NCORES])
    iotaq = sb("iotaq", [128, BLK])
    idxd = sb("idxd", [128, 2 * BLK])
    io255 = sb("io255", [128, 1])
    neg30 = sb("neg30", [128, 1])
    io127 = sb("io127", [128, 1])
    am_fc = sb("am_fc", [128, 1])
    am_pr = sb("am_pr", [128, 1])
    thr_fc = sb("thr_fc", [128, 1])
    thr_pr = sb("thr_pr", [128, 1])
    nthr_fc = sb("nthr_fc", [128, 1])
    nthr_pr = sb("nthr_pr", [128, 1])
    pb = [ctx.enter_context(nc.psum_tensor(f"pb{i}", [128, 512], F32)) for i in range(8)]
    M_sb = kv_sb = None

    def ph_const():
        B.dma("sync", slope_row[0:1, :], slope_in[None, :], "wd")
        B.dma("sync", G[0:1, 0:NCORES], gates_in[None, :], "wd")
        m_slope = m_gates = B.cnt["wd"]

        B.op("gpsimd", "gx", lambda e: e.memset(ones_c[:, :], 1.0))
        B.op("gpsimd", "gx", lambda e: e.memset(ones_r[:, :], 1.0))
        B.op("gpsimd", "gx", lambda e: e.memset(ones_bc[:, :], 1.0))
        v = B.op("gpsimd", "gx", lambda e: e.memset(ident[:, :], 0.0))
        B.wait("gpsimd", "gx", v)
        B.op("gpsimd", "gx", lambda e: e.affine_select(
            out=ident[:, :], in_=ident[:, :], compare_op=ALU.not_equal,
            fill=1.0, base=0, pattern=[[-1, 128]], channel_multiplier=1))
        B.op("gpsimd", "gx", lambda e: e.iota(iotaq[:, :], [[1, BLK]], base=1,
                                         channel_multiplier=0, allow_small_or_imprecise_dtypes=True))
        B.op("gpsimd", "gx", lambda e: e.memset(neg30[:, :], -30.0))
        v2 = B.op("gpsimd", "gx", lambda e: e.iota(
            idxd[:, :], [[-128, 2], [1, BLK]], base=0, channel_multiplier=-1, allow_small_or_imprecise_dtypes=True))
        B.wait("gpsimd", "gx", v2)
        B.op("gpsimd", "gx", lambda e: e.affine_select(
            out=idxd[:, :], in_=idxd[:, :], compare_op=ALU.is_ge,
            fill=1.0e30, base=0, pattern=[[-128, 2], [1, BLK]],
            channel_multiplier=-1))
        B.op("gpsimd", "gx", lambda e: e.iota(io255[:, :], [[0, 1]], base=255,
                                         channel_multiplier=-1, allow_small_or_imprecise_dtypes=True))
        v4 = B.op("gpsimd", "gx", lambda e: e.iota(io127[:, :], [[0, 1]], base=127,
                                                   channel_multiplier=-1, allow_small_or_imprecise_dtypes=True))

        B.wait("scalar", "wd", m_slope)
        B.op("scalar", "sc", lambda e: e.activation(
            out=bd_bc[0:1, :], in_=slope_row[0:1, :], func=AF.Exp, scale=-256.0))
        B.op("scalar", "sc", lambda e: e.activation(
            out=bd4m1_bc[0:1, :], in_=slope_row[0:1, :], func=AF.Exp, scale=-1024.0))
        s_ng = B.op("scalar", "sc", lambda e: e.mul(
            negs_bc[0:1, :], slope_row[0:1, :], -1.0))

        B.wait("tensor", "sc", s_ng)
        B.wait("tensor", "gx", v4)
        B.wait("tensor", "wd", m_gates)
        B.op("tensor", "mm", lambda e: e.matmul(
            pb[0][:, 0:NH], ones_r[:, :], negs_bc[0:1, :], start=True, stop=True))
        B.op("tensor", "mm", lambda e: e.matmul(
            pb[0][:, 64:64 + NH], ones_r[:, :], bd_bc[0:1, :], start=True, stop=True))
        B.op("tensor", "mm", lambda e: e.matmul(
            pb[0][:, 128:128 + NH], ones_r[:, :], bd4m1_bc[0:1, :], start=True, stop=True))
        mmv = B.op("tensor", "mm", lambda e: e.matmul(
            pb[0][:, 192:192 + NCORES], ones_r[:, :], G[0:1, 0:NCORES],
            start=True, stop=True))

        B.wait("vector", "mm", mmv)
        B.op("vector", "vx", lambda e: e.tensor_copy(negs_bc[:, :], pb[0][:, 0:NH]))
        B.op("vector", "vx", lambda e: e.tensor_copy(bd_bc[:, :], pb[0][:, 64:64 + NH]))
        B.op("vector", "vx", lambda e: e.tensor_copy(G[:, :], pb[0][:, 192:192 + NCORES]))
        B.op("vector", "vx", lambda e: e.tensor_single_scalar(
            bd4m1_bc[:, :], pb[0][:, 128:128 + NH], 1.0, ALU.subtract))

        B.dma("gpsimd", x_sb.ap().rearrange("p (t d) -> p t d", d=D),
              xin.rearrange("(t p) d -> p t d", p=128), "gd")
        B.wait("gpsimd", "gd", B.cnt["gd"])

    def emit_ln(xs, ybuf, st, dump):
        s1 = B.op("scalar", "sc", lambda e: e.activation(
            out=dump[:, 0:D], in_=xs, func=AF.Square, accum_out=st["ssq"][:, 0:1]))
        B.op("vector", "vx", lambda e: e.reduce_sum(
            st["ssum"][:, 0:1], xs, axis=AX.X))
        B.wait("vector", "sc", s1)
        B.op("vector", "vx", lambda e: e.tensor_single_scalar(
            st["mean"][:, 0:1], st["ssum"][:, 0:1], 1.0 / D, ALU.mult))
        B.op("vector", "vx", lambda e: e.tensor_single_scalar(
            st["var"][:, 0:1], st["ssq"][:, 0:1], 1.0 / D, ALU.mult))
        B.op("vector", "vx", lambda e: e.tensor_tensor(
            st["m2"][:, 0:1], st["mean"][:, 0:1], st["mean"][:, 0:1], ALU.mult))
        v2 = B.op("vector", "vx", lambda e: e.scalar_tensor_tensor(
            out=st["var"][:, 0:1], in0=st["var"][:, 0:1], scalar=1e-5,
            in1=st["m2"][:, 0:1], op0=ALU.add, op1=ALU.subtract))
        B.wait("scalar", "vx", v2)
        s2 = B.op("scalar", "sc", lambda e: e.activation(
            out=st["rstd"][:, 0:1], in_=st["var"][:, 0:1], func=AF.Sqrt))
        B.wait("vector", "sc", s2)
        B.op("vector", "vx", lambda e: e.reciprocal(st["rstd"][:, 0:1], st["rstd"][:, 0:1]))
        B.op("vector", "vx", lambda e: e.tensor_tensor(
            st["m2"][:, 0:1], st["rstd"][:, 0:1], st["rstd"][:, 0:1], ALU.mult))
        B.op("vector", "vx", lambda e: e.tensor_tensor(
            st["m2"][:, 0:1], st["m2"][:, 0:1], st["var"][:, 0:1], ALU.mult))
        B.op("vector", "vx", lambda e: e.tensor_scalar(
            st["m2"][:, 0:1], st["m2"][:, 0:1], -0.5, 1.5, op0=ALU.mult, op1=ALU.add))
        B.op("vector", "vx", lambda e: e.tensor_tensor(
            st["rstd"][:, 0:1], st["rstd"][:, 0:1], st["m2"][:, 0:1], ALU.mult))
        v4 = B.op("vector", "vx", lambda e: e.tensor_scalar(
            ybuf[:, 0:D], xs, st["mean"][:, 0:1], st["rstd"][:, 0:1],
            op0=ALU.subtract, op1=ALU.mult))
        return v4

    def stats_tiles(alloc, pfx):
        return {k: alloc(pfx + k, [128, 1]) for k in
                ("ssq", "ssum", "mean", "var", "m2", "rstd", "nmr")}

    def ph_l0_sweep1(bufs):
        yT = bufs["yT"]; wb = bufs["wb"]; ev = bufs["ev"]
        ktok = bufs["ktok"]; vtok = bufs["vtok"]; Cb = bufs["Cb"]
        KD = bufs["KD"]; ybuf = bufs["ybuf"]; st = bufs["st"]
        dump = ybuf

        bank_evac = {0: 0, 1: 0}
        for mt in range(NMT):
            s3 = emit_ln(x_sb[:, mt * D:(mt + 1) * D], ybuf, st, dump)
            B.wait("tensor", "vx", s3)
            for k in range(16):
                bi = k % 2
                B.wait("tensor", "vx", bank_evac[bi])
                mmv = B.op("tensor", "mm", lambda e, k=k, bi=bi: e.transpose(
                    pb[bi][:, 0:128], ybuf[:, k * 128:(k + 1) * 128], ident[:, :]))
                B.wait("vector", "mm", mmv)
                bank_evac[bi] = B.op("vector", "vx",
                                     lambda e, k=k, mt=mt, bi=bi: e.tensor_copy(
                                         yT[:, k * T + mt * 128: k * T + (mt + 1) * 128],
                                         pb[bi][:, 0:128]))
            B.wait("scalar", "mm", B.cnt["mm"])

        yT_done = B.cnt["vx"]
        vM = B.op("vector", "vx", lambda e: e.memset(M_sb[:, :], 0.0))
        B.wait("tensor", "vx", yT_done)

        dests = [d_q, d_kf, None, d_u]
        wlast = {0: 0, 1: 0}
        prev_evac = ("vx", 0)
        ev_spill = 0
        tr_bank_evac = {0: 0, 1: 0}
        vtok_spill = 0
        c_spill = 0
        kd_ready = 0
        for h in range(NH):
            B.wait("scalar", "vx", max(tr_bank_evac.values()))
            s_kd1 = B.op("scalar", "sc", lambda e, h=h: e.activation(
                out=KD[:, 0:1], in_=io255[:, :], func=AF.Exp, scale=negs_bc[:, h:h + 1]))
            kd_ready = B.op("scalar", "sc", lambda e, h=h: e.activation(
                out=KD[:, 1:2], in_=io127[:, :], func=AF.Exp, scale=negs_bc[:, h:h + 1]))
            for r in range(4):
                bi = (h * 4 + r) % 2
                wsem = "w0" if bi == 0 else "w1"
                B.wait("sync", "mm", wlast[bi])
                mw = B.dma("sync", wb[bi].ap(), w4qkvu[r, h], wsem)
                B.wait("tensor", wsem, mw)
                B.wait("tensor", prev_evac[0], prev_evac[1])
                for half in range(2):
                    for kc in range(16):
                        B.op("tensor", "mm",
                             lambda e, kc=kc, half=half, bi=bi:
                             e.matmul(pb[2 + half][:, :],
                                      wb[bi][:, kc * 128:(kc + 1) * 128],
                                      yT[:, kc * T + half * 512: kc * T + (half + 1) * 512],
                                      start=(kc == 0), stop=(kc == 15)),
                             inc=(1 if kc == 15 else 0))
                wlast[bi] = B.cnt["mm"]
                g1 = B.cnt["mm"]
                eeng, esem = (("scalar", "sc") if r in (0, 1) else ("vector", "vx"))
                B.wait(eeng, "mm", g1)
                B.wait(eeng, "ev", ev_spill)
                if r in (0, 1):
                    B.op("scalar", "sc", lambda e: e.activation(
                        out=ev[:, 0:512], in_=pb[2][:, :], func=AF.Silu))
                    e2 = B.op("scalar", "sc", lambda e: e.activation(
                        out=ev[:, 512:1024], in_=pb[3][:, :], func=AF.Silu))
                else:
                    B.op("vector", "vx", lambda e: e.tensor_copy(ev[:, 0:512], pb[2][:, :]))
                    e2 = B.op("vector", "vx", lambda e: e.tensor_copy(
                        ev[:, 512:1024], pb[3][:, :]))
                prev_evac = (esem, e2)
                if r != 2:
                    B.wait("gpsimd", esem, e2)
                    ev_spill = B.dma("gpsimd", dests[r][h * 128:(h + 1) * 128, :],
                                     ev[:, 0:1024], "ev")
                if r in (1, 2):
                    tgt = ktok if r == 1 else vtok
                    B.wait("tensor", esem, e2)
                    if r == 1:
                        B.wait("tensor", "mm", 0)
                    for t in range(8):
                        tb = t % 2
                        B.wait("tensor", "vx", tr_bank_evac[tb])
                        mmv = B.op("tensor", "mm", lambda e, t=t, tb=tb: e.transpose(
                            pb[4 + tb][:, 0:128], ev[:, t * 128:(t + 1) * 128],
                            ident[:, :]))
                        B.wait("vector", "mm", mmv)
                        if r == 1:
                            B.wait("vector", "sc", kd_ready)
                            B.wait("vector", "gd", 0)
                            tr_bank_evac[tb] = B.op(
                                "vector", "vx", lambda e, t=t, tb=tb: e.tensor_scalar(
                                    tgt[:, t * 128:(t + 1) * 128], pb[4 + tb][:, 0:128],
                                    KD[:, (t % 2):(t % 2) + 1], None, op0=ALU.mult))
                        else:
                            B.wait("vector", "vt", vtok_spill)
                            tr_bank_evac[tb] = B.op(
                                "vector", "vx", lambda e, t=t, tb=tb: e.tensor_copy(
                                    tgt[:, t * 128:(t + 1) * 128], pb[4 + tb][:, 0:128]))
                    if r == 2:
                        B.wait("gpsimd", "vx", tr_bank_evac[(7) % 2])
                        vtok_spill = B.dma(
                            "gpsimd",
                            d_vt.rearrange("(t p) d -> p t d", p=128)[:, :, h * 128:(h + 1) * 128],
                            vtok.ap().rearrange("p (t d) -> p t d", d=128), "vt")
            B.wait("tensor", "vx", max(tr_bank_evac[0], tr_bank_evac[1]))
            vM_h = vM
            for b in range(NBLK):
                mmv = 0
                for sc_ in range(2):
                    t = 2 * b + sc_
                    mmv = B.op("tensor", "mm", lambda e, t=t: e.matmul(
                        pb[6][:, 0:128], ktok[:, t * 128:(t + 1) * 128],
                        vtok[:, t * 128:(t + 1) * 128],
                        start=(t % 2 == 0), stop=(t % 2 == 1)),
                        inc=(1 if t % 2 == 1 else 0))
                B.wait("vector", "mm", mmv)
                B.wait("vector", "cs", c_spill)
                B.op("vector", "vx", lambda e, b=b: e.tensor_copy(
                    Cb[:, b * 128:(b + 1) * 128], pb[6][:, 0:128]))
                vM_h = B.op("vector", "vx", lambda e, h=h: e.scalar_tensor_tensor(
                    out=M_sb[:, h * 128:(h + 1) * 128],
                    in0=M_sb[:, h * 128:(h + 1) * 128],
                    scalar=bd_bc[:, h:h + 1], in1=pb[6][:, 0:128],
                    op0=ALU.mult, op1=ALU.add))
                B.wait("tensor", "vx", vM_h)
            vM = vM_h
            B.wait("gpsimd", "vx", vM)
            c_spill = B.dma("gpsimd", d_C[h].rearrange("b p e -> p b e"),
                            Cb.ap().rearrange("p (b e) -> p b e", e=128), "cs")

        B.dma("gpsimd", ag_in0.ap().rearrange("(h p) e -> p h e", p=128),
              M_sb.ap().rearrange("p (h e) -> p h e", e=128), "gd")
        for s_ in ("gd", "ev", "vt", "cs"):
            B.wait("gpsimd", s_, B.cnt[s_])
        B.op("gpsimd", "cc", lambda e: e.collective_compute(
            "AllGather", ALU.bypass, replica_groups=[list(range(world))],
            ins=[ag_in0.ap().opt()], outs=[ag_out0.ap().opt()]))

    def ph_l0_sweep2(bufs):
        qb = bufs["qb"]; kb = bufs["kb"]; vtb = bufs["vtb"]; ub = bufs["ub"]
        Cl = bufs["Cl"]; ob = bufs["ob"]; agm = bufs["agm"]
        qp = bufs["qp"]; qd = bufs["qd"]; dct = bufs["dct"]; pbuf = bufs["pbuf"]
        djc = bufs["djc"]; tmph = bufs["tmph"]; wv = bufs["wv"]
        dump = bufs["dump"]; rmsc = bufs["rmsc"]
        r_a = bufs["r_a"]; r_b = bufs["r_b"]; r_c = bufs["r_c"]

        B.wait("gpsimd", "cc", B.cnt["cc"])
        vz = B.op("vector", "vx", lambda e: e.memset(kv_sb[:, :], 0.0))
        for j in range(world):
            B.wait("gpsimd", "vx", vz)
            agl = B.dma("gpsimd", agm.ap().rearrange("p (h e) -> p h e", e=128),
                        ag_out0.ap()[j * NH * DH:(j + 1) * NH * DH, :]
                        .rearrange("(h p) e -> p h e", p=128), "gd")
            B.wait("vector", "gd", agl)
            for h in range(NH):
                B.op("vector", "vx", lambda e, h=h: e.scalar_tensor_tensor(
                    out=djc[:, 0:1], in0=bd4m1_bc[:, h:h + 1], scalar=G[:, j:j + 1],
                    in1=ones_c[:, :], op0=ALU.mult, op1=ALU.add))
                B.op("vector", "vx", lambda e, h=h: e.tensor_scalar(
                    tmph[:, :], agm[:, h * 128:(h + 1) * 128], G[:, j:j + 1], None,
                    op0=ALU.mult))
                vz = B.op("vector", "vx", lambda e, h=h: e.scalar_tensor_tensor(
                    out=kv_sb[:, h * 128:(h + 1) * 128],
                    in0=kv_sb[:, h * 128:(h + 1) * 128],
                    scalar=djc[:, 0:1], in1=tmph[:, :], op0=ALU.mult, op1=ALU.add))

        pm_done = 0
        o_done = 0
        kvup = vz
        wuse = {0: 0, 1: 0}
        for b in range(NBLK):
            c0 = b * BLK
            B.wait("gpsimd", "mm", B.cnt["mm"])
            B.wait("gpsimd", "vx", B.cnt["vx"])
            for s_ in ("lq", "lk", "lv", "lu", "lc"):
                B.wait("gpsimd", s_, B.cnt[s_])
            ld_q = B.dma("gpsimd", qb.ap().rearrange("p (c m) -> p c m", m=BLK),
                         d_q.rearrange("(c p) m -> p c m", p=128)[:, :, c0:c0 + BLK], "lq")
            ld_k = B.dma("gpsimd", kb.ap().rearrange("p (c m) -> p c m", m=BLK),
                         d_kf.rearrange("(c p) m -> p c m", p=128)[:, :, c0:c0 + BLK], "lk")
            ld_v = B.dma("gpsimd", vtb.ap().rearrange("p (s d) -> p s d", d=D),
                         d_vt.rearrange("(s p) d -> p s d", p=128)[:, 2 * b:2 * b + 2, :], "lv")
            ld_u = B.dma("gpsimd", ub.ap().rearrange("p (c m) -> p c m", m=BLK),
                         d_u.rearrange("(c p) m -> p c m", p=128)[:, :, c0:c0 + BLK], "lu")
            ld_c = B.dma("gpsimd", Cl.ap().rearrange("p (h e) -> p h e", e=128),
                         d_C[:, b].rearrange("h p e -> p h e"), "lc")

            v_qp = 0
            o_ev = {0: 0, 1: 0}
            for h in range(NH):
                B.wait("scalar", "vx", max(v_qp, pm_done))
                s_qd = B.op("scalar", "sc", lambda e, h=h: e.activation(
                    out=qd[:, :], in_=iotaq[:, :], func=AF.Exp,
                    scale=negs_bc[:, h:h + 1]))
                s_dc = B.op("scalar", "sc", lambda e, h=h: e.activation(
                    out=dct[:, :], in_=idxd[:, :], func=AF.Exp,
                    scale=negs_bc[:, h:h + 1]))
                B.wait("vector", "lq", ld_q)
                B.wait("vector", "sc", s_qd)
                B.wait("vector", "mm", o_done)
                v_qp = B.op("vector", "vx", lambda e, h=h: e.tensor_tensor(
                    qp[:, :], qb[:, h * BLK:(h + 1) * BLK], qd[:, :], ALU.mult))
                B.wait("tensor", "lq", ld_q); B.wait("tensor", "lk", ld_k)
                B.wait("tensor", "vx", pm_done)
                mmv = 0
                for sc_ in range(2):
                    mmv = B.op("tensor", "mm", lambda e, sc_=sc_, h=h: e.matmul(
                        pb[sc_][:, 0:BLK],
                        kb[:, h * BLK + sc_ * 128: h * BLK + (sc_ + 1) * 128],
                        qb[:, h * BLK:(h + 1) * BLK], start=True, stop=True))
                B.wait("vector", "mm", mmv)
                B.wait("vector", "sc", s_dc)
                for sc_ in range(2):
                    pm_done = B.op("vector", "vx", lambda e, sc_=sc_: e.tensor_tensor(
                        pbuf[:, sc_ * BLK:(sc_ + 1) * BLK], pb[sc_][:, 0:BLK],
                        dct[:, sc_ * BLK:(sc_ + 1) * BLK], ALU.mult))
                opb = pb[2 + h % 2]
                B.wait("tensor", "vx", max(pm_done, v_qp, kvup, o_ev[h % 2]))
                B.wait("tensor", "lv", ld_v)
                B.op("tensor", "mm", lambda e, h=h, opb=opb: e.matmul(
                    opb[:, 0:BLK], kv_sb[:, h * 128:(h + 1) * 128], qp[:, :],
                    start=True, stop=False), 0)
                for sc_ in range(2):
                    o_done = B.op("tensor", "mm",
                                  lambda e, sc_=sc_, h=h, opb=opb: e.matmul(
                                      opb[:, 0:BLK],
                                      vtb[:, sc_ * D + h * 128: sc_ * D + (h + 1) * 128],
                                      pbuf[:, sc_ * BLK:(sc_ + 1) * BLK],
                                      start=False, stop=(sc_ == 1)),
                                  inc=(1 if sc_ == 1 else 0))
                B.wait("vector", "mm", o_done)
                o_ev[h % 2] = B.op("vector", "vx", lambda e, h=h, opb=opb: e.tensor_copy(
                    ob[:, h * BLK:(h + 1) * BLK], opb[:, 0:BLK]))
                B.wait("vector", "lc", ld_c)
                kvup = B.op("vector", "vx", lambda e, h=h: e.scalar_tensor_tensor(
                    out=kv_sb[:, h * 128:(h + 1) * 128],
                    in0=kv_sb[:, h * 128:(h + 1) * 128],
                    scalar=bd_bc[:, h:h + 1], in1=Cl[:, h * 128:(h + 1) * 128],
                    op0=ALU.mult, op1=ALU.add))

            B.wait("scalar", "vx", kvup)
            prev_mm = 0
            for c in range(16):
                B.wait("scalar", "mm", prev_mm)
                sqd = B.op("scalar", "sc", lambda e, c=c: e.activation(
                    out=dump[:, 0:BLK], in_=ob[:, c * BLK:(c + 1) * BLK],
                    func=AF.Square))
                B.wait("tensor", "sc", sqd)
                prev_mm = B.op("tensor", "mm", lambda e, c=c: e.matmul(
                    pb[6][0:1, 0:BLK], ones_c[:, :], dump[:, 0:BLK],
                    start=(c == 0), stop=(c == 15)))
            B.wait("vector", "mm", prev_mm)
            B.op("vector", "vx", lambda e: e.tensor_single_scalar(
                r_a[0:1, :], pb[6][0:1, 0:BLK], 1.0 / D, ALU.mult))
            v1 = B.op("vector", "vx", lambda e: e.tensor_single_scalar(
                r_a[0:1, :], r_a[0:1, :], 1e-6, ALU.add))
            B.wait("scalar", "vx", v1)
            s1 = B.op("scalar", "sc", lambda e: e.activation(
                out=r_b[0:1, :], in_=r_a[0:1, :], func=AF.Sqrt))
            B.wait("vector", "sc", s1)
            B.op("vector", "vx", lambda e: e.reciprocal(r_b[0:1, :], r_b[0:1, :]))
            B.op("vector", "vx", lambda e: e.tensor_tensor(
                r_c[0:1, :], r_b[0:1, :], r_b[0:1, :], ALU.mult))
            B.op("vector", "vx", lambda e: e.tensor_tensor(
                r_c[0:1, :], r_c[0:1, :], r_a[0:1, :], ALU.mult))
            B.op("vector", "vx", lambda e: e.tensor_scalar(
                r_c[0:1, :], r_c[0:1, :], -0.5, 1.5, op0=ALU.mult, op1=ALU.add))
            v2 = B.op("vector", "vx", lambda e: e.tensor_tensor(
                r_b[0:1, :], r_b[0:1, :], r_c[0:1, :], ALU.mult))
            B.wait("gpsimd", "vx", v2)
            B.dma("gpsimd", d_scr[0:BLK][None, :], r_b[0:1, :], "lr")
            B.wait("gpsimd", "lr", B.cnt["lr"])
            for t_ in range(2):
                g1 = B.dma("gpsimd", rmsc[:, t_:t_ + 1],
                           d_scr[t_ * 128:(t_ + 1) * 128][:, None], "lr")

            B.wait("vector", "lu", ld_u)
            g_done = 0
            for c in range(16):
                g_done = B.op("vector", "vx", lambda e, c=c: e.tensor_tensor(
                    ob[:, c * BLK:(c + 1) * BLK], ob[:, c * BLK:(c + 1) * BLK],
                    ub[:, c * BLK:(c + 1) * BLK], ALU.mult))

            for n in range(4):
                evv = 0
                for c in range(16):
                    bi = c % 2
                    wsem = "w0" if bi == 0 else "w1"
                    B.wait("sync", "mm", wuse[bi])
                    mw = B.dma("sync", wv[bi].ap(),
                               woutT[c * 128:(c + 1) * 128, n * 512:(n + 1) * 512], wsem)
                    B.wait("tensor", wsem, mw)
                    if c == 0:
                        B.wait("tensor", "vx", g_done)
                    for s_ in range(2):
                        B.op("tensor", "mm", lambda e, c=c, s_=s_, bi=bi: e.matmul(
                            pb[4 + s_][:, :],
                            ob[:, c * BLK + s_ * 128: c * BLK + (s_ + 1) * 128],
                            wv[bi].ap(), start=(c == 0), stop=(c == 15)),
                            inc=(1 if s_ == 1 else 0))
                    wuse[bi] = B.cnt["mm"]
                B.wait("vector", "mm", B.cnt["mm"])
                B.wait("vector", "lr", g1)
                for s_ in range(2):
                    mt = 2 * b + s_
                    evv = B.op("vector", "vx",
                               lambda e, s_=s_, mt=mt, n=n: e.scalar_tensor_tensor(
                                   out=x_sb[:, mt * D + n * 512: mt * D + (n + 1) * 512],
                                   in0=pb[4 + s_][:, :], scalar=rmsc[:, s_:s_ + 1],
                                   in1=x_sb[:, mt * D + n * 512: mt * D + (n + 1) * 512],
                                   op0=ALU.mult, op1=ALU.add))
                B.wait("tensor", "vx", evv)

    def ph_wstats(bufs):
        wsb = bufs["wsb"]
        dmp = bufs["dmp"]
        acol = bufs["acol"]
        acol2 = bufs["acol2"]
        acc = bufs["acc"]
        s1t = bufs["s1t"]
        s2t = bufs["s2t"]
        srow = bufs["srow"]

        W = 1024
        vz = B.vop(lambda e: e.memset(acc[:, :], 0.0))
        suse = {0: ("sc", 0), 1: ("vx", 0)}

        def stream(dram, nr, ncol, per_tile):
            idx = 0
            for r in range(nr):
                for cb in range(ncol // W):
                    bi = idx % 2
                    wsem = "w0" if bi == 0 else "w1"
                    B.wait("sync", suse[bi][0], suse[bi][1])
                    mw = B.dma("sync", wsb[bi].ap(),
                               dram[r * 128:(r + 1) * 128, cb * W:(cb + 1) * W], wsem)
                    B.wait("scalar" if bi == 0 else "vector", wsem, mw)
                    per_tile(r, cb, bi)
                    suse[bi] = (("sc", B.cnt["sc"]) if bi == 0
                                else ("vx", B.cnt["vx"]))
                    idx += 1

        def glob_tile(col):
            def f(r, cb, bi):
                if bi == 0:
                    B.wait("scalar", "vx", B.cnt["vx"])
                    s = B.sop(lambda e: e.activation(
                        out=dmp[:, :], in_=wsb[bi][:, :], func=AF.Abs,
                        accum_out=acol[:, 0:1]))
                    B.wait("vector", "sc", s)
                    B.vop(lambda e: e.tensor_tensor(
                        acc[:, col:col + 1], acc[:, col:col + 1], acol[:, 0:1],
                        ALU.add))
                else:
                    B.vop(lambda e: e.reduce_sum(
                        acol2[:, 0:1], wsb[bi][:, :], axis=AX.X,
                        apply_absolute_value=True))
                    B.vop(lambda e: e.tensor_tensor(
                        acc[:, col:col + 1], acc[:, col:col + 1], acol2[:, 0:1],
                        ALU.add))
            return f
        stream(wfcT, 16, H0, glob_tile(0))
        stream(wprojT, 40, D, glob_tile(1))

        def row_tile(tgt, nchunks, inv):
            def f(r, cb, bi):
                if bi == 0:
                    B.wait("scalar", "vx", B.cnt["vx"])
                    s = B.sop(lambda e: e.activation(
                        out=dmp[:, :], in_=wsb[bi][:, :], func=AF.Abs,
                        accum_out=acol[:, 0:1]))
                    B.wait("vector", "sc", s)
                else:
                    s = None
                    B.vop(lambda e: e.reduce_sum(
                        acol[:, 0:1], wsb[bi][:, :], axis=AX.X,
                        apply_absolute_value=True))
                if cb == 0:
                    B.vop(lambda e, r=r: e.tensor_copy(
                        tgt[:, r:r + 1], acol[:, 0:1]))
                else:
                    B.vop(lambda e, r=r: e.tensor_tensor(
                        tgt[:, r:r + 1], tgt[:, r:r + 1], acol[:, 0:1], ALU.add))
                if cb == nchunks - 1:
                    B.vop(lambda e, r=r: e.tensor_scalar(
                        tgt[:, r:r + 1], tgt[:, r:r + 1], inv, 1e-5,
                        op0=ALU.mult, op1=ALU.max))
            return f

        vlast = B.cnt["vx"]
        B.wait("tensor", "vx", vlast)
        B.op("tensor", "mm", lambda e: e.matmul(
            pb[0][0:1, 0:1], acc[:, 0:1], ones_c[:, :], start=True, stop=True))
        m1 = B.op("tensor", "mm", lambda e: e.matmul(
            pb[0][0:1, 1:2], acc[:, 1:2], ones_c[:, :], start=True, stop=True))
        B.wait("vector", "mm", m1)
        B.vop(lambda e: e.tensor_scalar(
            srow[0:1, 0:1], pb[0][0:1, 0:1], 1.0 / (H0 * D), 1e-5,
            op0=ALU.mult, op1=ALU.max))
        B.vop(lambda e: e.tensor_scalar(
            srow[0:1, 1:2], pb[0][0:1, 1:2], 1.0 / (H0 * D), 1e-5,
            op0=ALU.mult, op1=ALU.max))
        B.vop(lambda e: e.tensor_single_scalar(
            srow[0:1, 2:3], srow[0:1, 0:1], 0.7, ALU.mult))
        B.vop(lambda e: e.tensor_single_scalar(
            srow[0:1, 3:4], srow[0:1, 1:2], 0.7, ALU.mult))
        B.vop(lambda e: e.tensor_single_scalar(
            srow[0:1, 4:5], srow[0:1, 2:3], -1.0, ALU.mult))
        v2 = B.vop(lambda e: e.tensor_single_scalar(
            srow[0:1, 5:6], srow[0:1, 3:4], -1.0, ALU.mult))
        B.wait("tensor", "vx", v2)
        m2 = B.op("tensor", "mm", lambda e: e.matmul(
            pb[1][:, 0:6], ones_r[:, :], srow[0:1, 0:6], start=True, stop=True))
        B.wait("vector", "mm", m2)
        B.vop(lambda e: e.tensor_copy(am_fc[:, 0:1], pb[1][:, 0:1]))
        B.vop(lambda e: e.tensor_copy(am_pr[:, 0:1], pb[1][:, 1:2]))
        B.vop(lambda e: e.tensor_copy(thr_fc[:, 0:1], pb[1][:, 2:3]))
        B.vop(lambda e: e.tensor_copy(thr_pr[:, 0:1], pb[1][:, 3:4]))
        B.vop(lambda e: e.tensor_copy(nthr_fc[:, 0:1], pb[1][:, 4:5]))
        v3 = B.vop(lambda e: e.tensor_copy(nthr_pr[:, 0:1], pb[1][:, 5:6]))


    def ph_tern(bufs):
        wsb = bufs["wsb"]; tn = bufs["tn"]; qo = bufs["qo"]; qo2 = bufs["qo2"]
        thp = bufs["thp"]; thn = bufs["thn"]; s1r = bufs["s1r"]; tn2 = bufs["tn2"]
        W = 1024
        suse = {0: ("vx", 0), 1: ("gx", 0)}
        qfree = {0: 0, 1: 0}
        sgn = [0]

        def tern_global(dramT, dst, nr, ncol, thr, nthr):
            idx = 0
            for r in range(nr):
                for cb in range(ncol // W):
                    bi = idx % 2
                    wsem = "w0" if bi == 0 else "w1"
                    B.wait("sync", suse[bi][0], suse[bi][1])
                    mw = B.dma("sync", wsb[bi].ap(),
                               dramT[r * 128:(r + 1) * 128, cb * W:(cb + 1) * W], wsem)
                    qi = idx % 2
                    qsem = "ev" if qi == 0 else "vt"
                    if qi == 0:
                        B.wait("vector", qsem, qfree[qi])
                        B.wait("vector", wsem, mw)
                        B.vop(lambda e, bi=bi: e.tensor_scalar(
                            tn[:, :], wsb[bi][:, :], nthr[:, 0:1], None,
                            op0=ALU.is_lt))
                        vq = B.vop(lambda e, bi=bi, qi=qi:
                                   e.scalar_tensor_tensor(
                                       out=qo[qi][:, :], in0=wsb[bi][:, :],
                                       scalar=thr[:, 0:1], in1=tn[:, :],
                                       op0=ALU.is_gt, op1=ALU.subtract))
                        esem = "vx"
                    else:
                        B.wait("scalar", wsem, mw)
                        B.wait("scalar", "vx", sgn[0])
                        B.sop(lambda e, bi=bi: e.activation(
                            out=tn2[:, :], in_=wsb[bi][:, :], func=AF.Sign,
                            bias=nthr[:, 0:1]))
                        s2 = B.sop(lambda e, bi=bi: e.activation(
                            out=qo2[:, :], in_=wsb[bi][:, :], func=AF.Sign,
                            bias=thr[:, 0:1]))
                        B.wait("vector", "sc", s2)
                        B.wait("vector", qsem, qfree[qi])
                        vq = B.vop(lambda e, qi=qi: e.scalar_tensor_tensor(
                            out=qo[qi][:, :], in0=tn2[:, :], scalar=1.0,
                            in1=qo2[:, :], op0=ALU.mult, op1=ALU.add))
                        vq2 = B.vop(lambda e, qi=qi: e.tensor_single_scalar(
                            qo[qi][:, :], qo[qi][:, :], 0.5, ALU.mult))
                        vq = vq2
                        sgn[0] = vq
                        esem = "vx"
                    suse[bi] = (esem if qi == 0 else "sc",
                                vq if qi == 0 else B.cnt["sc"])
                    B.wait("gpsimd", esem, vq)
                    qfree[qi] = B.dma(
                        "gpsimd", dst[r * 128:(r + 1) * 128, cb * W:(cb + 1) * W],
                        qo[qi].ap(), qsem)
                    idx += 1
        tern_global(wfcT, d_tq1l0, 16, H0, thr_fc, nthr_fc)
        tern_global(wprojT, d_tq2l0, 40, D, thr_pr, nthr_pr)


    def emit_l1wprep(tb):
        wsb = tb["t_wsb"]; tn = tb["t_tn"]; qo = tb["t_qo"]; qo2 = tb["t_qo2"]
        thp = tb["t_thp"]; thn = tb["t_thn"]; s1r = tb["t_s1r"]
        s1t = tb["t_s1t"]; s2t = tb["t_s2t"]; acol = tb["t_acol"]
        W = 1024
        suse = {0: ("vx", B.cnt["vx"]), 1: ("vx", B.cnt["vx"])}

        def rstats(dram, tgt, nr, nchunks, inv):
            idx = 0
            for r in range(nr):
                for cb in range(nchunks):
                    bi = idx % 2
                    wsem = "w2" if bi == 0 else "w3"
                    B.wait("sync", suse[bi][0], suse[bi][1])
                    mw = B.dma("sync", wsb[bi].ap(),
                               dram[r * 128:(r + 1) * 128, cb * W:(cb + 1) * W],
                               wsem)
                    B.wait("vector", wsem, mw)
                    B.vop(lambda e, bi=bi: e.reduce_sum(
                        acol[:, 0:1], wsb[bi][:, :], axis=AX.X,
                        apply_absolute_value=True))
                    if cb == 0:
                        B.vop(lambda e, r=r, tgt=tgt: e.tensor_copy(
                            tgt[:, r:r + 1], acol[:, 0:1]))
                    else:
                        B.vop(lambda e, r=r, tgt=tgt: e.tensor_tensor(
                            tgt[:, r:r + 1], tgt[:, r:r + 1], acol[:, 0:1],
                            ALU.add))
                    if cb == nchunks - 1:
                        B.vop(lambda e, r=r, tgt=tgt, inv=inv: e.tensor_scalar(
                            tgt[:, r:r + 1], tgt[:, r:r + 1], inv, 1e-5,
                            op0=ALU.mult, op1=ALU.max))
                    suse[bi] = ("vx", B.cnt["vx"])
                    idx += 1
        rstats(w1n, s1t, 64, D // W, 1.0 / D)
        rstats(w2n, s2t, 16, H1 // W, 1.0 / H1)

        vlast = B.cnt["vx"]
        B.wait("tensor", "vx", vlast)
        m3 = B.op("tensor", "mm", lambda e: e.transpose(
            pb[2][0:64, 0:128], s1t[:, 0:64], ident[:, :]))
        m4 = B.op("tensor", "mm", lambda e: e.transpose(
            pb[3][0:16, 0:128], s2t[:, 0:16], ident[:, :]))
        B.wait("vector", "mm", m4)
        B.vop(lambda e: e.tensor_copy(thp[0:64, 0:128], pb[2][0:64, 0:128]))
        v4 = B.vop(lambda e: e.tensor_copy(thp[0:16, 128:256], pb[3][0:16, 0:128]))
        B.wait("scalar", "vx", v4)
        dlu = B.dma("scalar", d_s1.rearrange("(r c) -> r c", c=128),
                    thp[0:64, 0:128], "t1")
        dlc = B.dma("scalar", d_s2.rearrange("(r c) -> r c", c=128),
                    thp[0:16, 128:256], "t2")
        qfree = {0: dlu, 1: dlc}
        B.wait("vector", "t1", dlu)
        B.wait("vector", "t2", dlc)
        B.wait("sync", "t1", dlu)
        B.wait("sync", "t2", dlc)

        def trows(dramT, dst, scl_dram, nr, ncol):
            idx = 0
            for cb in range(ncol // W):
                ml = B.dma("sync", s1r[0:1, 0:W],
                           scl_dram[cb * W:(cb + 1) * W][None, :], "t3")
                B.wait("tensor", "t3", ml)
                B.wait("tensor", suse[0][0], suse[0][1])
                B.wait("tensor", suse[1][0], suse[1][1])
                mm1 = B.op("tensor", "mm", lambda e: e.matmul(
                    pb[4][:, 0:512], ones_r[:, :], s1r[0:1, 0:512],
                    start=True, stop=True))
                mm2 = B.op("tensor", "mm", lambda e: e.matmul(
                    pb[5][:, 0:512], ones_r[:, :], s1r[0:1, 512:1024],
                    start=True, stop=True))
                B.wait("vector", "mm", mm2)
                B.vop(lambda e: e.tensor_single_scalar(
                    thp[:, 0:512], pb[4][:, 0:512], 0.5, ALU.mult))
                B.vop(lambda e: e.tensor_single_scalar(
                    thp[:, 512:1024], pb[5][:, 0:512], 0.5, ALU.mult))
                B.vop(lambda e: e.tensor_single_scalar(
                    thn[:, 0:512], pb[4][:, 0:512], -0.5, ALU.mult))
                B.vop(lambda e: e.tensor_single_scalar(
                    thn[:, 512:1024], pb[5][:, 0:512], -0.5, ALU.mult))
                for r in range(nr):
                    bi = idx % 2
                    wsem = "w2" if bi == 0 else "w3"
                    B.wait("sync", suse[bi][0], suse[bi][1])
                    mw = B.dma("sync", wsb[bi].ap(),
                               dramT[r * 128:(r + 1) * 128, cb * W:(cb + 1) * W],
                               wsem)
                    qi = idx % 2
                    qsem = "t1" if qi == 0 else "t2"
                    B.wait("vector", wsem, mw)
                    B.wait("vector", qsem, qfree[qi])
                    B.vop(lambda e, bi=bi: e.tensor_tensor(
                        tn[:, :], wsb[bi][:, :], thn[:, :], ALU.is_lt))
                    B.vop(lambda e, bi=bi: e.tensor_tensor(
                        qo2[:, :], wsb[bi][:, :], thp[:, :], ALU.is_gt))
                    vq = B.vop(lambda e, qi=qi: e.tensor_tensor(
                        qo[qi][:, :], qo2[:, :], tn[:, :], ALU.subtract))
                    suse[bi] = ("vx", vq)
                    B.wait("scalar", "vx", vq)
                    qfree[qi] = B.dma(
                        "scalar", dst[r * 128:(r + 1) * 128, cb * W:(cb + 1) * W],
                        qo[qi].ap(), qsem)
                    idx += 1
        trows(w1T, d_tq1l1, d_s1, 16, H1)
        trows(w2T, d_tq2l1, d_s2, 64, D)
        B.wait("vector", "w2", B.cnt["w2"])
        B.wait("vector", "w3", B.cnt["w3"])
        B.wait("scalar", "t1", B.cnt["t1"])
        B.wait("scalar", "t2", B.cnt["t2"])
        B.wait("tensor", "t3", B.cnt["t3"])

    def ph_l0_ffn(bufs):
        ybuf = bufs["ybuf"]; st = bufs["st"]
        y2h = bufs["y2h"]; y2l = bufs["y2l"]
        wsb = bufs["wsb"]
        rb = bufs["rb"]; sqf = bufs["sqf"]
        sqh = bufs["sqh"]; sql = bufs["sql"]
        dump = ybuf

        wuse = {0: 0, 1: 0, 2: 0, 3: 0}
        prev_ln = 0
        for mc in range(4):
            bank_evac = {0: B.cnt["vx"], 1: B.cnt["vx"]}
            for mt_rel in range(2):
                mt = 2 * mc + mt_rel
                B.wait("scalar", "mm", B.cnt["mm"])
                B.wait("scalar", "vx", prev_ln)
                s3 = emit_ln(x_sb[:, mt * D:(mt + 1) * D], ybuf, st, dump)
                prev_ln = s3
                B.wait("tensor", "vx", s3)
                for k in range(16):
                    bi = k % 2
                    B.wait("tensor", "vx", bank_evac[bi])
                    mmv = B.op("tensor", "mm", lambda e, k=k, bi=bi: e.transpose(
                        pb[bi][:, 0:128], ybuf[:, k * 128:(k + 1) * 128], ident[:, :]))
                    B.wait("vector", "mm", mmv)
                    col = k * BLK + mt_rel * 128
                    B.vop(lambda e, bi=bi, col=col: e.tensor_copy(
                        y2h[:, col:col + 128], pb[bi][:, 0:128]))
                    bank_evac[bi] = B.vop(lambda e, bi=bi, col=col: e.tensor_tensor(
                        y2l[:, col:col + 128], pb[bi][:, 0:128], y2h[:, col:col + 128],
                        ALU.subtract))
            yT_done = B.cnt["vx"]
            B.wait("tensor", "vx", yT_done)

            set_done = {0: B.cnt["vx"], 1: B.cnt["vx"]}
            prev_sq = 0
            for cb in range(5):
                bset = [pb[4 * (cb % 2) + i] for i in range(4)]
                B.wait("tensor", "vx", set_done[cb % 2])
                for kr in range(16):
                    bi = (cb * 16 + kr) % 4
                    wsem = B.WSEMS[bi]
                    B.wait("sync", "mm", wuse[bi])
                    mw = B.dma("sync", wsb[bi][:, 0:1024],
                               d_tq1l0[kr * 128:(kr + 1) * 128,
                                       cb * 1024:(cb + 1) * 1024], wsem)
                    B.wait("tensor", wsem, mw)
                    for j in range(8):
                        bank = bset[j // 2]
                        half = (j % 2) * BLK
                        for hl, ysrc in ((0, y2h), (1, y2l)):
                            B.op("tensor", "mm",
                                 lambda e, kr=kr, j=j, hl=hl, ysrc=ysrc,
                                 bank=bank, half=half, bi=bi: e.matmul(
                                     bank[:, half:half + BLK],
                                     wsb[bi][:, j * 128:(j + 1) * 128],
                                     ysrc[:, kr * BLK:(kr + 1) * BLK],
                                     start=(kr == 0 and j % 2 == 0 and hl == 0),
                                     stop=(kr == 15 and j % 2 == 1 and hl == 1)),
                                 inc=(1 if (j == 7 and hl == 1) else 0))
                    wuse[bi] = B.cnt["mm"]
                gmm = B.cnt["mm"]
                for j in range(8):
                    bank = bset[j // 2]
                    half = (j % 2) * BLK
                    hc = cb * 8 + j
                    B.wait("vector", "mm", gmm)
                    B.wait("vector", "sc", prev_sq)
                    B.vop(lambda e, bank=bank, half=half: e.tensor_relu(
                        rb[:, 0:BLK], bank[:, half:half + BLK]))
                    vr = B.cnt["vx"]
                    B.wait("scalar", "vx", vr)
                    prev_sq = B.sop(lambda e: e.activation(
                        out=sqf[:, 0:BLK], in_=rb[:, 0:BLK], func=AF.Square,
                        scale=am_fc[:, 0:1]))
                    B.wait("vector", "sc", prev_sq)
                    col = hc * BLK
                    B.vop(lambda e, col=col: e.tensor_copy(
                        sqh[:, col:col + BLK], sqf[:, 0:BLK]))
                    B.vop(lambda e, col=col: e.tensor_tensor(
                        sql[:, col:col + BLK], sqf[:, 0:BLK], sqh[:, col:col + BLK],
                        ALU.subtract))
                set_done[cb % 2] = B.cnt["vx"]
            sq_done = B.cnt["vx"]

            for hc in range(40):
                bi = hc % 4
                wsem = B.WSEMS[bi]
                B.wait("sync", "mm", wuse[bi])
                mw = B.dma("sync", wsb[bi].ap(),
                           d_tq2l0[hc * 128:(hc + 1) * 128, :], wsem)
                B.wait("tensor", wsem, mw)
                if hc == 0:
                    B.wait("tensor", "vx", sq_done)
                for s_ in range(2):
                    for n in range(4):
                        for hl, sqsrc in ((0, sqh), (1, sql)):
                            B.op("tensor", "mm",
                                 lambda e, hc=hc, s_=s_, n=n, sqsrc=sqsrc, bi=bi:
                                 e.matmul(
                                     pb[4 * s_ + n][:, :],
                                     sqsrc[:, hc * BLK + s_ * 128: hc * BLK + (s_ + 1) * 128],
                                     wsb[bi][:, n * 512:(n + 1) * 512],
                                     start=(hc == 0 and hl == 0),
                                     stop=(hc == 39 and hl == 1)),
                                 inc=(1 if (s_ == 1 and n == 3 and hl == 1) else 0))
                wuse[bi] = B.cnt["mm"]
            B.wait("vector", "mm", B.cnt["mm"])
            evv = 0
            for s_ in range(2):
                for n in range(4):
                    mt = 2 * mc + s_
                    evv = B.vop(lambda e, s_=s_, mt=mt, n=n: e.scalar_tensor_tensor(
                        out=x_sb[:, mt * D + n * 512: mt * D + (n + 1) * 512],
                        in0=pb[4 * s_ + n][:, :], scalar=am_pr[:, 0:1],
                        in1=x_sb[:, mt * D + n * 512: mt * D + (n + 1) * 512],
                        op0=ALU.mult, op1=ALU.add))
            B.wait("tensor", "vx", evv)

    def ph_l1_attn_a(bufs):
        y3T = bufs["y3T"]; wb = bufs["wb"]; ev = bufs["ev"]; vtok = bufs["vtok"]
        ybuf = bufs["ybuf"]; st = bufs["st"]
        dump = ybuf

        bank_evac = {0: B.cnt["vx"], 1: B.cnt["vx"]}
        prev_ln = 0
        for mt in range(NMT):
            B.wait("scalar", "mm", B.cnt["mm"])
            B.wait("scalar", "vx", prev_ln)
            s3 = emit_ln(x_sb[:, mt * D:(mt + 1) * D], ybuf, st, dump)
            prev_ln = s3
            B.wait("tensor", "vx", s3)
            for k in range(16):
                bi = k % 2
                B.wait("tensor", "vx", bank_evac[bi])
                mmv = B.op("tensor", "mm", lambda e, k=k, bi=bi: e.transpose(
                    pb[bi][:, 0:128], ybuf[:, k * 128:(k + 1) * 128], ident[:, :]))
                B.wait("vector", "mm", mmv)
                bank_evac[bi] = B.vop(lambda e, k=k, mt=mt, bi=bi: e.tensor_copy(
                    y3T[:, k * T + mt * 128: k * T + (mt + 1) * 128], pb[bi][:, 0:128]))
        yT_done = B.cnt["vx"]
        B.wait("tensor", "vx", yT_done)

        wlast = {0: 0, 1: 0}
        prev_evac = ("vx", 0)
        ev_spill = {"q": 0, "k": 0}
        vt_spill = 0
        tr_bank_evac = {0: B.cnt["vx"], 1: B.cnt["vx"]}
        regs = [(w3q, "q"), (w3k, "k"), (w3v, "v")]
        for h in range(NH):
            for ri, (wT, r) in enumerate(regs):
                bi = (h * 3 + ri) % 2
                wsem = "w0" if bi == 0 else "w1"
                B.wait("sync", "mm", wlast[bi])
                mw = B.dma("sync", wb[bi].ap(), wT[h], wsem)
                B.wait("tensor", wsem, mw)
                B.wait("tensor", prev_evac[0], prev_evac[1])
                for half in range(2):
                    for kc in range(16):
                        B.op("tensor", "mm",
                             lambda e, kc=kc, half=half, bi=bi:
                             e.matmul(pb[2 + half][:, :],
                                      wb[bi][:, kc * 128:(kc + 1) * 128],
                                      y3T[:, kc * T + half * 512: kc * T + (half + 1) * 512],
                                      start=(kc == 0), stop=(kc == 15)),
                             inc=(1 if kc == 15 else 0))
                wlast[bi] = B.cnt["mm"]
                g1 = B.cnt["mm"]
                B.wait("vector", "mm", g1)
                B.wait("vector", "lq", ev_spill["q"])
                B.wait("vector", "ev", ev_spill["k"])
                if r != "v":
                    B.vop(lambda e: e.tensor_copy(ev[:, 0:512], pb[2][:, :]))
                    e2 = B.vop(lambda e: e.tensor_copy(ev[:, 512:1024], pb[3][:, :]))
                    prev_evac = ("vx", e2)
                    B.wait("gpsimd", "vx", e2)
                    if r == "q":
                        ev_spill["q"] = B.dma("gpsimd", d_q[h * 128:(h + 1) * 128, :],
                                              ev[:, 0:1024], "lq")
                    else:
                        ev_spill["k"] = B.dma("gpsimd",
                                              ag_in_k.ap()[h * 128:(h + 1) * 128, :],
                                              ev[:, 0:1024], "ev")
                else:
                    B.vop(lambda e: e.tensor_copy(ev[:, 0:512], pb[2][:, :]))
                    e2 = B.vop(lambda e: e.tensor_copy(ev[:, 512:1024], pb[3][:, :]))
                    prev_evac = ("vx", e2)
                    B.wait("tensor", "vx", e2)
                    for t in range(8):
                        tb_ = t % 2
                        B.wait("tensor", "vx", tr_bank_evac[tb_])
                        mmv = B.op("tensor", "mm", lambda e, t=t, tb_=tb_: e.transpose(
                            pb[4 + tb_][:, 0:128], ev[:, t * 128:(t + 1) * 128],
                            ident[:, :]))
                        B.wait("vector", "mm", mmv)
                        B.wait("vector", "vt", vt_spill)
                        tr_bank_evac[tb_] = B.vop(
                            lambda e, t=t, tb_=tb_: e.tensor_copy(
                                vtok[:, t * 128:(t + 1) * 128], pb[4 + tb_][:, 0:128]))
                    B.wait("gpsimd", "vx", max(tr_bank_evac.values()))
                    vt_spill = B.dma(
                        "gpsimd",
                        ag_in_v.ap().rearrange("(t p) d -> p t d", p=128)[:, :, h * 128:(h + 1) * 128],
                        vtok.ap().rearrange("p (t d) -> p t d", d=128), "vt")

        for s_ in ("lq", "ev", "vt"):
            B.wait("gpsimd", s_, B.cnt[s_])
        B.op("gpsimd", "cc", lambda e: e.collective_compute(
            "AllGather", ALU.bypass, replica_groups=[list(range(world))],
            ins=[ag_in_k.ap().opt()], outs=[ag_out_k.ap().opt()]))
        B.op("gpsimd", "cc", lambda e: e.collective_compute(
            "AllGather", ALU.bypass, replica_groups=[list(range(world))],
            ins=[ag_in_v.ap().opt()], outs=[ag_out_v.ap().opt()]))
        emit_l1wprep(bufs)
        B.wait("gpsimd", "cc", B.cnt["cc"])

        kh = ag_out_k.ap().tensor
        vh = ag_out_v.ap().tensor

        def mk(off_fn, dims):
            return lambda: bass.AP(off_fn[0], nc.snap(off_fn[1]()), dims)

        def prev_off():
            pid = B.eng.partition_id()
            return (pid + world - 1) % world

        def self_off():
            return B.eng.partition_id()

        B.dma("gpsimd", kwin[:, 0:T],
              mk((kh, lambda: prev_off() * (D * T)), [[T, D], [1, T]]), "gd")
        B.dma("gpsimd", kwin[:, T:2 * T],
              mk((kh, lambda: self_off() * (D * T)), [[T, D], [1, T]]), "gd")
        B.dma("gpsimd", vwin[0:T, :],
              mk((vh, lambda: prev_off() * (T * D)), [[D, T], [1, D]]), "gd")
        B.dma("gpsimd", vwin[T:2 * T, :],
              mk((vh, lambda: self_off() * (T * D)), [[D, T], [1, D]]), "gd")
        B.wait("gpsimd", "gd", B.cnt["gd"])

    def ph_l1_attn_b(bufs):
        qmb = bufs["qmb"]; osb = bufs["osb"]; masks = bufs["masks"]
        ktile = bufs["ktile"]; vtile = bufs["vtile"]
        tbv = bufs["tbv"]; pbm = bufs["pbm"]
        lsum = bufs["lsum"]
        wvb = bufs["wvb"]
        SC = 1.0 / float(np.sqrt(128.0))

        def lslot(h):
            if h < 12:
                return lsum, 32 * (h // 4), (h % 4) * 512
            if h < 15:
                return wvb[0], 32 * (h - 12), 0
            return wvb[1], 0, 0

        wuse = {0: 0, 1: 0}
        for mb in range(2):
            B.wait("gpsimd", "vx", B.cnt["vx"])
            B.wait("gpsimd", "mm", B.cnt["mm"])
            for s_ in ("lr", "lq", "lk", "lv", "lu", "lc"):
                B.wait("gpsimd", s_, B.cnt[s_])
            ml = B.dma("gpsimd", masks.ap().rearrange("p (k m) -> p k m", m=512),
                       lmask[mb].rearrange("k p m -> p k m"), "lr")
            lq = B.dma("gpsimd", qmb.ap().rearrange("p (c m) -> p c m", m=512),
                       d_q.rearrange("(c p) m -> p c m", p=128)[:, :, mb * 512:(mb + 1) * 512],
                       "lq")
            vz = B.vop(lambda e: e.memset(osb[:, :], 0.0))

            tanh_done = {0: 0, 1: 0, 2: 0}
            av_evac = {0: ("vx", vz), 1: ("vx", vz)}
            lrow_acc = {0: 0, 1: 0}
            pbm_freed = {0: 0, 1: 0, 2: 0}
            vmask = {0: 0, 1: 0, 2: 0}
            kv_ld = {}
            st_mm_done = {}

            def issue_kv(st_):
                par = st_ % 2
                ks, vs = ("lk", "lv") if par == 0 else ("lu", "lc")
                B.wait("gpsimd", "mm", st_mm_done.get(st_ - 2, 0))
                lk_ = B.dma("gpsimd",
                            ktile[par].ap().rearrange("p (c s) -> p c s", s=128),
                            kwin.rearrange("(c p) t -> p c t", p=128)
                            [:, :, mb * 512 + st_ * 128: mb * 512 + (st_ + 1) * 128], ks)
                lv_ = B.dma("gpsimd", vtile[par].ap(),
                            vwin[mb * 512 + st_ * 128: mb * 512 + (st_ + 1) * 128, :], vs)
                kv_ld[st_] = (ks, lk_, vs, lv_)

            score_mm = {}

            def emit_score(st_, h, ti):
                par = st_ % 2
                ktl = ktile[par]
                ks, lk, vs, lv = kv_ld[st_]
                sb_ = ti % 3
                B.wait("tensor", ks, lk)
                B.wait("tensor", "lq", lq)
                B.wait("tensor", "sc", tanh_done[sb_])
                mmv = B.op("tensor", "mm", lambda e, h=h, sb_=sb_, ktl=ktl: e.matmul(
                    pb[sb_][:, :], ktl[:, h * 128:(h + 1) * 128],
                    qmb[:, h * 512:(h + 1) * 512], start=True, stop=True))
                score_mm[(st_, h)] = mmv

            def emit_rest(st_, h, ti):
                par = st_ % 2
                vtl = vtile[par]
                ks, lk, vs, lv = kv_ld[st_]
                sb_ = ti % 3
                mmv = score_mm.pop((st_, h))
                B.wait("scalar", "mm", max(mmv, pbm_freed[sb_]))
                B.wait("scalar", "vx", vmask[sb_])
                B.sop(lambda e, sb_=sb_: e.activation(
                    out=tbv[sb_][:, :], in_=pb[sb_][:, :], func=AF.Tanh,
                    scale=SC / 30.0))
                tanh_done[sb_] = B.cnt["sc"]
                s_e = B.sop(lambda e, sb_=sb_: e.activation(
                    out=pbm[sb_][:, :], in_=tbv[sb_][:, :], func=AF.Exp,
                    scale=30.0, bias=neg30[:, 0:1]))
                B.wait("vector", "sc", s_e)
                B.wait("vector", "lr", ml)
                v_m = B.vop(lambda e, st_=st_, sb_=sb_: e.tensor_tensor(
                    pbm[sb_][:, :], pbm[sb_][:, :],
                    masks[:, st_ * 512:(st_ + 1) * 512], ALU.mult))
                vmask[sb_] = v_m
                ab = 3 + (h % 2)
                B.wait("tensor", "vx", v_m)
                B.wait("tensor", av_evac[h % 2][0], av_evac[h % 2][1])
                B.wait("tensor", vs, lv)
                av = B.op("tensor", "mm",
                          lambda e, h=h, ab=ab, sb_=sb_, vtl=vtl: e.matmul(
                              pb[ab][:, :], vtl[:, h * 128:(h + 1) * 128],
                              pbm[sb_][:, :], start=True, stop=True))
                lb_ = 5 + (h % 2)
                B.wait("tensor", "vx", lrow_acc[h % 2])
                B.op("tensor", "mm", lambda e, sb_=sb_, lb_=lb_: e.matmul(
                    pb[lb_][0:1, 0:512], ones_c[:, :], pbm[sb_][:, :],
                    start=True, stop=True))
                pbm_freed[sb_] = B.cnt["mm"]
                B.wait("vector", "mm", pbm_freed[sb_])
                lbuf, lpr, lcc = lslot(h)
                if st_ == 0:
                    lrow_acc[h % 2] = B.vop(lambda e, lbuf=lbuf, lpr=lpr, lcc=lcc,
                                            lb_=lb_: e.tensor_copy(
                                                lbuf[lpr:lpr + 1, lcc:lcc + 512],
                                                pb[lb_][0:1, 0:512]))
                else:
                    lrow_acc[h % 2] = B.vop(lambda e, lbuf=lbuf, lpr=lpr, lcc=lcc,
                                            lb_=lb_: e.tensor_tensor(
                                                lbuf[lpr:lpr + 1, lcc:lcc + 512],
                                                lbuf[lpr:lpr + 1, lcc:lcc + 512],
                                                pb[lb_][0:1, 0:512], ALU.add))
                B.wait("vector", "mm", av)
                av_evac[h % 2] = ("vx", B.vop(
                    lambda e, h=h, ab=ab: e.tensor_tensor(
                        osb[:, h * 512:(h + 1) * 512],
                        osb[:, h * 512:(h + 1) * 512], pb[ab][:, :], ALU.add)))

            issue_kv(0)
            issue_kv(1)
            for i in range(193):
                if i < 192:
                    st_, h_ = divmod(i, 16)
                    emit_score(st_, h_, i)
                if i >= 1:
                    pst, ph_ = divmod(i - 1, 16)
                    emit_rest(pst, ph_, i - 1)
                    if ph_ == 15:
                        st_mm_done[pst] = B.cnt["mm"]
                        if pst + 2 < 12:
                            issue_kv(pst + 2)

            for k_ in range(3):
                B.vop(lambda e, k_=k_: e.reciprocal(
                    lsum[32 * k_:32 * k_ + 1, :], lsum[32 * k_:32 * k_ + 1, :]))
            for k_ in range(3):
                B.vop(lambda e, k_=k_: e.reciprocal(
                    wvb[0][32 * k_:32 * k_ + 1, 0:512],
                    wvb[0][32 * k_:32 * k_ + 1, 0:512]))
            B.vop(lambda e: e.reciprocal(wvb[1][0:1, 0:512], wvb[1][0:1, 0:512]))
            vrec0 = B.cnt["vx"]
            nbuse = {6: 0, 7: 0}
            for h in range(NH):
                lbuf, lpr, lcc = lslot(h)
                nb = 6 + (h % 2)
                B.wait("tensor", "vx", max(vrec0, nbuse[nb]))
                m1 = B.op("tensor", "mm", lambda e, lbuf=lbuf, lpr=lpr, lcc=lcc,
                          nb=nb: e.matmul(
                    pb[nb][:, 0:512], ones_bc[lpr:lpr + 1, 0:128],
                    lbuf[lpr:lpr + 1, lcc:lcc + 512], start=True, stop=True))
                B.wait("vector", "mm", m1)
                nbuse[nb] = B.vop(lambda e, h=h, nb=nb: e.tensor_tensor(
                    osb[:, h * 512:(h + 1) * 512],
                    osb[:, h * 512:(h + 1) * 512], pb[nb][:, :], ALU.mult))
            wuse[0] = wuse[1] = B.cnt["mm"]
            on_done = B.cnt["vx"]

            for n in range(4):
                for h in range(NH):
                    bi = h % 2
                    wsem = "w0" if bi == 0 else "w1"
                    B.wait("sync", "mm", wuse[bi])
                    mw = B.dma("sync", wvb[bi].ap(),
                               woT[h * 128:(h + 1) * 128, n * 512:(n + 1) * 512], wsem)
                    B.wait("tensor", wsem, mw)
                    if h == 0:
                        B.wait("tensor", "vx", on_done)
                    for ms in range(4):
                        B.op("tensor", "mm", lambda e, h=h, ms=ms, bi=bi: e.matmul(
                            pb[ms][:, :],
                            osb[:, h * 512 + ms * 128: h * 512 + (ms + 1) * 128],
                            wvb[bi].ap(), start=(h == 0), stop=(h == NH - 1)),
                            inc=(1 if ms == 3 else 0))
                    wuse[bi] = B.cnt["mm"]
                B.wait("vector", "mm", B.cnt["mm"])
                evv = 0
                for ms in range(4):
                    mt = mb * 4 + ms
                    evv = B.vop(lambda e, ms=ms, mt=mt, n=n: e.tensor_tensor(
                        x_sb[:, mt * D + n * 512: mt * D + (n + 1) * 512],
                        x_sb[:, mt * D + n * 512: mt * D + (n + 1) * 512],
                        pb[ms][:, :], ALU.add))
                B.wait("tensor", "vx", evv)

    def ph_l1_ffn(bufs):
        ybuf = bufs["gbuf"][:, D:2 * D]
        st = bufs["st"]
        sc1 = bufs["sc1"]; sc2 = bufs["sc2"]
        s1r = bufs["s1r"]
        zb = bufs["zb"]; fb = bufs["fb"]; db = bufs["db"]
        ib = bufs["ib"]
        xif = bufs["xif"]
        xif2 = bufs["gbuf"]
        RW = 512
        xiT = bufs["xiT"]
        gbuf = bufs["gbuf"]
        k2T = bufs["k2T"]
        k2Tb = bufs["k2Tb"]
        wsb = bufs["wsb"]
        mxc = bufs["mxc"]
        W = 1024

        def emit_round(src_ap, n, s_col, out_ap, out_cols):
            for cc in range(n // RW):
                sl = slice(cc * RW, (cc + 1) * RW)
                B.vop(lambda e, sl=sl: e.tensor_scalar(
                    zb[:, :], src_ap[:, sl], mxc[:, s_col:s_col + 1], None,
                    op0=ALU.mult))
                B.vop(lambda e: e.tensor_copy(ib[:, :], zb[:, :]))
                B.vop(lambda e: e.tensor_copy(fb[:, :], ib[:, :]))
                B.vop(lambda e: e.tensor_sub(db[:, :], zb[:, :], fb[:, :]))
                B.vop(lambda e: e.scalar_tensor_tensor(
                    out=zb[:, :], in0=db[:, :], scalar=0.5, in1=fb[:, :],
                    op0=ALU.is_gt, op1=ALU.add))
                B.vop(lambda e: e.tensor_single_scalar(
                    db[:, :], db[:, :], -0.5, ALU.is_lt))
                B.vop(lambda e, sl=sl, cc=cc: e.tensor_tensor(
                    out_ap[:, out_cols.start + cc * RW: out_cols.start + (cc + 1) * RW],
                    zb[:, :], db[:, :], ALU.subtract))

        for tgt, dsrc, ncol in ((sc1, d_s1, H1), (sc2, d_s2, D)):
            for cb in range(ncol // W):
                ml = B.dma("gpsimd", s1r[0:1, 0:W], dsrc[cb * W:(cb + 1) * W][None, :], "lr")
                B.wait("tensor", "lr", ml)
                B.wait("tensor", "vx", B.cnt["vx"])
                B.op("tensor", "mm", lambda e: e.matmul(
                    pb[4][:, 0:512], ones_r[:, :], s1r[0:1, 0:512],
                    start=True, stop=True))
                m2 = B.op("tensor", "mm", lambda e: e.matmul(
                    pb[5][:, 0:512], ones_r[:, :], s1r[0:1, 512:1024],
                    start=True, stop=True))
                B.wait("vector", "mm", m2)
                B.vop(lambda e, tgt=tgt, cb=cb: e.tensor_copy(
                    tgt[:, cb * W: cb * W + 512], pb[4][:, 0:512]))
                B.vop(lambda e, tgt=tgt, cb=cb: e.tensor_copy(
                    tgt[:, cb * W + 512: (cb + 1) * W], pb[5][:, 0:512]))
                B.wait("gpsimd", "mm", m2)

        wuse = {0: 0, 1: 0, 2: 0, 3: 0}
        prev_ln = 0

        def preamble(mt, k2t):
            nonlocal prev_ln
            xs = x_sb[:, mt * D:(mt + 1) * D]
            B.wait("scalar", "mm", B.cnt["mm"])
            B.wait("scalar", "vx", prev_ln)
            s3 = emit_ln(xs, ybuf, st, ybuf)
            prev_ln = s3
            B.vop(lambda e: e.reduce_max(mxc[:, 0:1], ybuf[:, 0:D], axis=AX.X,
                                         apply_absolute_value=True))
            B.vop(lambda e: e.tensor_single_scalar(mxc[:, 0:1], mxc[:, 0:1], 1e-5,
                                                   ALU.max))
            B.vop(lambda e: e.reciprocal(mxc[:, 1:2], mxc[:, 0:1]))
            B.vop(lambda e: e.tensor_single_scalar(mxc[:, 1:2], mxc[:, 1:2], 127.0,
                                                   ALU.mult))
            B.vop(lambda e: e.reciprocal(mxc[:, 2:3], mxc[:, 1:2]))
            emit_round(ybuf, D, 1, xif2, slice(0, D))
            xi_done = B.cnt["vx"]
            B.wait("tensor", "vx", xi_done)
            bank_evac = {0: B.cnt["sc"], 1: B.cnt["sc"]}
            for k in range(16):
                bi = k % 2
                B.wait("tensor", "sc", bank_evac[bi])
                mmv = B.op("tensor", "mm", lambda e, k=k, bi=bi: e.transpose(
                    pb[bi][:, 0:128], xif2[:, k * 128:(k + 1) * 128], ident[:, :]))
                B.wait("scalar", "mm", mmv)
                bank_evac[bi] = B.sop(lambda e, k=k, bi=bi: e.copy(
                    xiT[:, k * 128:(k + 1) * 128], pb[bi][:, 0:128]))
            xiT_done_sc = B.cnt["sc"]

            prev_gelu = 0
            set_done = {0: B.cnt["vx"], 1: B.cnt["vx"]}
            for cb4 in range(4):
                bset = [pb[4 * (cb4 % 2) + i] for i in range(4)]
                B.wait("tensor", "vx", set_done[cb4 % 2])
                if cb4 == 0:
                    B.wait("tensor", "sc", xiT_done_sc)
                for kr in range(16):
                    bi = (cb4 * 16 + kr) % 3
                    wsem = B.WSEMS[bi]
                    B.wait("sync", "mm", wuse[bi])
                    mw = B.dma("sync", wsb[bi].ap(),
                               d_tq1l1[kr * 128:(kr + 1) * 128,
                                       cb4 * 2048:(cb4 + 1) * 2048], wsem)
                    B.wait("tensor", wsem, mw)
                    for j in range(4):
                        B.op("tensor", "mm",
                             lambda e, kr=kr, j=j, bi=bi, bset=bset: e.matmul(
                                 bset[j][:, :], xiT[:, kr * 128:(kr + 1) * 128],
                                 wsb[bi][:, j * 512:(j + 1) * 512],
                                 start=(kr == 0), stop=(kr == 15)),
                             inc=(1 if j == 3 else 0))
                    wuse[bi] = B.cnt["mm"]
                gmm = B.cnt["mm"]
                for j in range(4):
                    hb = cb4 * 4 + j
                    B.wait("vector", "mm", gmm)
                    B.wait("vector", "sc", prev_gelu)
                    B.vop(lambda e, hb=hb, j=j, bset=bset: e.tensor_tensor(
                        zb[:, 0:512], bset[j][:, :], sc1[:, hb * 512:(hb + 1) * 512],
                        ALU.mult))
                    vt_ = B.cnt["vx"]
                    B.wait("scalar", "vx", vt_)
                    prev_gelu = B.sop(lambda e, hb=hb: e.activation(
                        out=gbuf[:, hb * 512:(hb + 1) * 512], in_=zb[:, 0:512],
                        func=AF.Gelu_apprx_tanh, scale=mxc[:, 2:3]))
                set_done[cb4 % 2] = B.cnt["vx"]
            g_done = B.cnt["sc"]

            B.wait("vector", "sc", g_done)
            B.vop(lambda e: e.reduce_max(mxc[:, 0:1], gbuf[:, :], axis=AX.X,
                                         apply_absolute_value=True))
            B.vop(lambda e: e.tensor_single_scalar(mxc[:, 0:1], mxc[:, 0:1], 1e-5,
                                                   ALU.max))
            B.vop(lambda e: e.reciprocal(mxc[:, 1:2], mxc[:, 0:1]))
            B.vop(lambda e: e.tensor_single_scalar(mxc[:, 1:2], mxc[:, 1:2], 127.0,
                                                   ALU.mult))
            B.vop(lambda e: e.reciprocal(mxc[:, 2:3], mxc[:, 1:2]))
            tr_banks = {0: ("sc", B.cnt["sc"]), 1: ("sc", B.cnt["sc"])}
            tr_done = {0: 0, 1: 0}
            for cc in range(H1 // RW):
                xpar = cc % 2
                xx = xif[:, xpar * RW:(xpar + 1) * RW]
                B.wait("vector", "mm", tr_done[xpar])
                B.vop(lambda e, cc=cc: e.tensor_scalar(
                    zb[:, :], gbuf[:, cc * RW:(cc + 1) * RW], mxc[:, 1:2], None,
                    op0=ALU.mult))
                B.vop(lambda e: e.tensor_copy(ib[:, :], zb[:, :]))
                B.vop(lambda e: e.tensor_copy(fb[:, :], ib[:, :]))
                B.vop(lambda e: e.tensor_sub(db[:, :], zb[:, :], fb[:, :]))
                B.vop(lambda e: e.scalar_tensor_tensor(
                    out=zb[:, :], in0=db[:, :], scalar=0.5, in1=fb[:, :],
                    op0=ALU.is_gt, op1=ALU.add))
                B.vop(lambda e: e.tensor_single_scalar(
                    db[:, :], db[:, :], -0.5, ALU.is_lt))
                k2_done = B.vop(lambda e, xx=xx: e.tensor_sub(xx, zb[:, :], db[:, :]))
                B.wait("tensor", "vx", k2_done)
                for k in range(RW // 128):
                    bi = k % 2
                    B.wait("tensor", tr_banks[bi][0], tr_banks[bi][1])
                    mmv = B.op("tensor", "mm", lambda e, k=k, bi=bi, xx=xx: e.transpose(
                        pb[bi][:, 0:128], xx[:, k * 128:(k + 1) * 128], ident[:, :]))
                    B.wait("scalar", "mm", mmv)
                    tr_banks[bi] = ("sc", B.sop(
                        lambda e, cc=cc, k=k, bi=bi, k2t=k2t: e.copy(
                            k2t[:, (cc * 4 + k) * 128:(cc * 4 + k + 1) * 128],
                            pb[bi][:, 0:128])))
                tr_done[xpar] = B.cnt["mm"]
            B.wait("vector", "sc", B.cnt["sc"])

        for pr in range(NMT // 2):
            ma, mb_ = 2 * pr, 2 * pr + 1
            preamble(ma, k2T)
            B.vop(lambda e: e.tensor_copy(mxc[:, 3:4], mxc[:, 2:3]))
            preamble(mb_, k2Tb)
            k2_all = B.cnt["vx"]
            k2_all_sc = B.cnt["sc"]

            for hc in range(64):
                bi = hc % 3
                wsem = B.WSEMS[bi]
                B.wait("sync", "mm", wuse[bi])
                mw = B.dma("sync", wsb[bi].ap(),
                           d_tq2l1[hc * 128:(hc + 1) * 128, :], wsem)
                B.wait("tensor", wsem, mw)
                if hc == 0:
                    B.wait("tensor", "vx", k2_all)
                    B.wait("tensor", "sc", k2_all_sc)
                for n in range(4):
                    B.op("tensor", "mm", lambda e, hc=hc, n=n, bi=bi: e.matmul(
                        pb[4 + n][:, :], k2T[:, hc * 128:(hc + 1) * 128],
                        wsb[bi][:, n * 512:(n + 1) * 512],
                        start=(hc == 0), stop=(hc == 63)),
                        inc=0)
                for n in range(4):
                    B.op("tensor", "mm", lambda e, hc=hc, n=n, bi=bi: e.matmul(
                        pb[n][:, :], k2Tb[:, hc * 128:(hc + 1) * 128],
                        wsb[bi][:, n * 512:(n + 1) * 512],
                        start=(hc == 0), stop=(hc == 63)),
                        inc=(1 if n == 3 else 0))
                wuse[bi] = B.cnt["mm"]
            g2 = B.cnt["mm"]
            B.wait("vector", "mm", g2)
            evv = 0
            for n in range(4):
                B.vop(lambda e, n=n: e.tensor_tensor(
                    zb[:, 0:512], pb[4 + n][:, :], sc2[:, n * 512:(n + 1) * 512],
                    ALU.mult))
                evv = B.vop(lambda e, n=n, ma=ma: e.scalar_tensor_tensor(
                    out=x_sb[:, ma * D + n * 512: ma * D + (n + 1) * 512],
                    in0=zb[:, 0:512], scalar=mxc[:, 3:4],
                    in1=x_sb[:, ma * D + n * 512: ma * D + (n + 1) * 512],
                    op0=ALU.mult, op1=ALU.add))
            for n in range(4):
                B.vop(lambda e, n=n: e.tensor_tensor(
                    zb[:, 0:512], pb[n][:, :], sc2[:, n * 512:(n + 1) * 512],
                    ALU.mult))
                evv = B.vop(lambda e, n=n, mb_=mb_: e.scalar_tensor_tensor(
                    out=x_sb[:, mb_ * D + n * 512: mb_ * D + (n + 1) * 512],
                    in0=zb[:, 0:512], scalar=mxc[:, 2:3],
                    in1=x_sb[:, mb_ * D + n * 512: mb_ * D + (n + 1) * 512],
                    op0=ALU.mult, op1=ALU.add))
            B.wait("tensor", "vx", evv)
            B.wait("tensor", "vx", evv)

    def ph_out():
        B.dma("gpsimd", out.rearrange("(t p) d -> p t d", p=128),
              x_sb.ap().rearrange("p (t d) -> p t d", d=D), "od")
        for name, dst in dbg_outs.items():
            srcs = {"d_q": d_q, "d_kf": d_kf, "d_u": d_u, "d_vt": d_vt,
                    "d_C": d_C, "ag_out0": ag_out0.ap(),
                    "d_tq1l0": d_tq1l0, "d_tq2l0": d_tq2l0,
                    "d_tq1l1": d_tq1l1, "d_tq2l1": d_tq2l1,
                    "kwin": kwin, "vwin": vwin}
            B.dma("gpsimd", dst, srcs[name], "od")
        B.wait("gpsimd", "od", B.cnt["od"])

    B.phase(ph_const, name='const')

    if upto >= 2:
        with ExitStack() as wctx:
            def sbw(name, shape, dt=F32):
                return wctx.enter_context(nc.sbuf_tensor(name, shape, dt))
            bufs_w = dict(
                wsb=[sbw("ws0", [128, 1024]), sbw("ws1", [128, 1024])],
                dmp=sbw("dmp", [128, 1024]),
                acol=sbw("acol", [128, 1]),
                acol2=sbw("acol2", [128, 1]),
                acc=sbw("acc", [128, 2]),
                s1t=sbw("s1t", [128, 64]),
                s2t=sbw("s2t", [128, 16]),
                srow=sbw("srow", [1, 8]),
                tn=sbw("tn", [128, 1024]),
                tn2=sbw("tn2", [128, 1024]),
                qo=[sbw("qo0", [128, 1024], BF16), sbw("qo1", [128, 1024], BF16)],
                qo2=sbw("qo2", [128, 1024]),
                qo3=sbw("qo3", [128, 1024]),
                thp=sbw("thp", [128, 1024]),
                thn=sbw("thn", [128, 1024]),
                s1r=sbw("s1r", [1, 1024]),
            )
            B.phase(lambda: ph_wstats(bufs_w), name='wstats')
            B.phase(lambda: ph_tern(bufs_w), name='tern')

    l0outer = ExitStack()
    M_sb = l0outer.enter_context(nc.sbuf_tensor("M_sb", [128, NH * DH], F32))
    kv_sb = l0outer.enter_context(nc.sbuf_tensor("kv_sb", [128, NH * DH], F32))
    with ExitStack() as l1ctx:
        def sbl(name, shape, dt=F32):
            return l1ctx.enter_context(nc.sbuf_tensor(name, shape, dt))
        bufs1 = dict(
            yT=sbl("yT", [128, 16 * T]),
            wb=[sbl("wb0", [128, 16 * 128]), sbl("wb1", [128, 16 * 128])],
            ev=sbl("ev", [128, 1024]),
            ktok=sbl("ktok", [128, 8 * 128]),
            vtok=sbl("vtok", [128, 8 * 128]),
            Cb=sbl("Cb", [128, 4 * 128]),
            KD=sbl("KD", [128, 2]),
            ybuf=sbl("ybuf", [128, D]),
            st=stats_tiles(sbl, "s1_"),
        )
        B.phase(lambda: ph_l0_sweep1(bufs1), name='l0s1')

    with ExitStack() as l2ctx:
        def sbl2(name, shape, dt=F32):
            return l2ctx.enter_context(nc.sbuf_tensor(name, shape, dt))
        bufs2 = dict(
            qb=sbl2("qb", [128, NH * BLK]),
            kb=sbl2("kb", [128, NH * BLK]),
            vtb=sbl2("vtb", [128, 2 * D]),
            ub=sbl2("ub", [128, NH * BLK]),
            Cl=sbl2("Cl", [128, NH * 128]),
            ob=sbl2("ob", [128, NH * BLK]),
            agm=sbl2("agm", [128, NH * 128]),
            qp=sbl2("qp", [128, BLK]),
            qd=sbl2("qd", [128, BLK]),
            dct=sbl2("dct", [128, 2 * BLK]),
            pbuf=sbl2("pbuf", [128, 2 * BLK]),
            djc=sbl2("djc", [128, 1]),
            tmph=sbl2("tmph", [128, 128]),
            wv=[sbl2("wv0", [128, 512]), sbl2("wv1", [128, 512])],
            dump=sbl2("dump2", [128, BLK]),
            rmsc=sbl2("rmsc", [128, 2]),
            r_a=sbl2("r_a", [1, BLK]),
            r_b=sbl2("r_b", [1, BLK]),
            r_c=sbl2("r_c", [1, BLK]),
        )
        B.phase(lambda: ph_l0_sweep2(bufs2), name='l0s2')
    l0outer.close()

    if upto >= 2:
        with ExitStack() as fctx:
            def sbf(name, shape, dt=F32):
                return fctx.enter_context(nc.sbuf_tensor(name, shape, dt))
            bufs3 = dict(
                ybuf=sbf("f_ybuf", [128, D]),
                st=stats_tiles(sbf, "f_st_"),
                y2h=sbf("y2h", [128, 16 * BLK], BF16),
                y2l=sbf("y2l", [128, 16 * BLK], BF16),
                wsb=[sbf(f"fw{i}", [128, 2048], BF16) for i in range(4)],
                rb=sbf("rb", [128, BLK]),
                sqf=sbf("sqf", [128, BLK]),
                sqh=sbf("sqh", [128, 40 * BLK], BF16),
                sql=sbf("sql", [128, 40 * BLK], BF16),
            )
            B.phase(lambda: ph_l0_ffn(bufs3), name='l0f')

    if upto >= 3:
        with ExitStack() as actx:
            def sba(name, shape, dt=F32):
                return actx.enter_context(nc.sbuf_tensor(name, shape, dt))
            bufs4 = dict(
                y3T=sba("y3T", [128, 16 * T]),
                wb=[sba("aw0", [128, 16 * 128]), sba("aw1", [128, 16 * 128])],
                ev=sba("a_ev", [128, 1024]),
                vtok=sba("a_vtok", [128, 8 * 128]),
                ybuf=sba("a_ybuf", [128, D]),
                st=stats_tiles(sba, "a_st_"),
                t_wsb=[sba("t_ws0", [128, 1024]), sba("t_ws1", [128, 1024])],
                t_tn=sba("t_tn", [128, 1024]),
                t_qo=[sba("t_qo0", [128, 1024], BF16),
                      sba("t_qo1", [128, 1024], BF16)],
                t_qo2=sba("t_qo2", [128, 1024]),
                t_thp=sba("t_thp", [128, 1024]),
                t_thn=sba("t_thn", [128, 1024]),
                t_s1r=sba("t_s1r", [1, 1024]),
                t_s1t=sba("t_s1t", [128, 64]),
                t_s2t=sba("t_s2t", [128, 16]),
                t_acol=sba("t_acol", [128, 1]),
            )
            B.phase(lambda: ph_l1_attn_a(bufs4), name='l1a_a')
        with ExitStack() as bctx:
            def sbb(name, shape, dt=F32):
                return bctx.enter_context(nc.sbuf_tensor(name, shape, dt))
            bufs5 = dict(
                qmb=sbb("qmb", [128, NH * 512]),
                osb=sbb("osb", [128, NH * 512]),
                masks=sbb("masks", [128, NKT * 512], BF16),
                ktile=[sbb("ktile0", [128, 2048]), sbb("ktile1", [128, 2048])],
                vtile=[sbb("vtile0", [128, 2048]), sbb("vtile1", [128, 2048])],
                tbv=[sbb(f"tbv{i}", [128, 512]) for i in range(3)],
                pbm=[sbb(f"pbm{i}", [128, 512]) for i in range(3)],
                lsum=sbb("lsum", [128, 2048]),
                wvb=[sbb("bw0", [128, 512]), sbb("bw1", [128, 512])],
            )
            B.phase(lambda: ph_l1_attn_b(bufs5), name='l1a_b')

    if upto >= 4:
        with ExitStack() as gctx:
            def sbg(name, shape, dt=F32):
                return gctx.enter_context(nc.sbuf_tensor(name, shape, dt))
            bufs6 = dict(
                st=stats_tiles(sbg, "g_st_"),
                sc1=sbg("sc1", [128, H1]),
                sc2=sbg("sc2", [128, D]),
                s1r=sbg("g_s1r", [1, 1024]),
                zb=sbg("g_zb", [128, 512]),
                fb=sbg("g_fb", [128, 512]),
                db=sbg("g_db", [128, 512]),
                ib=sbg("g_ib", [128, 512], I32),
                xif=sbg("g_xif", [128, 2 * 512]),
                xiT=sbg("g_xiT", [128, 16 * 128], BF16),
                gbuf=sbg("g_gbuf", [128, H1]),
                k2T=sbg("g_k2T", [128, 64 * 128], BF16),
                k2Tb=sbg("g_k2Tb", [128, 64 * 128], BF16),
                wsb=[sbg(f"g_w{i}", [128, 2048], BF16) for i in range(3)],
                mxc=sbg("g_mxc", [128, 4]),
            )
            B.phase(lambda: ph_l1_ffn(bufs6), name='l1f')

    B.phase(ph_out, name='out')
    B.ctx.close()
    return nc


def _tp(a):
    return np.ascontiguousarray(np.asarray(a, dtype=np.float32).T)


def _pack4(w):
    r = w.shape[0] // D
    t = np.asarray(w, np.float32).reshape(r, NH, 128, NH, 128)
    return np.ascontiguousarray(t.transpose(0, 1, 4, 3, 2)).reshape(r, NH, 128, D)


def _masks_for_core(c):
    import ml_dtypes
    m = np.zeros((2, NKT, 128, 512), dtype=ml_dtypes.bfloat16)
    sp = np.arange(128)[:, None]
    mp = np.arange(512)[None, :]
    for mb in range(2):
        gm = T * c + 512 * mb
        for k in range(NKT):
            bj = gm - WIN + 128 * k
            i = gm + mp
            j = bj + sp
            ok = (j <= i) & (i - j < WIN) & (j >= 0)
            m[mb, k] = ok.astype(np.float32)
    return m


def kernel(L0_attn_x, L0_attn_slope_rate, L0_attn_W_qkvu, L0_attn_W_out,
           L0_ffn_w_fc, L0_ffn_w_proj, L1_attn_Wq, L1_attn_Wk, L1_attn_Wv,
           L1_attn_Wo, L1_ffn_w1, L1_ffn_w2, _upto=4, _dbg=()):
    x = np.ascontiguousarray(np.asarray(L0_attn_x, np.float32).reshape(NCORES * T, D))
    shared = dict(
        w4qkvu=_pack4(L0_attn_W_qkvu), woutT=_tp(L0_attn_W_out),
        wfcT=_tp(L0_ffn_w_fc), wprojT=_tp(L0_ffn_w_proj),
        w3q=_pack4(L1_attn_Wq)[0], w3k=_pack4(L1_attn_Wk)[0],
        w3v=_pack4(L1_attn_Wv)[0],
        woT=_tp(L1_attn_Wo),
        w1n=np.ascontiguousarray(np.asarray(L1_ffn_w1, np.float32)),
        w1T=_tp(L1_ffn_w1),
        w2n=np.ascontiguousarray(np.asarray(L1_ffn_w2, np.float32)),
        w2T=_tp(L1_ffn_w2),
        slope_in=np.ascontiguousarray(
            np.asarray(L0_attn_slope_rate, np.float32).reshape(NH)),
    )
    in_maps = []
    for c in range(NCORES):
        m = dict(shared)
        m["xin"] = np.ascontiguousarray(x[c * T:(c + 1) * T])
        m["gates_in"] = (np.arange(NCORES) < c).astype(np.float32)
        m["lmask"] = _masks_for_core(c)
        in_maps.append(m)
    nc = build_nc(upto=_upto, dbg=_dbg)
    res = run_bass_kernel_spmd(nc, in_maps, core_ids=list(range(NCORES)))
    outs = np.concatenate([res.results[c]["out"] for c in range(NCORES)], axis=0)
    if _dbg:
        dbgd = [{k: v for k, v in res.results[c].items() if k.startswith("dbg_")}
                for c in range(NCORES)]
        return outs.reshape(1, NCORES * T, D), dbgd
    return outs.reshape(1, NCORES * T, D)
